# version 04
# baseline (speedup 1.0000x reference)
import math
import numpy as np
import ml_dtypes
from contextlib import ExitStack

import concourse.bass as bass
import concourse.mybir as mybir
from concourse.bass_utils import run_bass_kernel_spmd

F32 = mybir.dt.float32
BF16 = mybir.dt.bfloat16
AF = mybir.ActivationFunctionType
ALU = mybir.AluOpType
AX = mybir.AxisListType

NCORES = 8
C_SEQ = 2
L_SEQ = 16384
D_MODEL = 1024
NH = 16
HD = 64
NFEAT = 256
NR = 4
TCORE = 4096
SEQ_SLICE = 2048
AG = 512
NAG = TCORE // AG
BG = 1024
NBG = TCORE // BG
FFD = 4096
NFC = 32
EPS_ATT = 1e-6
EPS_LN = 1e-5
LN16 = math.log(16.0)
INV3 = 1.0 / 512.0

bfarr = lambda a: np.ascontiguousarray(np.asarray(a), dtype=ml_dtypes.bfloat16)
f32arr = lambda a: np.ascontiguousarray(np.asarray(a), dtype=np.float32)



def _hadamard64():
    h = np.array([[1.0]])
    while h.shape[0] < 64:
        h = np.block([[h, h], [h, -h]])
    return h


def _origidx():
    o = np.zeros(64, dtype=np.int64)
    o[:32] = 2 * np.arange(32)
    o[32:] = 2 * np.arange(32) + 1
    return o


def _rope_tables(core_id):
    t = np.arange(TCORE)
    l = core_id * SEQ_SLICE + (t % SEQ_SLICE)
    pos = np.maximum(l - 1, 0).astype(np.float64)
    j = np.arange(32)
    freq = 10000.0 ** (-2.0 * j / 64.0)
    ang = pos[None, :] * freq[:, None]
    cos, sin = np.cos(ang), np.sin(ang)
    ctab = np.zeros((128, TCORE))
    stab = np.zeros((128, TCORE))
    for hf in range(2):
        b = 64 * hf
        ctab[b:b + 32] = cos
        ctab[b + 32:b + 64] = cos
        stab[b:b + 32] = -sin
        stab[b + 32:b + 64] = sin
    return bfarr(ctab), bfarr(stab)


def _jmat():
    j64 = np.zeros((64, 64))
    j64[np.arange(32), 32 + np.arange(32)] = 1.0
    j64[32 + np.arange(32), np.arange(32)] = 1.0
    out = np.zeros((128, 128))
    out[0:64, 0:64] = j64
    out[64:128, 64:128] = j64
    return bfarr(out)


def _selectors():
    sel = np.zeros((128, 8, 16))
    for c in range(8):
        sel[0:64, c, 2 * c] = 1.0
        sel[64:128, c, 2 * c + 1] = 1.0
    return bfarr(sel.reshape(128, 128))


def _expand8():
    e = np.zeros((8, 128))
    for i in range(8):
        e[i, 16 * i:16 * i + 16] = 1.0
    return bfarr(e)


def make_in_maps(inp):
    x = np.asarray(inp["L0_attn_x"])
    w_qkv = np.asarray(inp["L0_attn_w_qkv"])
    w_out = np.asarray(inp["L0_attn_w_out"])
    signs = np.asarray(inp["L0_attn_sorf_signs"])
    scales = np.asarray(inp["L0_attn_sorf_scales"])
    gate_w = np.asarray(inp["L0_ffn_gate_w"])
    w_gate = np.asarray(inp["L0_ffn_w_gate"])
    w_up = np.asarray(inp["L0_ffn_w_up"])
    w_down = np.asarray(inp["L0_ffn_w_down"])
    A1 = np.asarray(inp["L0_ffn_A1"])
    B1 = np.asarray(inp["L0_ffn_B1"])
    A3 = np.asarray(inp["L0_ffn_A3"])
    B3 = np.asarray(inp["L0_ffn_B3"])
    A2 = np.asarray(inp["L0_ffn_A2"])
    B2 = np.asarray(inp["L0_ffn_B2"])

    orig = _origidx()
    perm = np.concatenate([64 * h + orig for h in range(NH)])
    w_q = w_qkv[0:1024][perm]
    w_k = w_qkv[1024:2048][perm]
    w_v = w_qkv[2048:3072]

    proj_layout = lambda w: f32arr(w.T.reshape(8, 128, w.shape[0]).transpose(1, 0, 2).reshape(128, 8 * w.shape[0]))
    wqT = proj_layout(w_q)
    wkT = proj_layout(w_k)
    wvT = proj_layout(w_v)
    woT = f32arr(w_out.T.reshape(8, 128, 1024).transpose(1, 0, 2).reshape(128, 8192))

    ii = np.arange(64)
    primeof = np.zeros(64, dtype=np.int64)
    primeof[orig] = np.arange(64)
    diag_all = np.zeros((64, NH * NR * 4 * 64), dtype=np.float32)
    for h in range(NH):
        for r in range(NR):
            base = ((h * NR) + r) * 256
            dgm = np.zeros((64, 64), dtype=np.float32)
            dgm[ii, ii] = scales[h, r]
            d2m = np.zeros((64, 64), dtype=np.float32)
            d2m[ii, ii] = signs[h, r, 1]
            d3m = np.zeros((64, 64), dtype=np.float32)
            d3m[ii, ii] = signs[h, r, 2]
            d1ptm = np.zeros((64, 64), dtype=np.float32)
            d1ptm[ii, primeof] = signs[h, r, 0]
            diag_all[:, base:base + 64] = dgm
            diag_all[:, base + 64:base + 128] = d2m
            diag_all[:, base + 128:base + 192] = d3m
            diag_all[:, base + 192:base + 256] = d1ptm

    gate_wT = f32arr(gate_w.T.reshape(8, 128, 8).transpose(1, 0, 2).reshape(128, 64))
    a1catT = f32arr(A1.transpose(2, 0, 1).reshape(8, 128, 128).transpose(1, 0, 2).reshape(128, 1024))
    a3catT = f32arr(A3.transpose(2, 0, 1).reshape(8, 128, 128).transpose(1, 0, 2).reshape(128, 1024))
    b2cat = f32arr(B2.transpose(0, 2, 1).reshape(128, 1024))

    wg_t = w_gate.T.reshape(8, 128, 32, 128)
    wu_t = w_up.T.reshape(8, 128, 32, 128)
    b1l = B1.transpose(0, 2, 1).reshape(128, 32, 128)
    b3l = B3.transpose(0, 2, 1).reshape(128, 32, 128)
    a2l = A2.transpose(2, 0, 1).reshape(32, 128, 128)
    wd_r = w_down.T.reshape(32, 128, 1024)
    stream = np.zeros((32, 128, 3456), dtype=np.float32)
    for fc in range(32):
        stream[fc, :, 0:1024] = wg_t[:, :, fc, :].transpose(1, 0, 2).reshape(128, 1024)
        stream[fc, :, 1024:2048] = wu_t[:, :, fc, :].transpose(1, 0, 2).reshape(128, 1024)
        stream[fc, :, 2048:2176] = b1l[:, fc, :]
        stream[fc, :, 2176:2304] = b3l[:, fc, :]
        stream[fc, :, 2304:2432] = a2l[fc]
        stream[fc, :, 2432:3456] = wd_r[fc]
    stream = f32arr(stream)

    shared = dict(
        wqT=wqT, wkT=wkT, wvT=wvT, woT=woT, sorf_diag_all=diag_all,
        gate_wT=gate_wT, a1catT=a1catT, a3catT=a3catT, b2cat=b2cat,
        ffn_stream=stream,
        hmat=bfarr(_hadamard64()), jmat=_jmat(), sel=_selectors(),
        exp8=_expand8(), ident=bfarr(np.eye(128)),
        onescol=bfarr(np.ones((1, 128))), onescol2=bfarr(np.ones((128, 1))),
    )
    in_maps = []
    for i in range(NCORES):
        xs = np.concatenate([x[0, i * SEQ_SLICE:(i + 1) * SEQ_SLICE],
                             x[1, i * SEQ_SLICE:(i + 1) * SEQ_SLICE]], axis=0)
        ctab, stab = _rope_tables(i)
        m = dict(shared)
        xgT = xs.T.reshape(8, 128, NAG, AG).transpose(1, 2, 0, 3).reshape(128, NAG, 8 * AG)
        m.update(xg=f32arr(xs), xgT=f32arr(xgT), ctab=ctab, stab=stab)
        in_maps.append(m)
    return in_maps



SEM_NAMES = ["sHc", "sHa0", "sHa1", "sHm", "sHf0", "sHf1", "sGc", "sGw", "sGs", "sT", "sV", "sS", "sC", "sO"]


class DummyInstr:
    def then_inc(self, sem, n):
        return self


class DryEng:
    def __getattr__(self, name):
        def f(*a, **k):
            return DummyInstr()
        return f


class Env:
    def __init__(self):
        self.reset()
        self.marks = {}
        self.dry = True
        self.sems = None

    def reset(self):
        self.vals = {s: 0 for s in SEM_NAMES}
        self.log = []

    def begin_real(self, sems):
        self.dry_log = self.log
        self.reset()
        self.dry = False
        self.sems = sems

    def inc(self, instr, sem, n=1):
        self.vals[sem] += n
        self.log.append((sem, n))
        if not self.dry:
            instr.then_inc(self.sems[sem], n)

    def check_passes_match(self):
        assert self.dry_log == self.log, (
            f"dry/real inc divergence at index "
            f"{next(i for i, (a, b) in enumerate(zip(self.dry_log, self.log)) if a != b) if self.dry_log != self.log and len(self.dry_log) == len(self.log) else (len(self.log), len(self.dry_log))}")

    def mark(self, name, sem):
        if self.dry:
            assert name not in self.marks, f"dup mark {name}"
            self.marks[name] = (sem, self.vals[sem])

    def wait(self, eng, name):
        if not self.dry:
            sem, v = self.marks[name]
            eng.wait_ge(self.sems[sem], v)



def build_program(debug=(), phases=("P0", "A", "AR", "B", "C"), alv=7, stub_y=True, nag=NAG, sub=3):
    nc = bass.Bass("TRN2", target_bir_lowering=False)

    xg_d = nc.dram_tensor("xg", [TCORE, D_MODEL], F32, kind="ExternalInput")
    xgT_d = nc.dram_tensor("xgT", [128, NAG, 8 * AG], F32, kind="ExternalInput")
    wqT_d = nc.dram_tensor("wqT", [128, 8192], F32, kind="ExternalInput")
    wkT_d = nc.dram_tensor("wkT", [128, 8192], F32, kind="ExternalInput")
    wvT_d = nc.dram_tensor("wvT", [128, 8192], F32, kind="ExternalInput")
    woT_d = nc.dram_tensor("woT", [128, 8192], F32, kind="ExternalInput")
    diag_d = nc.dram_tensor("sorf_diag_all", [64, NH * NR * 4 * 64], F32, kind="ExternalInput")
    gate_wT_d = nc.dram_tensor("gate_wT", [128, 64], F32, kind="ExternalInput")
    a1catT_d = nc.dram_tensor("a1catT", [128, 1024], F32, kind="ExternalInput")
    a3catT_d = nc.dram_tensor("a3catT", [128, 1024], F32, kind="ExternalInput")
    b2cat_d = nc.dram_tensor("b2cat", [128, 1024], F32, kind="ExternalInput")
    stream_d = nc.dram_tensor("ffn_stream", [NFC, 128, 3456], F32, kind="ExternalInput")
    hmat_d = nc.dram_tensor("hmat", [64, 64], BF16, kind="ExternalInput")
    jmat_d = nc.dram_tensor("jmat", [128, 128], BF16, kind="ExternalInput")
    sel_d = nc.dram_tensor("sel", [128, 128], BF16, kind="ExternalInput")
    exp8_d = nc.dram_tensor("exp8", [8, 128], BF16, kind="ExternalInput")
    ident_d = nc.dram_tensor("ident", [128, 128], BF16, kind="ExternalInput")
    ones_d = nc.dram_tensor("onescol", [1, 128], BF16, kind="ExternalInput")
    ones2_d = nc.dram_tensor("onescol2", [128, 1], BF16, kind="ExternalInput")
    ctab_d = nc.dram_tensor("ctab", [128, TCORE], BF16, kind="ExternalInput")
    stab_d = nc.dram_tensor("stab", [128, TCORE], BF16, kind="ExternalInput")
    y_d = nc.dram_tensor("y", [128, 128, 256], F32, kind="ExternalOutput")
    streambf_d = nc.dram_tensor("streambf", [NFC, 128, 3456], BF16)
    cc_in_d = nc.dram_tensor("cc_in", [128, 4160], F32)
    cc_out_d = nc.dram_tensor("cc_out", [128, 4160], F32)

    dbg = {}
    def dbg_t(name, shape, dtype):
        if name in debug:
            dbg[name] = nc.dram_tensor("dbg_" + name, shape, dtype, kind="ExternalOutput")
    dbg_t("fpt", [128, 2048], BF16)
    dbg_t("sacc", [128, 4160], F32)
    dbg_t("kphi", [128, NH * 4 * 256], BF16)
    dbg_t("x1t", [128, 8192], F32)
    dbg_t("attnt", [128, 4096], BF16)
    dbg_t("fsc", [128, 64], F32)
    dbg_t("ohb", [128, 4096], BF16)
    dbg_t("m2r", [1, 512], BF16)
    dbg_t("r2r", [1, 512], BF16)
    dbg_t("t2b", [128, 4096], BF16)

    es = ExitStack()
    with es:
        SB = lambda n, shp, dt=F32: es.enter_context(nc.sbuf_tensor(n, shp, dt))
        PSUM = lambda n, shp, dt=F32: es.enter_context(nc.psum_tensor(n, shp, dt))

        fpt = SB("fptS", [128, 2048], BF16)
        s_acc = SB("s_acc", [128, 4160], F32)
        s_bf = SB("s_bf", [128, 4160], BF16)
        mrow = SB("mrow", [1, TCORE], BF16)
        rrow = SB("rrow", [1, TCORE], BF16)
        ident = SB("identS", [128, 128], BF16)
        onescol = SB("onescolS", [1, 128], BF16)
        onescol2 = SB("onescol2S", [128, 1], BF16)
        jmat = SB("jmatS", [128, 128], BF16)
        sel = SB("selS", [128, 8 * 16], BF16)

        sorf_tmp = None

        xTbuf = [SB(f"xTbuf{p}", [128, 8 * AG], F32) for p in range(2)]
        ctbuf = [SB(f"ctbuf{p}", [128, AG], BF16) for p in range(2)]
        stbuf = [SB(f"stbuf{p}", [128, AG], BF16) for p in range(2)]
        xh = [SB(f"xh{p}", [128, 8 * AG], BF16) for p in range(2)]
        wkT = SB("wkTs", [128, 8 * 1024], BF16)
        wvT = SB("wvTs", [128, 8 * 1024], BF16)
        kT = SB("kTs", [128, 8 * AG], BF16)
        t1b = SB("t1b", [128, 8 * AG], BF16)
        t2b = SB("t2b", [128, 8 * AG], BF16)
        sqb = SB("sqb", [128, 8 * AG], BF16)
        kphi = SB("kphiS", [128, NH * 4 * 256], BF16)
        diag_all = kphi
        vaug = SB("vaug", [128, 4 * 1040], BF16)
        stat4m = SB("stat4m", [128, 4], F32)
        stat4s = SB("stat4s", [128, 4], F32)
        tmp4 = SB("tmp4", [128, 4], F32)
        rv4 = SB("rv4", [128, 4], F32)
        rstd4 = SB("rstd4", [128, 4], F32)
        mcol_bf = SB("mcol_bf", [128, 8], BF16)
        cvec = SB("cvec", [128, 4], F32)
        fsc = SB("fsc", [128, 64], F32)
        nsb = fsc
        tmp8a = SB("tmp8a", [128, 8], F32)
        a1catTs = SB("a1catTs", [128, 1024], BF16)
        a3catTs = SB("a3catTs", [128, 1024], BF16)

        tmp8b = SB("tmp8b", [128, 8], F32)


        K = [PSUM(f"bankpair{i}", [128, 1024]) for i in range(4)]
        half = lambda i, hh: K[i][:, 512 * hh:512 * (hh + 1)]
        fstage = kT
        sorf_tmp = [kT[0:64, 2048:2112], kT[0:64, 2112:2176]]
        hmat = kT[0:64, 2176:2240]
        b2cats = None
        ohbuf = vaug
        sacc_bf = lambda: s_acc[:].bitcast(BF16)
        oTbuf_ap = lambda: s_acc[:].bitcast(BF16)[:, 0:4096]
        qphib_ap = lambda: s_acc[:].bitcast(BF16)[:, 4096:6144]
        b2cats = s_acc[:].bitcast(BF16)[:, 0:1024]
        gatewT = s_acc[:].bitcast(BF16)[:, 6912:6976]
        exp8s = s_acc[:].bitcast(BF16)[0:8, 6976:7104]
        mask12 = s_acc[:].bitcast(BF16)[:, 7104:7168]
        rwb = s_acc[:].bitcast(BF16)[:, 7168:7176]
        m2row = s_acc[:].bitcast(BF16)[0:1, 7176:7688]
        r2row = s_acc[:].bitcast(BF16)[0:1, 7688:8200]
        zhat = s_acc[:].bitcast(BF16)[:, 4096:4608]
        def mixT_ap(g, fc):
            if fc < 16:
                return xTbuf[g % 2][:].bitcast(BF16)[:, 512 * fc:512 * (fc + 1)]
            if fc < 24:
                return kT[:, 512 * (fc - 16):512 * (fc - 15)]
            return xh[1 - g % 2][:, 512 * (fc - 24):512 * (fc - 23)]

        env = Env()

        def prog_sync(s):
            e = env
            e.inc(s.dma_start(out=ident[:], in_=ident_d.ap()), "sHc", 16)
            e.inc(s.dma_start(out=onescol[:], in_=ones_d.ap()), "sHc", 16)
            e.inc(s.dma_start(out=onescol2[:], in_=ones2_d.ap()), "sHc", 16)
            e.inc(s.dma_start(out=jmat[:], in_=jmat_d.ap()), "sHc", 16)
            e.inc(s.dma_start(out=hmat, in_=hmat_d.ap()), "sHc", 16)
            e.inc(s.dma_start(out=sel[:], in_=sel_d.ap()), "sHc", 16)
            e.mark("H_consts", "sHc")
            if "P0" in phases:
                e.wait(s, "S_fpt_evac")
                e.inc(s.dma_start(out=fpt[64:128, :], in_=fstage[0:64, 0:2048]), "sHm", 16)
                e.mark("H_fpt", "sHm")
            if "A" in phases:
                for g in range(nag):
                    p = g % 2
                    if g >= 2:
                        e.wait(s, f"A_vfree_{g-2}")
                    sp = f"sHa{g % 2}"
                    e.inc(s.dma_start(out=xTbuf[p][:], in_=xgT_d.ap()[:, g]), sp, 16)
                    e.inc(s.dma_start(out=ctbuf[p][:], in_=ctab_d.ap()[:, AG * g:AG * (g + 1)]), sp, 16)
                    e.inc(s.dma_start(out=stbuf[p][:], in_=stab_d.ap()[:, AG * g:AG * (g + 1)]), sp, 16)
                    e.mark(f"A_h_{g}", sp)
            if "AR" in phases:
                e.wait(s, "A_sacc_done")
                e.inc(s.dma_start(out=cc_in_d.ap(), in_=s_acc[:]), "sHm", 16)
                e.mark("H_ccin", "sHm")
            if "B" in phases:
                def bload(g):
                    p = g % 2
                    if g >= 2:
                        e.wait(s, f"C_done_{g-2}" if "C" in phases else f"B_v_done_{g-2}")
                    else:
                        e.wait(s, f"A_vfree_{NAG-2+g}")
                    if g == 3 and "x1t" in dbg:
                        e.inc(s.dma_start(out=dbg["x1t"].ap(), in_=kphi[:].bitcast(F32)), "sO", 16)
                        e.mark("H_x1dump", "sO")
                    sp = f"sHa{p}"
                    e.inc(s.dma_start(out=xTbuf[p][:], in_=xgT_d.ap()[:, g]), sp, 16)
                    e.inc(s.dma_start(out=ctbuf[p][:], in_=ctab_d.ap()[:, AG * g:AG * (g + 1)]), sp, 16)
                    e.inc(s.dma_start(out=stbuf[p][:], in_=stab_d.ap()[:, AG * g:AG * (g + 1)]), sp, 16)
                    e.mark(f"B_h_{g}", sp)
                if "C" not in phases:
                    for g in range(NAG):
                        bload(g)
            if "sacc" in dbg:
                if "AR" in phases:
                    e.wait(s, "G_sbf")
                    e.inc(s.dma_start(out=dbg["sacc"].ap(), in_=cc_out_d.ap()), "sO", 16)
                else:
                    e.wait(s, "A_sacc_done")
                    e.inc(s.dma_start(out=dbg["sacc"].ap(), in_=s_acc[:]), "sO", 16)
            if "fpt" in dbg:
                e.wait(s, "H_fpt")
                e.inc(s.dma_start(out=dbg["fpt"].ap(), in_=fpt[:]), "sO", 16)
            if "kphi" in dbg:
                e.wait(s, "A_sacc_done")
                e.inc(s.dma_start(out=dbg["kphi"].ap(), in_=kphi[:]), "sO", 16)
            if "C" in phases:
                wdbuf = lambda q: s_acc[:].bitcast(BF16)[:, 3456 * q + 2432:3456 * q + 3456]
                bload(0)
                bload(1)
                for g in range(NAG):
                    for ps in range(2):
                        for fc in range(NFC):
                            q = fc % 2
                            sem = "sHf0" if q == 0 else "sHf1"
                            if ps == 0 and fc < 2:
                                e.wait(s, f"C_c1done_{g}")
                            elif ps == 1 and fc < 2:
                                e.wait(s, f"C_dpsum_{g}_0")
                            else:
                                e.wait(s, f"C_wfree_{g}_{ps}_{fc-2}")
                            e.inc(s.dma_start(out=wdbuf(q),
                                              in_=streambf_d.ap()[fc][:, 2432:3456]), sem, 16)
                            e.mark(f"C_wd_{g}_{ps}_{fc}", sem)
                        e.wait(s, f"C_x2_{g}_{ps}")
                        bi = g * 16 + ps * 8
                        e.inc(s.dma_start(
                            out=y_d.ap().rearrange("a b c -> a (b c)")[:, 256 * bi:256 * (bi + 8)],
                            in_=sqb[:].bitcast(F32)[:, 0:2048]), "sO", 16)
                        e.mark(f"C_y_{g}_{ps}", "sO")
                    if g + 2 < NAG:
                        bload(g + 2)
            if "C" not in phases and stub_y:
                e.inc(s.dma_start(out=y_d.ap().rearrange("a b c -> (a b c)"),
                                  in_=xg_d.ap().rearrange("a b -> (a b)")), "sO", 16)
            e.mark("final_out", "sO")
            e.wait(s, "final_out")

        def prog_gpsimd(g_):
            e = env
            if "P0" in phases:
                e.inc(g_.dma_start(out=diag_all[0:64, :], in_=diag_d.ap()), "sGc", 16)
                e.mark("G_diag", "sGc")
            if "A" in phases:
                if alv >= 2:
                    e.inc(g_.dma_start(out=wkT[:], in_=wkT_d.ap()), "sGw", 16)
                    e.inc(g_.dma_start(out=wvT[:], in_=wvT_d.ap()), "sGw", 16)
                e.mark("G_wkv", "sGw")
                if alv >= 6:
                    i2 = g_.memset(vaug[:].rearrange("p (t h c) -> p t h c", t=4, c=65)[:, :, :, 64:65], 1.0)
                    e.inc(i2, "sC", 1)
                else:
                    e.inc(g_.drain(), "sC", 1)
                e.mark("G_vaug_ones", "sC")
            if "B" in phases:
                e.wait(g_, f"A_te_done_{NAG-1}")
                e.inc(g_.dma_start(out=wkT[:], in_=wqT_d.ap()), "sGw", 16)
                e.inc(g_.dma_start(out=wvT[:], in_=woT_d.ap()), "sGw", 16)
                e.mark("G_wq", "sGw")
            if "C" in phases:
                for fc in range(NFC):
                    e.inc(g_.dma_start(out=streambf_d.ap()[fc], in_=stream_d.ap()[fc]), "sGc", 16)
                e.mark("G_cache", "sGc")
            if "AR" in phases:
                e.wait(g_, "H_ccin")
                cc = g_.collective_compute(
                    "AllReduce", ALU.add,
                    replica_groups=[list(range(NCORES))],
                    ins=[cc_in_d.ap().opt()],
                    outs=[cc_out_d.ap().opt()],
                )
                e.inc(cc, "sC", 1)
                e.mark("C_ccdone", "sC")
                e.wait(g_, "C_ccdone")
                e.inc(g_.dma_start(out=s_bf[:], in_=cc_out_d.ap()), "sGs", 16)
                e.mark("G_sbf", "sGs")
            if "AR" not in phases and "B" in phases:
                e.wait(g_, "A_sacc_done")
                e.inc(g_.dma_start(out=s_bf[:], in_=s_acc[:]), "sGs", 16)
                e.mark("G_sbf", "sGs")
            if "C" in phases:
                if "AR" in phases:
                    e.wait(g_, "H_ccin")
                else:
                    e.wait(g_, "G_sbf")
                e.inc(g_.dma_start(out=gatewT, in_=gate_wT_d.ap()), "sGc", 16)
                e.inc(g_.dma_start(out=a1catTs[:], in_=a1catT_d.ap()), "sGc", 16)
                e.inc(g_.dma_start(out=a3catTs[:], in_=a3catT_d.ap()), "sGc", 16)
                e.inc(g_.dma_start(out=exp8s, in_=exp8_d.ap()), "sGc", 16)
                e.mark("G_cconsts", "sGc")
            if "C" in phases:
                bundle = lambda q: s_acc[:].bitcast(BF16)[:, 3456 * q:3456 * (q + 1)]
                for g in range(NAG):
                    for fc in range(NFC):
                        q = fc % 2
                        sem = "sGc" if q == 0 else "sGw"
                        if fc < 2:
                            e.wait(g_, f"B_te_done_{g}")
                            if g == 0:
                                e.wait(g_, "G_cconsts")
                            else:
                                e.wait(g_, f"C_bfree_{g-1}_{NFC-2+fc}")
                        else:
                            e.wait(g_, f"C_bfree_{g}_{fc-2}")
                        e.inc(g_.dma_start(out=bundle(q), in_=streambf_d.ap()[fc]), sem, 16)
                        e.mark(f"C_bun_{g}_{fc}", sem)
                    e.wait(g_, f"C_c1done_{g}")
                    e.inc(g_.dma_start(out=b2cats, in_=b2cat_d.ap()), "sGc", 16)
                    e.mark(f"C_b2_{g}", "sGc")
                e.mark("G_cdone", "sGc")

        def prog_tensor(t):
            e = env
            if "P0" in phases:
                e.wait(t, "H_consts")
                e.wait(t, "G_diag")
                pf = [K[3][0:64, 0:64], K[3][0:64, 512:576]]
                for h in range(NH):
                    for r in range(NR):
                        base = ((h * NR) + r) * 256
                        dgm = diag_all[0:64, base:base + 64]
                        d2m = diag_all[0:64, base + 64:base + 128]
                        d3m = diag_all[0:64, base + 128:base + 192]
                        d1ptm = diag_all[0:64, base + 192:base + 256]
                        lhs_seq = [hmat, d3m, hmat, d2m, hmat, d1ptm]
                        for si in range(6):
                            rhs = dgm if si == 0 else sorf_tmp[(si + 1) % 2]
                            if si >= 1:
                                e.wait(t, f"S_sorf_{h}_{r}_{si-1}_done")
                            elif not (h == 0 and r == 0):
                                ph, pr = (h, r - 1) if r > 0 else (h - 1, NR - 1)
                                e.wait(t, f"S_sorf_{ph}_{pr}_5_done")
                            mm = t.matmul(pf[si % 2], lhs_seq[si], rhs, start=True, stop=True)
                            e.inc(mm, "sT", 1)
                            e.mark(f"T_sorf_{h}_{r}_{si}", "sT")
                e.mark("T_fpt_mm", "sT")

            if "A" in phases:
                if alv >= 2:
                    e.wait(t, "G_wkv")
                if "P0" in phases:
                    e.wait(t, "H_fpt")
                for g in range(nag):
                    p = g % 2
                    e.wait(t, f"A_h_{g}")
                    if g > 0:
                        if alv >= 5:
                            e.wait(t, f"A_sexp_done_{g-1}")
                        elif alv >= 3:
                            e.wait(t, f"A_t2_{g-1}_7")
                        if alv >= 4:
                            e.wait(t, f"A_nsb_{g-1}")
                        else:
                            e.wait(t, f"A_xh_{g-1}")
                    e.wait(t, f"A_xc_{g}")
                    e.wait(t, f"A_sqx_{g}")
                    for tt in range(4):
                        for d in range(8):
                            mm = t.matmul(K[0][:, 128 * tt:128 * tt + 1],
                                          xh[p][:, AG * d + 128 * tt:AG * d + 128 * (tt + 1)],
                                          onescol2[:],
                                          start=(d == 0), stop=(d == 7))
                        for d in range(8):
                            mm = t.matmul(K[0][:, 512 + 128 * tt:512 + 128 * tt + 1],
                                          sqb[:, AG * d + 128 * tt:AG * d + 128 * (tt + 1)],
                                          onescol2[:],
                                          start=(d == 0), stop=(d == 7))
                    e.inc(mm, "sT", 1)
                    e.mark(f"A_statmm_{g}", "sT")
                    e.wait(t, f"A_statbf_{g}")
                    for j in range(8):
                        dst = K[2][0:1, 128 * j:128 * j + 128] if j < 4 else \
                              K[2][0:1, 512 + 128 * (j - 4):512 + 128 * (j - 4) + 128]
                        mm = t.matmul(dst, mcol_bf[:, j:j + 1], ident[:],
                                      start=True, stop=True)
                    e.inc(mm, "sT", 1)
                    e.mark(f"A_stT_{g}", "sT")
                    e.wait(t, f"A_strow_{g}")
                    mm = t.matmul(half(0, 0), onescol[:], mrow[0:1, AG * g:AG * (g + 1)],
                                  start=True, stop=True)
                    mm = t.matmul(half(0, 1), onescol[:], rrow[0:1, AG * g:AG * (g + 1)],
                                  start=True, stop=True)
                    e.inc(mm, "sT", 1)
                    e.mark(f"A_mbrb_{g}", "sT")
                    e.wait(t, f"A_xh_{g}")
                    for c in range(8 if alv >= 2 else 0):
                        if c >= 2:
                            e.wait(t, f"A_kt_{g}_{c-2}")
                        elif g > 0 and c < 2:
                            e.wait(t, f"A_kt_{g-1}_{6+c}")
                        dst = half(1, c % 2)
                        for d in range(8):
                            mm = t.matmul(dst, wkT[:, 1024 * d + 128 * c:1024 * d + 128 * (c + 1)],
                                          xh[p][:, AG * d:AG * (d + 1)],
                                          start=(d == 0), stop=(d == 7))
                        e.inc(mm, "sT", 1)
                        e.mark(f"A_kp_{g}_{c}", "sT")
                    for c in range(8 if alv >= 3 else 0):
                        if c == 0:
                            e.wait(t, f"A_strow_{g}")
                        e.wait(t, f"A_kt_{g}_{c}")
                        if c >= 2:
                            e.wait(t, f"A_t2_{g}_{c-2}")
                        mm = t.matmul(half(2, c % 2), jmat[:], kT[:, AG * c:AG * (c + 1)],
                                      start=True, stop=True)
                        e.inc(mm, "sT", 1)
                        e.mark(f"A_jp_{g}_{c}", "sT")
                    e.wait(t, "G_vaug_ones") if g == 0 else None
                    for tt in range(4 if alv >= 6 else 0):
                        if tt >= 1:
                            e.wait(t, f"A_vaug_{g}_{tt-1}")
                        elif g > 0:
                            e.wait(t, f"A_vaug_{g-1}_3")
                            if alv >= 7:
                                e.wait(t, f"A_spadd_{g-1}_31")
                        for nh2 in range(2):
                            dst = half(3, nh2)
                            for d in range(8):
                                mm = t.matmul(dst,
                                              xh[p][:, AG * d + 128 * tt:AG * d + 128 * (tt + 1)],
                                              wvT[:, 1024 * d + 512 * nh2:1024 * d + 512 * (nh2 + 1)],
                                              start=(d == 0), stop=(d == 7))
                        e.inc(mm, "sT", 1)
                        e.mark(f"A_vp_{g}_{tt}", "sT")
                    e.wait(t, f"A_xh_{g}")
                    if alv >= 4:
                        e.wait(t, f"A_sq_{g}")
                    for tt in range(4 if alv >= 4 else 0):
                        for c in range(8):
                            mm = t.matmul(K[0][:, 128 * tt:128 * tt + 16],
                                          sqb[:, AG * c + 128 * tt:AG * c + 128 * (tt + 1)],
                                          sel[:, 16 * c:16 * (c + 1)],
                                          start=(c == 0), stop=(c == 7))
                        e.inc(mm, "sT", 1)
                    e.mark(f"A_ns_{g}", "sT")
                    if alv >= 5:
                        e.wait(t, f"A_t2_{g}_7")
                    cnt = 0
                    for h in range(NH if alv >= 5 else 0):
                        c, par = h // 2, (h % 2) * 64
                        for tt in range(4):
                            if cnt >= 2:
                                e.wait(t, f"A_expk_{g}_{cnt-2}")
                            dst = half(2, cnt % 2)[:, 0:256]
                            sl = slice(AG * c + 128 * tt, AG * c + 128 * (tt + 1))
                            mm = t.matmul(dst, t1b[par:par + 64, sl],
                                          fpt[par:par + 64, 256 * c:256 * (c + 1)],
                                          start=True, stop=False)
                            mm = t.matmul(dst, t2b[par:par + 64, sl],
                                          fpt[par:par + 64, 256 * c:256 * (c + 1)],
                                          start=False, stop=True)
                            e.inc(mm, "sT", 1)
                            e.mark(f"A_fk_{g}_{cnt}", "sT")
                            cnt += 1
                    if alv >= 7:
                        e.wait(t, f"A_vaug_{g}_3")
                        e.wait(t, f"A_expk_{g}_63")
                    scnt = 0
                    slot_cols = [0, 512, 128, 640]
                    for h in range(NH if alv >= 7 else 0):
                        for m in range(2):
                            if scnt >= 2:
                                e.wait(t, f"A_spadd_{g}_{scnt-2}")
                            dst = K[3][:, slot_cols[scnt % 4]:slot_cols[scnt % 4] + 65]
                            for tt in range(4):
                                kbase = (h * 4 + tt) * 256 + 128 * m
                                mm = t.matmul(dst, kphi[:, kbase:kbase + 128],
                                              vaug[:, 1040 * tt + 65 * h:1040 * tt + 65 * (h + 1)],
                                              start=(tt == 0), stop=(tt == 3))
                            e.inc(mm, "sT", 1)
                            e.mark(f"A_sp_{g}_{scnt}", "sT")
                            scnt += 1
                    e.mark(f"A_te_done_{g}", "sT")

            if "B" in phases:
                qT = kT
                woTb = wvT
                e.wait(t, "G_wq")
                e.wait(t, "G_sbf")
                e.wait(t, f"A_sexp_done_{NAG-1}")
                e.wait(t, f"A_nsb_{NAG-1}")
                for g in range(NAG):
                    p = g % 2
                    cc = g // (NAG // 2)
                    e.wait(t, f"B_h_{g}")
                    if g > 0:
                        e.wait(t, f"B_t2_{g-1}_7")
                    mm = t.matmul(half(0, 0), onescol[:], mrow[0:1, AG * g:AG * (g + 1)],
                                  start=True, stop=True)
                    mm = t.matmul(half(0, 1), onescol[:], rrow[0:1, AG * g:AG * (g + 1)],
                                  start=True, stop=True)
                    e.inc(mm, "sT", 1)
                    e.mark(f"B_mbrb_{g}", "sT")
                    e.wait(t, f"B_xh_{g}")
                    for c in range(8):
                        if c >= 2:
                            e.wait(t, f"B_qt_{g}_{c-2}")
                        elif g > 0:
                            e.wait(t, f"B_qt_{g-1}_{6+c}")
                        dst = half(1, c % 2)
                        for d in range(8):
                            mm = t.matmul(dst, wkT[:, 1024 * d + 128 * c:1024 * d + 128 * (c + 1)],
                                          xh[p][:, AG * d:AG * (d + 1)],
                                          start=(d == 0), stop=(d == 7))
                        e.inc(mm, "sT", 1)
                        e.mark(f"B_qp_{g}_{c}", "sT")
                    for c in range(8):
                        e.wait(t, f"B_qt_{g}_{c}")
                        if c >= 2:
                            e.wait(t, f"B_t2_{g}_{c-2}")
                        mm = t.matmul(half(0, c % 2), jmat[:], qT[:, AG * c:AG * (c + 1)],
                                      start=True, stop=True)
                        e.inc(mm, "sT", 1)
                        e.mark(f"B_jp_{g}_{c}", "sT")
                    e.wait(t, f"B_sq_{g}")
                    if g > 0:
                        e.wait(t, f"B_div_{g-1}_7")
                        e.wait(t, f"B_div_{g-1}_6")
                    for tt in range(4):
                        for c in range(8):
                            mm = t.matmul(K[2][:, 128 * tt:128 * tt + 16],
                                          sqb[:, AG * c + 128 * tt:AG * c + 128 * (tt + 1)],
                                          sel[:, 16 * c:16 * (c + 1)],
                                          start=(c == 0), stop=(c == 7))
                    e.inc(mm, "sT", 1)
                    e.mark(f"B_ns_{g}", "sT")
                    e.wait(t, f"B_t2_{g}_7")
                    if g > 0:
                        e.wait(t, f"B_v_done_{g-1}")
                    for b in range(8):
                        if b >= 2:
                            e.wait(t, f"B_div_{g}_{b-2}")
                        npsum = K[2] if b % 2 == 0 else K[3]
                        for hh in range(2):
                            h = 2 * b + hh
                            c2, par = h // 2, (h % 2) * 64
                            if h >= 2:
                                e.wait(t, f"B_exp_{g}_{h-2}")
                            for m in range(2):
                                dst = half(hh, m) if False else half(0 if hh == 0 else 1, m)
                                sl = slice(AG * c2, AG * (c2 + 1))
                                mm = t.matmul(dst, fpt[par:par + 64, 256 * c2 + 128 * m:256 * c2 + 128 * (m + 1)].tensor[par:par + 64, 256 * c2 + 128 * m:256 * c2 + 128 * (m + 1)] if False else fpt[par:par + 64, 256 * c2 + 128 * m:256 * c2 + 128 * (m + 1)],
                                              t1b[par:par + 64, sl], start=True, stop=False)
                                mm = t.matmul(dst, fpt[par:par + 64, 256 * c2 + 128 * m:256 * c2 + 128 * (m + 1)],
                                              t2b[par:par + 64, sl], start=False, stop=True)
                            e.inc(mm, "sT", 1)
                            e.mark(f"B_ft_{g}_{h}", "sT")
                            e.wait(t, f"B_exp_{g}_{h}")
                            for tt in range(4):
                                dst = npsum[:, 128 * (4 * hh + tt):128 * (4 * hh + tt) + 65]
                                for m in range(2):
                                    off = 65 * ((cc * 16 + h) * 2 + m)
                                    mm = t.matmul(dst,
                                                  qphib_ap()[:, 1024 * (h % 2) + 512 * m + 128 * tt:1024 * (h % 2) + 512 * m + 128 * (tt + 1)],
                                                  s_bf[:, off:off + 65],
                                                  start=(m == 0), stop=(m == 1))
                            e.inc(mm, "sT", 1)
                        e.mark(f"B_nmm_{g}_{b}", "sT")
                    e.wait(t, f"B_div_{g}_7")
                    e.wait(t, f"B_div_{g}_6")
                    tslots = [K[2][:, 0:64].bitcast(BF16), K[2][:, 512:576].bitcast(BF16),
                              K[3][:, 0:64].bitcast(BF16), K[3][:, 512:576].bitcast(BF16)]
                    ti = 0
                    for tt in range(4):
                        for rnd in range(2):
                            if ti >= 1:
                                e.wait(t, f"B_oTev_{g}_{ti-1}")
                            for j in range(4):
                                c = 4 * rnd + j
                                mm = t.transpose(tslots[j][:, 0:128],
                                                 ohbuf[:, 1024 * tt + 128 * c:1024 * tt + 128 * (c + 1)],
                                                 ident[:])
                            e.inc(mm, "sT", 1)
                            e.mark(f"B_oTt_{g}_{ti}", "sT")
                            ti += 1
                    e.wait(t, f"B_oTev_{g}_7")
                    for dc in range(8):
                        if dc >= 2:
                            e.wait(t, f"B_at_{g}_{dc-2}")
                        dst = half(0, dc % 2)
                        for ec in range(8):
                            mm = t.matmul(dst, woTb[:, 1024 * ec + 128 * dc:1024 * ec + 128 * (dc + 1)],
                                          oTbuf_ap()[:, 512 * ec:512 * (ec + 1)],
                                          start=(ec == 0), stop=(ec == 7))
                        e.inc(mm, "sT", 1)
                        e.mark(f"B_pj_{g}_{dc}", "sT")
                    e.wait(t, f"B_atsq_{g}")
                    for tt in range(4):
                        for d in range(8):
                            mm = t.matmul(K[1][:, 128 * tt:128 * tt + 1],
                                          sqb[:, 512 * d + 128 * tt:512 * d + 128 * (tt + 1)],
                                          onescol2[:], start=(d == 0), stop=(d == 7))
                        for d in range(8):
                            mm = t.matmul(K[1][:, 512 + 128 * tt:512 + 128 * tt + 1],
                                          t2b[:, 512 * d + 128 * tt:512 * d + 128 * (tt + 1)],
                                          onescol2[:], start=(d == 0), stop=(d == 7))
                    e.inc(mm, "sT", 1)
                    e.mark(f"B_statmm_{g}", "sT")
                    e.wait(t, f"B_statbf_{g}")
                    for j in range(8):
                        dst = K[2][0:1, 128 * j:128 * j + 128] if j < 4 else \
                              K[2][0:1, 512 + 128 * (j - 4):512 + 128 * (j - 4) + 128]
                        mm = t.matmul(dst, mcol_bf[:, j:j + 1], ident[:], start=True, stop=True)
                    e.inc(mm, "sT", 1)
                    e.mark(f"B_stT_{g}", "sT")
                    e.wait(t, f"B_strow_{g}")
                    mm = t.matmul(half(3, 0), onescol[:], m2row, start=True, stop=True)
                    mm = t.matmul(half(3, 1), onescol[:], r2row, start=True, stop=True)
                    e.inc(mm, "sT", 1)
                    e.mark(f"B_mbrb2_{g}", "sT")
                    e.mark(f"B_te_done_{g}", "sT")

                    if "C" not in phases:
                        continue
                    x1T = kphi[:].bitcast(F32)
                    x1off = lambda c: 1024 * c + 512 * (g % 2)
                    bundle = lambda qq: s_acc[:].bitcast(BF16)[:, 3456 * qq:3456 * (qq + 1)]
                    wdbuf = lambda qq: s_acc[:].bitcast(BF16)[:, 3456 * qq + 2432:3456 * qq + 3456]
                    if g == 0:
                        e.wait(t, "G_cconsts")
                    e.wait(t, f"C_xb_{g}")
                    e.wait(t, f"C_sqx3_{g}")
                    for tt in range(4):
                        for d in range(8):
                            mm = t.matmul(K[0][:, 128 * tt:128 * tt + 1],
                                          xh[g % 2][:, 512 * d + 128 * tt:512 * d + 128 * (tt + 1)],
                                          onescol2[:], start=(d == 0), stop=(d == 7))
                        for d in range(8):
                            mm = t.matmul(K[0][:, 512 + 128 * tt:512 + 128 * tt + 1],
                                          t2b[:, 512 * d + 128 * tt:512 * d + 128 * (tt + 1)],
                                          onescol2[:], start=(d == 0), stop=(d == 7))
                    e.inc(mm, "sT", 1)
                    e.mark(f"C_statmm_{g}", "sT")
                    e.wait(t, f"C_statbf_{g}")
                    for j in range(8):
                        dst = K[2][0:1, 128 * j:128 * j + 128] if j < 4 else \
                              K[2][0:1, 512 + 128 * (j - 4):512 + 128 * (j - 4) + 128]
                        mm = t.matmul(dst, mcol_bf[:, j:j + 1], ident[:], start=True, stop=True)
                    e.inc(mm, "sT", 1)
                    e.mark(f"C_stT_{g}", "sT")
                    e.wait(t, f"C_strow_{g}")
                    mm = t.matmul(half(0, 0), onescol[:], m2row, start=True, stop=True)
                    mm = t.matmul(half(0, 1), onescol[:], r2row, start=True, stop=True)
                    e.inc(mm, "sT", 1)
                    e.mark(f"C_mbrb3_{g}", "sT")
                    e.wait(t, f"C_x2h_{g}")
                    for tt in range(4):
                        for c in range(8):
                            mm = t.matmul(K[1][:, 128 * tt:128 * tt + 8],
                                          t1b[:, 512 * c + 128 * tt:512 * c + 128 * (tt + 1)],
                                          gatewT[:, 8 * c:8 * (c + 1)],
                                          start=(c == 0), stop=(c == 7))
                    e.inc(mm, "sT", 1)
                    e.mark(f"C_gate_{g}", "sT")
                    e.wait(t, f"C_top2_{g}")
                    for tt in range(4):
                        mm = t.matmul(K[2][0:8, 128 * tt:128 * (tt + 1)],
                                      mask12[:, 16 * tt:16 * tt + 8], ident[:],
                                      start=True, stop=True)
                        mm = t.matmul(K[2][0:8, 512 + 128 * tt:512 + 128 * (tt + 1)],
                                      mask12[:, 16 * tt + 8:16 * (tt + 1)], ident[:],
                                      start=True, stop=True)
                        mm = t.matmul(K[3][0:1, 128 * tt:128 * (tt + 1)],
                                      rwb[:, 2 * tt:2 * tt + 1], ident[:],
                                      start=True, stop=True)
                        mm = t.matmul(K[3][0:1, 512 + 128 * tt:512 + 128 * (tt + 1)],
                                      rwb[:, 2 * tt + 1:2 * tt + 2], ident[:],
                                      start=True, stop=True)
                    e.inc(mm, "sT", 1)
                    e.mark(f"C_maskT_{g}", "sT")
                    e.wait(t, f"C_mrows_{g}")
                    mm = t.matmul(half(3, 0), exp8s, ctbuf[g % 2][0:8, :], start=True, stop=True)
                    mm = t.matmul(half(3, 1), exp8s, stbuf[g % 2][0:8, :], start=True, stop=True)
                    e.inc(mm, "sT", 1)
                    e.mark(f"C_mer_{g}", "sT")
                    e.wait(t, f"C_merc_{g}")
                    mm = t.matmul(half(3, 0), onescol[:], m2row, start=True, stop=True)
                    mm = t.matmul(half(3, 1), onescol[:], r2row, start=True, stop=True)
                    e.inc(mm, "sT", 1)
                    e.mark(f"C_wb_{g}", "sT")
                    e.wait(t, f"C_wbc_{g}")
                    for d in range(8):
                        mm = t.matmul(half(3, 0), a1catTs[:, 128 * d:128 * (d + 1)],
                                      t1b[:, 512 * d:512 * (d + 1)],
                                      start=(d == 0), stop=(d == 7))
                    e.inc(mm, "sT", 1)
                    e.mark(f"C_u1_{g}", "sT")
                    for d in range(8):
                        mm = t.matmul(half(3, 1), a3catTs[:, 128 * d:128 * (d + 1)],
                                      t1b[:, 512 * d:512 * (d + 1)],
                                      start=(d == 0), stop=(d == 7))
                    e.inc(mm, "sT", 1)
                    e.mark(f"C_u3_{g}", "sT")
                    for fc in range(NFC):
                        qq = fc % 2
                        e.wait(t, f"C_bun_{g}_{fc}")
                        if fc == 0:
                            e.wait(t, f"C_mu3_{g}")
                        if fc > 0:
                            e.wait(t, f"S_silu_{g}_{fc-1}")
                            e.wait(t, f"C_mix_{g}_{fc-1}")
                        for d in range(8):
                            mm = t.matmul(half(0, 0), bundle(qq)[:, 128 * d:128 * (d + 1)],
                                          t1b[:, 512 * d:512 * (d + 1)],
                                          start=(d == 0), stop=(d == 7))
                        for d in range(8):
                            mm = t.matmul(half(1, 0), bundle(qq)[:, 1024 + 128 * d:1024 + 128 * (d + 1)],
                                          t1b[:, 512 * d:512 * (d + 1)],
                                          start=(d == 0), stop=(d == 7))
                        e.inc(mm, "sT", 1)
                        e.mark(f"C_cc_{g}_{fc}", "sT")
                        e.wait(t, f"S_cs_{g}_{fc}")
                        mm = t.matmul(half(0, 0), bundle(qq)[:, 2048:2176],
                                      sqb[:, 0:512], start=False, stop=True,
                                      skip_group_check=True)
                        mm = t.matmul(half(1, 0), bundle(qq)[:, 2176:2304],
                                      sqb[:, 1024:1536], start=False, stop=True,
                                      skip_group_check=True)
                        mm = t.matmul(half(0, 1), bundle(qq)[:, 2048:2176],
                                      sqb[:, 512:1024], start=True, stop=True)
                        mm = t.matmul(half(1, 1), bundle(qq)[:, 2176:2304],
                                      sqb[:, 1536:2048], start=True, stop=True)
                        e.inc(mm, "sT", 1)
                        e.mark(f"C_dd_{g}_{fc}", "sT")
                        e.wait(t, f"C_w1_{g}_{fc}")
                        mm = t.matmul(half(2, 0), bundle(qq)[:, 2304:2432],
                                      t2b[:, 3072:3584],
                                      start=(fc == 0), stop=(fc == NFC - 1))
                        e.wait(t, f"C_mix_{g}_{fc}")
                        mm = t.matmul(half(2, 1), bundle(qq)[:, 2304:2432],
                                      mixT_ap(g, fc), start=(fc == 0), stop=(fc == NFC - 1))
                        e.inc(mm, "sT", 1)
                        e.mark(f"C_z_{g}_{fc}", "sT")
                        e.mark(f"C_bfree_{g}_{fc}", "sT")
                    e.mark(f"C_c1done_{g}", "sT")
                    for ps in range(2):
                        for fc in range(NFC):
                            qq = fc % 2
                            e.wait(t, f"C_wd_{g}_{ps}_{fc}")
                            if ps == 0 and fc == 0:
                                e.wait(t, f"C_zhat_{g}")
                            if ps == 1 and fc == 0:
                                e.wait(t, f"C_ffnt_{g}_0")
                            for q in range(2):
                                for dc in range(8):
                                    dst = (K[2] if q == 0 else K[3])[:, 128 * dc:128 * (dc + 1)]
                                    mm = t.matmul(dst, wdbuf(qq)[:, 128 * dc:128 * (dc + 1)],
                                                  mixT_ap(g, fc)[:, 256 * ps + 128 * q:256 * ps + 128 * (q + 1)],
                                                  start=(fc == 0 and dc % 4 == 0), stop=True,
                                                  skip_group_check=True)
                            e.inc(mm, "sT", 1)
                            e.mark(f"C_wfree_{g}_{ps}_{fc}", "sT")
                        if ps == 0:
                            e.wait(t, f"C_b2_{g}")
                        for q in range(2):
                            for dc in range(8):
                                dst = (K[2] if q == 0 else K[3])[:, 128 * dc:128 * (dc + 1)]
                                mm = t.matmul(dst, b2cats[:, 128 * dc:128 * (dc + 1)],
                                              zhat[:, 256 * ps + 128 * q:256 * ps + 128 * (q + 1)],
                                              start=False, stop=True, skip_group_check=True)
                        e.inc(mm, "sT", 1)
                        e.mark(f"C_dpsum_{g}_{ps}", "sT")
                        e.wait(t, f"C_ffnt_{g}_{ps}")
                        for q in range(2):
                            for dc in range(8):
                                mm = t.matmul(K[0][:, 128 * q:128 * q + 1],
                                              xh[g % 2][:, (q * 8 + dc) * 128:(q * 8 + dc + 1) * 128],
                                              onescol2[:], start=(dc == 0), stop=(dc == 7))
                            for dc in range(8):
                                mm = t.matmul(K[0][:, 512 + 128 * q:512 + 128 * q + 1],
                                              vaug[:, (q * 8 + dc) * 128:(q * 8 + dc + 1) * 128],
                                              onescol2[:], start=(dc == 0), stop=(dc == 7))
                        e.inc(mm, "sT", 1)
                        e.mark(f"C_statmm4_{g}_{ps}", "sT")
                        e.wait(t, f"C_statbf4_{g}_{ps}")
                        for (j, col) in [(0, 0), (1, 128), (4, 512), (5, 640)]:
                            mm = t.matmul(K[1][0:1, col:col + 128], mcol_bf[:, j:j + 1],
                                          ident[:], start=True, stop=True)
                        e.inc(mm, "sT", 1)
                        e.mark(f"C_stT4_{g}_{ps}", "sT")
                        e.wait(t, f"C_strow4_{g}_{ps}")
                        mm = t.matmul(K[1][:, 0:256], onescol[:], m2row[0:1, 0:256],
                                      start=True, stop=True)
                        mm = t.matmul(K[1][:, 512:768], onescol[:], r2row[0:1, 0:256],
                                      start=True, stop=True)
                        e.inc(mm, "sT", 1)
                        e.mark(f"C_mbrb4_{g}_{ps}", "sT")
                    e.mark(f"C_te_done_{g}", "sT")

        def prog_scalar(sc):
            e = env
            if "P0" in phases:
                pf = [K[3][0:64, 0:64], K[3][0:64, 512:576]]
                for h in range(NH):
                    for r in range(NR):
                        for si in range(6):
                            e.wait(sc, f"T_sorf_{h}_{r}_{si}")
                            src = pf[si % 2]
                            if si < 5:
                                i2 = sc.activation(sorf_tmp[si % 2], src, AF.Identity,
                                                   bias=0.0, scale=1.0)
                            else:
                                if h % 2 == 0:
                                    dst = fpt[0:64, 256 * (h // 2) + 64 * r:256 * (h // 2) + 64 * (r + 1)]
                                else:
                                    dst = fstage[0:64, 256 * (h // 2) + 64 * r:256 * (h // 2) + 64 * (r + 1)]
                                i2 = sc.activation(dst, src, AF.Identity, bias=0.0, scale=INV3)
                            e.inc(i2, "sS", 1)
                            e.mark(f"S_sorf_{h}_{r}_{si}_done", "sS")
                e.mark("S_fpt_evac", "sS")

            if "A" in phases:
                for g in range(nag):
                    e.wait(sc, f"A_xc_{g}")
                    if g > 0:
                        e.wait(sc, f"A_ns_{g-1}")
                    if sub & 8:
                        i2 = sc.drain()
                    else:
                        i2 = sc.activation(sqb[:], xh[g % 2][:], AF.Square)
                    e.inc(i2, "sS", 1)
                    e.mark(f"A_sqx_{g}", "sS")
                    e.wait(sc, f"A_stat_{g}")
                    i2 = sc.activation(rstd4[:], rv4[:], AF.Sqrt)
                    e.inc(i2, "sS", 1)
                    e.mark(f"A_sqrt_{g}", "sS")
                    e.wait(sc, f"A_stT_{g}")
                    col = 512 * g
                    i2 = sc.activation(mrow[0:1, col:col + 512], K[2][0:1, 0:512],
                                       AF.Identity, bias=0.0, scale=1.0)
                    i2 = sc.activation(rrow[0:1, col:col + 512], K[2][0:1, 512:1024],
                                       AF.Identity, bias=0.0, scale=1.0)
                    e.inc(i2, "sS", 1)
                    e.mark(f"A_strow_{g}", "sS")
                    for c in range(8 if alv >= 2 else 0):
                        e.wait(sc, f"A_kp_{g}_{c}")
                        if c == 0:
                            e.wait(sc, f"A_statmm_{g}")
                        if g > 0 and c == 0:
                            e.wait(sc, f"A_te_done_{g-1}")
                        i2 = sc.copy(kT[:, AG * c:AG * (c + 1)], half(1, c % 2))
                        e.inc(i2, "sS", 1)
                        e.mark(f"A_kt_{g}_{c}", "sS")
                        sc.drain()
                        i2 = sc.activation(sqb[:, AG * c:AG * (c + 1)],
                                           kT[:, AG * c:AG * (c + 1)], AF.Square)
                        e.inc(i2, "sS", 1)
                    e.mark(f"A_sq_{g}", "sS")
                    for tt in range(4 if alv >= 6 else 0):
                        e.wait(sc, f"A_vp_{g}_{tt}")
                        base = 1040 * tt
                        dst = vaug[:, base:base + 1040].rearrange("p (h c) -> p h c", c=65)
                        i2 = sc.copy(dst[:, 0:8, 0:64],
                                     half(3, 0).rearrange("p (h c) -> p h c", c=64))
                        i2 = sc.copy(dst[:, 8:16, 0:64],
                                     half(3, 1).rearrange("p (h c) -> p h c", c=64))
                        e.inc(i2, "sS", 1)
                        e.mark(f"A_vaug_{g}_{tt}", "sS")
                    if alv >= 4:
                        e.wait(sc, f"A_ns_{g}")
                        e.wait(sc, "V_cvec")
                    for tt in range(4 if alv >= 4 else 0):
                        i2 = sc.activation(nsb[:, 16 * tt:16 * (tt + 1)],
                                           K[0][:, 128 * tt:128 * tt + 16],
                                           AF.Identity, bias=cvec[:, 0:1], scale=-0.5)
                        e.inc(i2, "sS", 1)
                    sc.drain()
                    e.mark(f"A_nsb_{g}", "sS")
                    cnt = 0
                    for h in range(NH if alv >= 5 else 0):
                        for tt in range(4):
                            e.wait(sc, f"A_fk_{g}_{cnt}")
                            i2 = sc.activation(kphi[:, (h * 4 + tt) * 256:(h * 4 + tt + 1) * 256],
                                               half(2, cnt % 2)[:, 0:256], AF.Exp,
                                               bias=nsb[:, 16 * tt + h:16 * tt + h + 1],
                                               scale=1.0)
                            e.inc(i2, "sS", 1)
                            e.mark(f"A_expk_{g}_{cnt}", "sS")
                            cnt += 1
                    e.mark(f"A_sexp_done_{g}", "sS")

            if "B" in phases:
                qT = kT
                for g in range(NAG):
                    p = g % 2
                    for c in range(8):
                        e.wait(sc, f"B_qp_{g}_{c}")
                        if g == 0 and c == 0:
                            e.wait(sc, f"A_te_done_{NAG-1}")
                        elif g > 0 and c == 0:
                            e.wait(sc, f"B_te_done_{g-1}")
                        if g > 0 and c == 0 and "C" in phases:
                            e.wait(sc, f"C_y_{g-1}_1")
                        if g == 1 and c == 0 and "attnt" in dbg:
                            e.wait(sc, "H_atdump")
                        i2 = sc.copy(qT[:, AG * c:AG * (c + 1)], half(1, c % 2))
                        e.inc(i2, "sS", 1)
                        e.mark(f"B_qt_{g}_{c}", "sS")
                        sc.drain()
                        i2 = sc.activation(sqb[:, AG * c:AG * (c + 1)],
                                           qT[:, AG * c:AG * (c + 1)], AF.Square)
                        e.inc(i2, "sS", 1)
                    e.mark(f"B_sq_{g}", "sS")
                    e.wait(sc, f"B_ns_{g}")
                    e.wait(sc, "V_cvec")
                    for tt in range(4):
                        i2 = sc.activation(fsc[:, 16 * tt:16 * (tt + 1)],
                                           K[2][:, 128 * tt:128 * tt + 16],
                                           AF.Exp, bias=cvec[:, 0:1], scale=-0.5)
                        e.inc(i2, "sS", 1)
                    sc.drain()
                    e.mark(f"B_fsc_{g}", "sS")
                    for h in range(NH):
                        e.wait(sc, f"B_ft_{g}_{h}")
                        hh = h % 2
                        base = 1024 * (h % 2)
                        for m in range(2):
                            i2 = sc.activation(qphib_ap()[:, base + 512 * m:base + 512 * (m + 1)],
                                               half(0 if hh == 0 else 1, m), AF.Exp)
                        e.inc(i2, "sS", 1)
                        e.mark(f"B_exp_{g}_{h}", "sS")
                    tslots = [K[2][:, 0:64].bitcast(BF16), K[2][:, 512:576].bitcast(BF16),
                              K[3][:, 0:64].bitcast(BF16), K[3][:, 512:576].bitcast(BF16)]
                    ti = 0
                    for tt in range(4):
                        for rnd in range(2):
                            e.wait(sc, f"B_oTt_{g}_{ti}")
                            for j in range(4):
                                c = 4 * rnd + j
                                i2 = sc.copy(oTbuf_ap()[:, 512 * c + 128 * tt:512 * c + 128 * (tt + 1)],
                                             tslots[j][:, 0:128])
                            e.inc(i2, "sS", 1)
                            e.mark(f"B_oTev_{g}_{ti}", "sS")
                            ti += 1
                    for dc in range(8):
                        e.wait(sc, f"B_pj_{g}_{dc}")
                        i2 = sc.copy(sqb[:, 512 * dc:512 * (dc + 1)], half(0, dc % 2))
                        e.inc(i2, "sS", 1)
                        e.mark(f"B_at_{g}_{dc}", "sS")
                    sc.drain()
                    i2 = sc.activation(t2b[:], sqb[:], AF.Square)
                    e.inc(i2, "sS", 1)
                    e.mark(f"B_atsq_{g}", "sS")
                    e.wait(sc, f"B_stat_{g}")
                    i2 = sc.activation(rstd4[:], rv4[:], AF.Sqrt)
                    e.inc(i2, "sS", 1)
                    e.mark(f"B_sqrt_{g}", "sS")
                    e.wait(sc, f"B_stT_{g}")
                    i2 = sc.activation(m2row, K[2][0:1, 0:512],
                                       AF.Identity, bias=0.0, scale=1.0)
                    i2 = sc.activation(r2row, K[2][0:1, 512:1024],
                                       AF.Identity, bias=0.0, scale=1.0)
                    e.inc(i2, "sS", 1)
                    e.mark(f"B_strow_{g}", "sS")

                    if "C" not in phases:
                        continue
                    sqbf32 = sqb[:].bitcast(F32)
                    e.wait(sc, f"C_xb_{g}")
                    i2 = sc.activation(t2b[:], xh[g % 2][:], AF.Square)
                    e.inc(i2, "sS", 1)
                    e.mark(f"C_sqx3_{g}", "sS")
                    e.wait(sc, f"C_stat_{g}")
                    i2 = sc.activation(rstd4[:], rv4[:], AF.Sqrt)
                    e.inc(i2, "sS", 1)
                    e.mark(f"C_sqrt_{g}", "sS")
                    e.wait(sc, f"C_stT_{g}")
                    i2 = sc.activation(m2row, K[2][0:1, 0:512], AF.Identity,
                                       bias=0.0, scale=1.0)
                    i2 = sc.activation(r2row, K[2][0:1, 512:1024], AF.Identity,
                                       bias=0.0, scale=1.0)
                    e.inc(i2, "sS", 1)
                    e.mark(f"C_strow_{g}", "sS")
                    e.wait(sc, f"C_top2m_{g}")
                    i2 = sc.activation(fsc[:, 0:4], tmp8a[:, 4:8], AF.Sigmoid)
                    e.inc(i2, "sS", 1)
                    e.mark(f"S_sig_{g}", "sS")
                    e.wait(sc, f"C_maskT_{g}")
                    i2 = sc.copy(ctbuf[g % 2][0:8, :], K[2][0:8, 0:512])
                    i2 = sc.copy(stbuf[g % 2][0:8, :], K[2][0:8, 512:1024])
                    i2 = sc.activation(m2row, K[3][0:1, 0:512], AF.Identity,
                                       bias=0.0, scale=1.0)
                    i2 = sc.activation(r2row, K[3][0:1, 512:1024], AF.Identity,
                                       bias=0.0, scale=1.0)
                    e.inc(i2, "sS", 1)
                    e.mark(f"C_mrows_{g}", "sS")
                    for fc in range(NFC):
                        e.wait(sc, f"C_cc_{g}_{fc}")
                        i2 = sc.copy(sqbf32[:, 1024:1536], half(0, 0))
                        i2 = sc.copy(sqbf32[:, 1536:2048], half(1, 0))
                        e.inc(i2, "sS", 1)
                        e.mark(f"S_cs_{g}_{fc}", "sS")
                        e.wait(sc, f"C_dd_{g}_{fc}")
                        i2 = sc.activation(vaug[:].bitcast(F32)[:, 0:512], half(0, 0), AF.Silu)
                        e.inc(i2, "sS", 1)
                        e.mark(f"S_silu1_{g}_{fc}", "sS")
                        e.wait(sc, f"C_add_{g}_{fc}")
                        i2 = sc.activation(vaug[:].bitcast(F32)[:, 512:1024],
                                           vaug[:].bitcast(F32)[:, 1024:1536], AF.Silu)
                        e.inc(i2, "sS", 1)
                        e.mark(f"S_silu_{g}_{fc}", "sS")
                    for ps in range(2):
                        e.wait(sc, f"C_dpsum_{g}_{ps}")
                        i2 = sc.copy(xh[g % 2][:, 0:1024], K[2][:])
                        i2 = sc.copy(xh[g % 2][:, 1024:2048], K[3][:])
                        sc.drain()
                        i2 = sc.activation(vaug[:, 0:2048], xh[g % 2][:, 0:2048], AF.Square)
                        e.inc(i2, "sS", 1)
                        e.mark(f"C_ffnt_{g}_{ps}", "sS")
                        e.wait(sc, f"C_stat4_{g}_{ps}")
                        i2 = sc.activation(rstd4[:, 0:2], rv4[:, 0:2], AF.Sqrt)
                        e.inc(i2, "sS", 1)
                        e.mark(f"C_sqrt4_{g}_{ps}", "sS")
                        e.wait(sc, f"C_stT4_{g}_{ps}")
                        i2 = sc.activation(m2row[0:1, 0:256], K[1][0:1, 0:256], AF.Identity,
                                           bias=0.0, scale=1.0)
                        i2 = sc.activation(r2row[0:1, 0:256], K[1][0:1, 512:768], AF.Identity,
                                           bias=0.0, scale=1.0)
                        e.inc(i2, "sS", 1)
                        e.mark(f"C_strow4_{g}_{ps}", "sS")

        def prog_vector(v):
            e = env
            if "A" in phases:
                v.memset(s_acc[:], 0.0)
                i2 = v.memset(cvec[:, 0:1], -LN16)
                v.drain()
                e.inc(i2, "sV", 1)
                e.mark("V_cvec", "sV")
                for g in range(nag):
                    p = g % 2
                    e.wait(v, f"A_h_{g}")
                    if g > 0:
                        if alv >= 4:
                            e.wait(v, f"A_ns_{g-1}")
                        if alv >= 2:
                            e.wait(v, f"A_kp_{g-1}_7")
                        if alv >= 6:
                            e.wait(v, f"A_vp_{g-1}_3")
                        e.wait(v, f"A_statmm_{g-1}")
                    i2 = v.tensor_copy(xh[p][:], xTbuf[p][:])
                    v.drain()
                    e.inc(i2, "sV", 1)
                    e.mark(f"A_xc_{g}", "sV")
                    e.wait(v, f"A_statmm_{g}")
                    v.tensor_scalar(stat4m[:], K[0][:, 0:512].rearrange("p (tt o) -> p tt o", o=128)[:, :, 0:1].squeeze(2),
                                    1.0 / D_MODEL, None, ALU.mult)
                    v.tensor_scalar(stat4s[:], K[0][:, 512:1024].rearrange("p (tt o) -> p tt o", o=128)[:, :, 0:1].squeeze(2),
                                    1.0 / D_MODEL, None, ALU.mult)
                    v.drain()
                    v.tensor_tensor(tmp4[:], stat4m[:], stat4m[:], ALU.mult)
                    v.drain()
                    v.tensor_sub(stat4s[:], stat4s[:], tmp4[:])
                    v.drain()
                    v.tensor_scalar(stat4s[:], stat4s[:], EPS_LN, None, ALU.add)
                    v.drain()
                    i2 = v.reciprocal(rv4[:], stat4s[:])
                    e.inc(i2, "sV", 1)
                    e.mark(f"A_stat_{g}", "sV")
                    e.wait(v, f"A_sqrt_{g}")
                    if g > 0:
                        e.wait(v, f"A_stT_{g-1}")
                    v.tensor_copy(mcol_bf[:, 0:4], stat4m[:])
                    i2 = v.tensor_copy(mcol_bf[:, 4:8], rstd4[:])
                    v.drain()
                    e.inc(i2, "sV", 1)
                    e.mark(f"A_statbf_{g}", "sV")
                    e.wait(v, f"A_mbrb_{g}")
                    pass
                    for c in range(8 if (sub & 2) else 0):
                        v.tensor_sub(xTbuf[p][:, AG * c:AG * (c + 1)],
                                     xTbuf[p][:, AG * c:AG * (c + 1)], half(0, 0))
                        v.drain()
                        i2 = v.tensor_mul(xh[p][:, AG * c:AG * (c + 1)],
                                          xTbuf[p][:, AG * c:AG * (c + 1)], half(0, 1))
                        e.inc(i2, "sV", 1) if c == 7 else None
                    if not (sub & 2):
                        i2 = v.drain()
                        e.inc(i2, "sV", 1)
                    e.mark(f"A_xh_{g}", "sV")
                    for c in range(8 if alv >= 3 else 0):
                        e.wait(v, f"A_kt_{g}_{c}")
                        if g > 0 and c == 0:
                            e.wait(v, f"A_te_done_{g-1}")
                        i2 = v.tensor_mul(t1b[:, AG * c:AG * (c + 1)], ctbuf[p][:],
                                          kT[:, AG * c:AG * (c + 1)])
                        e.wait(v, f"A_jp_{g}_{c}")
                        i2 = v.tensor_mul(t2b[:, AG * c:AG * (c + 1)], stbuf[p][:],
                                          half(2, c % 2))
                        e.inc(i2, "sV", 1)
                        e.mark(f"A_t2_{g}_{c}", "sV")
                    e.mark(f"A_vfree_{g}", "sV")
                    if alv < 3:
                        pass
                    slot_cols = [0, 512, 128, 640]
                    scnt = 0
                    for h in range(NH if alv >= 7 else 0):
                        for m in range(2):
                            e.wait(v, f"A_sp_{g}_{scnt}")
                            cc = (g // (NAG // 2))
                            off = 65 * ((cc * 16 + h) * 2 + m)
                            i2 = v.tensor_add(s_acc[:, off:off + 65], s_acc[:, off:off + 65],
                                              K[3][:, slot_cols[scnt % 4]:slot_cols[scnt % 4] + 65])
                            e.inc(i2, "sV", 1)
                            e.mark(f"A_spadd_{g}_{scnt}", "sV")
                            scnt += 1
                    e.mark(f"A_v_done_{g}", "sV")
                e.mark("A_sacc_done", "sV")

            if "B" in phases:
                x1T = kphi[:].bitcast(F32)
                for g in range(NAG):
                    p = g % 2
                    e.wait(v, f"B_h_{g}")
                    e.wait(v, f"B_mbrb_{g}")
                    if g == 0:
                        e.wait(v, f"A_te_done_{NAG-1}")
                    for c in range(8):
                        v.tensor_sub(xh[p][:, AG * c:AG * (c + 1)],
                                     xTbuf[p][:, AG * c:AG * (c + 1)], half(0, 0))
                        v.drain()
                        i2 = v.tensor_mul(xh[p][:, AG * c:AG * (c + 1)],
                                          xh[p][:, AG * c:AG * (c + 1)], half(0, 1))
                        v.drain()
                        e.inc(i2, "sV", 1) if c == 7 else None
                    e.mark(f"B_xh_{g}", "sV")
                    for c in range(8):
                        e.wait(v, f"B_qt_{g}_{c}")
                        i2 = v.tensor_mul(t1b[:, AG * c:AG * (c + 1)], ctbuf[p][:],
                                          qT[:, AG * c:AG * (c + 1)] if False else kT[:, AG * c:AG * (c + 1)])
                        e.wait(v, f"B_jp_{g}_{c}")
                        i2 = v.tensor_mul(t2b[:, AG * c:AG * (c + 1)], stbuf[p][:],
                                          half(0, c % 2))
                        e.inc(i2, "sV", 1)
                        e.mark(f"B_t2_{g}_{c}", "sV")
                    e.mark(f"B_vfree_{g}", "sV")
                    e.wait(v, f"B_fsc_{g}")
                    for b in range(8):
                        e.wait(v, f"B_nmm_{g}_{b}")
                        npsum = K[2] if b % 2 == 0 else K[3]
                        d8 = npsum[:, 64:].rearrange("p (x) -> p x") if False else None
                        din2 = npsum[:].rearrange("p (hh tt o) -> p tt hh o", hh=2, tt=4)[:, :, :, 64:65]
                        t8a = tmp8a[:].rearrange("p (tt hh) -> p tt hh", tt=4)
                        t8b = tmp8b[:].rearrange("p (tt hh) -> p tt hh", tt=4)
                        fslice = fsc[:].rearrange("p (tt h) -> p tt h", tt=4)[:, :, 2 * b:2 * b + 2]
                        v.tensor_copy(t8a, din2.squeeze(3))
                        v.drain()
                        v.tensor_mul(t8b, t8a, fslice)
                        v.drain()
                        v.tensor_scalar_max(tmp8b[:], tmp8b[:], EPS_ATT)
                        v.drain()
                        v.reciprocal(tmp8b[:], tmp8b[:])
                        v.drain()
                        v.tensor_mul(t8a, t8b, fslice)
                        v.drain()
                        nin = npsum[:].rearrange("p (hh tt o) -> p tt hh o", hh=2, tt=4)[:, :, :, 0:64]
                        fbc = tmp8a[:].rearrange("p (tt hh) -> p tt hh", tt=4).unsqueeze(3).broadcast_to((128, 4, 2, 64))
                        oout = ohbuf[:, 0:4096].rearrange(
                            "p (tt hh2 e) -> p tt hh2 e", tt=4, e=64)[:, :, 2 * b:2 * b + 2, :]
                        i2 = v.tensor_mul(oout, nin, fbc)
                        v.drain()
                        e.inc(i2, "sV", 1)
                        e.mark(f"B_div_{g}_{b}", "sV")
                    e.wait(v, f"B_statmm_{g}")
                    v.tensor_scalar(stat4m[:], K[1][:, 0:512].rearrange("p (tt o) -> p tt o", o=128)[:, :, 0:1].squeeze(2),
                                    1.0 / D_MODEL, None, ALU.mult)
                    v.tensor_scalar(stat4s[:], K[1][:, 512:1024].rearrange("p (tt o) -> p tt o", o=128)[:, :, 0:1].squeeze(2),
                                    1.0 / D_MODEL, None, ALU.mult)
                    v.drain()
                    v.tensor_tensor(tmp4[:], stat4m[:], stat4m[:], ALU.mult)
                    v.drain()
                    v.tensor_sub(stat4s[:], stat4s[:], tmp4[:])
                    v.drain()
                    v.tensor_scalar(stat4s[:], stat4s[:], EPS_LN, None, ALU.add)
                    v.drain()
                    i2 = v.reciprocal(rv4[:], stat4s[:])
                    e.inc(i2, "sV", 1)
                    e.mark(f"B_stat_{g}", "sV")
                    e.wait(v, f"B_sqrt_{g}")
                    if g > 0:
                        e.wait(v, f"B_stT_{g-1}")
                    v.tensor_copy(mcol_bf[:, 0:4], stat4m[:])
                    i2 = v.tensor_copy(mcol_bf[:, 4:8], rstd4[:])
                    v.drain()
                    e.inc(i2, "sV", 1)
                    e.mark(f"B_statbf_{g}", "sV")
                    e.wait(v, f"B_mbrb2_{g}")
                    if g == 2 and "x1t" in dbg:
                        e.wait(v, "H_x1dump")
                    for c in range(8):
                        dst = x1T[:, 1024 * c + 512 * (g % 2):1024 * c + 512 * (g % 2) + 512]
                        v.tensor_sub(dst, sqb[:, 512 * c:512 * (c + 1)], half(3, 0))
                        v.drain()
                        v.tensor_mul(dst, dst, half(3, 1))
                        v.drain()
                        i2 = v.tensor_add(dst, dst, xTbuf[p][:, AG * c:AG * (c + 1)])
                        v.drain()
                        e.inc(i2, "sV", 1) if c == 7 else None
                    e.mark(f"B_v_done_{g}", "sV")

                    if "C" not in phases:
                        continue
                    x1off = lambda c: 1024 * c + 512 * (g % 2)
                    vf = vaug[:].bitcast(F32)
                    sqbf32 = sqb[:].bitcast(F32)
                    t2f = t2b[:].bitcast(F32)
                    for c in range(8):
                        i2 = v.tensor_copy(xh[g % 2][:, 512 * c:512 * (c + 1)],
                                           x1T[:, x1off(c):x1off(c) + 512])
                    v.drain()
                    e.inc(i2, "sV", 1)
                    e.mark(f"C_xb_{g}", "sV")
                    e.wait(v, f"C_statmm_{g}")
                    v.tensor_scalar(stat4m[:], K[0][:, 0:512].rearrange("p (tt o) -> p tt o", o=128)[:, :, 0:1].squeeze(2),
                                    1.0 / D_MODEL, None, ALU.mult)
                    v.tensor_scalar(stat4s[:], K[0][:, 512:1024].rearrange("p (tt o) -> p tt o", o=128)[:, :, 0:1].squeeze(2),
                                    1.0 / D_MODEL, None, ALU.mult)
                    v.drain()
                    v.tensor_tensor(tmp4[:], stat4m[:], stat4m[:], ALU.mult)
                    v.drain()
                    v.tensor_sub(stat4s[:], stat4s[:], tmp4[:])
                    v.drain()
                    v.tensor_scalar(stat4s[:], stat4s[:], EPS_LN, None, ALU.add)
                    v.drain()
                    i2 = v.reciprocal(rv4[:], stat4s[:])
                    e.inc(i2, "sV", 1)
                    e.mark(f"C_stat_{g}", "sV")
                    e.wait(v, f"C_sqrt_{g}")
                    v.tensor_copy(mcol_bf[:, 0:4], stat4m[:])
                    i2 = v.tensor_copy(mcol_bf[:, 4:8], rstd4[:])
                    v.drain()
                    e.inc(i2, "sV", 1)
                    e.mark(f"C_statbf_{g}", "sV")
                    e.wait(v, f"C_mbrb3_{g}")
                    for c in range(8):
                        v.tensor_sub(t1b[:, 512 * c:512 * (c + 1)],
                                     x1T[:, x1off(c):x1off(c) + 512], half(0, 0))
                        v.drain()
                        i2 = v.tensor_mul(t1b[:, 512 * c:512 * (c + 1)],
                                          t1b[:, 512 * c:512 * (c + 1)], half(0, 1))
                        v.drain()
                    e.inc(i2, "sV", 1)
                    e.mark(f"C_x2h_{g}", "sV")
                    e.wait(v, f"C_gate_{g}")
                    for tt in range(4):
                        sl = K[1][:, 128 * tt:128 * tt + 8]
                        v.reduce_max(tmp8a[:, 0:1], sl, AX.X)
                        v.drain()
                        v.tensor_scalar(mask12[:, 16 * tt:16 * tt + 8], sl, tmp8a[:, 0:1], 2.0,
                                        ALU.is_equal, ALU.mult)
                        v.drain()
                        v.scalar_tensor_tensor(tmp8b[:], mask12[:, 16 * tt:16 * tt + 8], -5e29,
                                               sl, ALU.mult, ALU.add)
                        v.drain()
                        v.reduce_max(tmp8a[:, 1:2], tmp8b[:], AX.X)
                        v.drain()
                        v.tensor_scalar(mask12[:, 16 * tt + 8:16 * (tt + 1)], tmp8b[:],
                                        tmp8a[:, 1:2], 2.0, ALU.is_equal, ALU.mult)
                        i2 = v.tensor_tensor(tmp8a[:, 4 + tt:5 + tt], tmp8a[:, 0:1],
                                             tmp8a[:, 1:2], ALU.subtract)
                        v.drain()
                    e.inc(i2, "sV", 1)
                    e.mark(f"C_top2m_{g}", "sV")
                    e.wait(v, f"S_sig_{g}")
                    v.tensor_copy(rwb[:].rearrange("p (t two) -> p t two", two=2)[:, :, 0:1].squeeze(2),
                                  fsc[:, 0:4])
                    v.drain()
                    i2 = v.tensor_scalar(rwb[:].rearrange("p (t two) -> p t two", two=2)[:, :, 1:2].squeeze(2),
                                         fsc[:, 0:4], -1.0, 1.0, ALU.mult, ALU.add)
                    v.drain()
                    e.inc(i2, "sV", 1)
                    e.mark(f"C_top2_{g}", "sV")
                    e.wait(v, f"C_mer_{g}")
                    v.tensor_copy(t2b[:, 0:512], half(3, 0))
                    i2 = v.tensor_copy(t2b[:, 512:1024], half(3, 1))
                    v.drain()
                    e.inc(i2, "sV", 1)
                    e.mark(f"C_merc_{g}", "sV")
                    e.wait(v, f"C_wb_{g}")
                    v.tensor_copy(t2f[:, 512:1024], half(3, 0))
                    i2 = v.tensor_copy(t2f[:, 1024:1536], half(3, 1))
                    v.drain()
                    e.inc(i2, "sV", 1)
                    e.mark(f"C_wbc_{g}", "sV")
                    if g > 0:
                        e.wait(v, f"C_y_{g-1}_1")
                    e.wait(v, f"C_u1_{g}")
                    v.tensor_mul(sqb[:, 0:512], half(3, 0), t2b[:, 0:512])
                    i2 = v.tensor_mul(sqb[:, 512:1024], half(3, 0), t2b[:, 512:1024])
                    v.drain()
                    e.inc(i2, "sV", 1)
                    e.mark(f"C_mu1_{g}", "sV")
                    e.wait(v, f"C_u3_{g}")
                    v.tensor_mul(sqb[:, 1024:1536], half(3, 1), t2b[:, 0:512])
                    i2 = v.tensor_mul(sqb[:, 1536:2048], half(3, 1), t2b[:, 512:1024])
                    v.drain()
                    e.inc(i2, "sV", 1)
                    e.mark(f"C_mu3_{g}", "sV")
                    for fc in range(NFC):
                        e.wait(v, f"C_dd_{g}_{fc}")
                        v.tensor_add(vf[:, 1024:1536], half(0, 1), sqbf32[:, 1024:1536])
                        i2 = v.tensor_add(vf[:, 1536:2048], half(1, 1), sqbf32[:, 1536:2048])
                        v.drain()
                        e.inc(i2, "sV", 1)
                        e.mark(f"C_add_{g}_{fc}", "sV")
                        e.wait(v, f"S_silu_{g}_{fc}")
                        v.tensor_mul(sqbf32[:, 1024:1536], vf[:, 0:512], half(1, 0))
                        v.drain()
                        v.tensor_mul(sqbf32[:, 1536:2048], vf[:, 512:1024], vf[:, 1536:2048])
                        v.drain()
                        i2 = v.tensor_mul(t2b[:, 3072:3584], sqbf32[:, 1024:1536], t2f[:, 512:1024])
                        v.drain()
                        e.inc(i2, "sV", 1)
                        e.mark(f"C_w1_{g}_{fc}", "sV")
                        v.tensor_mul(vf[:, 0:512], sqbf32[:, 1536:2048], t2f[:, 1024:1536])
                        v.drain()
                        i2 = v.tensor_add(mixT_ap(g, fc), t2b[:, 3072:3584], vf[:, 0:512])
                        v.drain()
                        e.inc(i2, "sV", 1)
                        e.mark(f"C_mix_{g}_{fc}", "sV")
                    e.wait(v, f"C_z_{g}_{NFC-1}")
                    v.tensor_copy(vf[:, 0:512], half(2, 1))
                    v.drain()
                    v.tensor_sub(vf[:, 0:512], vf[:, 0:512], half(2, 0))
                    v.drain()
                    v.tensor_mul(vf[:, 0:512], vf[:, 0:512], t2b[:, 512:1024])
                    v.drain()
                    v.tensor_mul(vf[:, 512:1024], half(2, 0), t2b[:, 0:512])
                    v.drain()
                    i2 = v.tensor_add(zhat, vf[:, 512:1024], vf[:, 0:512])
                    v.drain()
                    e.inc(i2, "sV", 1)
                    e.mark(f"C_zhat_{g}", "sV")
                    for ps in range(2):
                        e.wait(v, f"C_statmm4_{g}_{ps}")
                        v.tensor_scalar(stat4m[:, 0:2], K[0][:, 0:256].rearrange("p (q o) -> p q o", o=128)[:, :, 0:1].squeeze(2),
                                        1.0 / D_MODEL, None, ALU.mult)
                        v.tensor_scalar(stat4s[:, 0:2], K[0][:, 512:768].rearrange("p (q o) -> p q o", o=128)[:, :, 0:1].squeeze(2),
                                        1.0 / D_MODEL, None, ALU.mult)
                        v.drain()
                        v.tensor_tensor(tmp4[:, 0:2], stat4m[:, 0:2], stat4m[:, 0:2], ALU.mult)
                        v.drain()
                        v.tensor_sub(stat4s[:, 0:2], stat4s[:, 0:2], tmp4[:, 0:2])
                        v.drain()
                        v.tensor_scalar(stat4s[:, 0:2], stat4s[:, 0:2], EPS_LN, None, ALU.add)
                        v.drain()
                        i2 = v.reciprocal(rv4[:, 0:2], stat4s[:, 0:2])
                        e.inc(i2, "sV", 1)
                        e.mark(f"C_stat4_{g}_{ps}", "sV")
                        e.wait(v, f"C_sqrt4_{g}_{ps}")
                        v.tensor_copy(mcol_bf[:, 0:2], stat4m[:, 0:2])
                        i2 = v.tensor_copy(mcol_bf[:, 4:6], rstd4[:, 0:2])
                        v.drain()
                        e.inc(i2, "sV", 1)
                        e.mark(f"C_statbf4_{g}_{ps}", "sV")
                        e.wait(v, f"C_mbrb4_{g}_{ps}")
                        if ps == 1:
                            e.wait(v, f"C_y_{g}_0")
                        for dc in range(8):
                            fT = xh[g % 2][:, 0:2048].rearrange("p (q e o) -> p q e o", q=2, o=128)[:, :, dc, :]
                            mb4 = K[1][:, 0:256].rearrange("p (q o) -> p q o", o=128)
                            rb4 = K[1][:, 512:768].rearrange("p (q o) -> p q o", o=128)
                            xt = vf[:, 2048 - 256:2048].rearrange("p (q o) -> p q o", o=128)
                            v.tensor_sub(xt, fT, mb4)
                            v.drain()
                            v.tensor_mul(xt, xt, rb4)
                            v.drain()
                            x1in = x1T[:, x1off(dc) + 256 * ps:x1off(dc) + 256 * ps + 256]
                            i2 = v.tensor_add(sqbf32[:, 256 * dc:256 * (dc + 1)],
                                              vf[:, 2048 - 256:2048], x1in)
                            v.drain()
                        e.inc(i2, "sV", 1)
                        e.mark(f"C_x2_{g}_{ps}", "sV")
                    e.mark(f"C_done_{g}", "sV")

        d = DryEng()
        for f in (prog_sync, prog_gpsimd, prog_tensor, prog_scalar, prog_vector):
            f(d)

        sems = {n: es.enter_context(nc.semaphore(n)) for n in SEM_NAMES}
        env.begin_real(sems)
        with nc.Block() as block:
            @block.sync
            def _(s):
                prog_sync(s)

            @block.gpsimd
            def _(g_):
                prog_gpsimd(g_)

            @block.tensor
            def _(t):
                prog_tensor(t)

            @block.scalar
            def _(sc):
                prog_scalar(sc)

            @block.vector
            def _(v):
                prog_vector(v)

        env.check_passes_match()
        nc._dbg_env = env

    return nc


def assemble_y(y):
    yb = y.reshape(128, 8, 2, 8, 256)
    xcore = yb.transpose(1, 2, 4, 3, 0).reshape(TCORE, D_MODEL)
    return xcore


def kernel(**inputs):
    in_maps = make_in_maps(inputs)
    nc = build_program()
    res = run_bass_kernel_spmd(nc, in_maps, core_ids=list(range(NCORES)))
    out = np.zeros((C_SEQ, L_SEQ, D_MODEL), dtype=np.float32)
    for i in range(NCORES):
        xcore = assemble_y(res.results[i]["y"])
        out[0, i * SEQ_SLICE:(i + 1) * SEQ_SLICE] = xcore[0:SEQ_SLICE]
        out[1, i * SEQ_SLICE:(i + 1) * SEQ_SLICE] = xcore[SEQ_SLICE:TCORE]
    return out


# version 05
# speedup vs baseline: 1.0150x; 1.0150x over previous
import math
import numpy as np
import ml_dtypes
from contextlib import ExitStack

import concourse.bass as bass
import concourse.mybir as mybir
from concourse.bass_utils import run_bass_kernel_spmd

F32 = mybir.dt.float32
BF16 = mybir.dt.bfloat16
AF = mybir.ActivationFunctionType
ALU = mybir.AluOpType
AX = mybir.AxisListType

NCORES = 8
C_SEQ = 2
L_SEQ = 16384
D_MODEL = 1024
NH = 16
HD = 64
NFEAT = 256
NR = 4
TCORE = 4096
SEQ_SLICE = 2048
AG = 512
NAG = TCORE // AG
BG = 1024
NBG = TCORE // BG
FFD = 4096
NFC = 32
EPS_ATT = 1e-6
EPS_LN = 1e-5
LN16 = math.log(16.0)
INV3 = 1.0 / 512.0

bfarr = lambda a: np.ascontiguousarray(np.asarray(a), dtype=ml_dtypes.bfloat16)
f32arr = lambda a: np.ascontiguousarray(np.asarray(a), dtype=np.float32)



def _hadamard64():
    h = np.array([[1.0]])
    while h.shape[0] < 64:
        h = np.block([[h, h], [h, -h]])
    return h


def _origidx():
    o = np.zeros(64, dtype=np.int64)
    o[:32] = 2 * np.arange(32)
    o[32:] = 2 * np.arange(32) + 1
    return o


def _rope_tables(core_id):
    t = np.arange(TCORE)
    l = core_id * SEQ_SLICE + (t % SEQ_SLICE)
    pos = np.maximum(l - 1, 0).astype(np.float64)
    j = np.arange(32)
    freq = 10000.0 ** (-2.0 * j / 64.0)
    ang = pos[None, :] * freq[:, None]
    cos, sin = np.cos(ang), np.sin(ang)
    ctab = np.zeros((128, TCORE))
    stab = np.zeros((128, TCORE))
    for hf in range(2):
        b = 64 * hf
        ctab[b:b + 32] = cos
        ctab[b + 32:b + 64] = cos
        stab[b:b + 32] = -sin
        stab[b + 32:b + 64] = sin
    return bfarr(ctab), bfarr(stab)


def _jmat():
    j64 = np.zeros((64, 64))
    j64[np.arange(32), 32 + np.arange(32)] = 1.0
    j64[32 + np.arange(32), np.arange(32)] = 1.0
    out = np.zeros((128, 128))
    out[0:64, 0:64] = j64
    out[64:128, 64:128] = j64
    return bfarr(out)


def _selectors():
    sel = np.zeros((128, 8, 16))
    for c in range(8):
        sel[0:64, c, 2 * c] = 1.0
        sel[64:128, c, 2 * c + 1] = 1.0
    return bfarr(sel.reshape(128, 128))


def _expand8():
    e = np.zeros((8, 128))
    for i in range(8):
        e[i, 16 * i:16 * i + 16] = 1.0
    return bfarr(e)


def make_in_maps(inp):
    x = np.asarray(inp["L0_attn_x"])
    w_qkv = np.asarray(inp["L0_attn_w_qkv"])
    w_out = np.asarray(inp["L0_attn_w_out"])
    signs = np.asarray(inp["L0_attn_sorf_signs"])
    scales = np.asarray(inp["L0_attn_sorf_scales"])
    gate_w = np.asarray(inp["L0_ffn_gate_w"])
    w_gate = np.asarray(inp["L0_ffn_w_gate"])
    w_up = np.asarray(inp["L0_ffn_w_up"])
    w_down = np.asarray(inp["L0_ffn_w_down"])
    A1 = np.asarray(inp["L0_ffn_A1"])
    B1 = np.asarray(inp["L0_ffn_B1"])
    A3 = np.asarray(inp["L0_ffn_A3"])
    B3 = np.asarray(inp["L0_ffn_B3"])
    A2 = np.asarray(inp["L0_ffn_A2"])
    B2 = np.asarray(inp["L0_ffn_B2"])

    orig = _origidx()
    perm = np.concatenate([64 * h + orig for h in range(NH)])
    w_q = w_qkv[0:1024][perm]
    w_k = w_qkv[1024:2048][perm]
    w_v = w_qkv[2048:3072]

    proj_layout = lambda w: f32arr(w.T.reshape(8, 128, w.shape[0]).transpose(1, 0, 2).reshape(128, 8 * w.shape[0]))
    wqT = proj_layout(w_q)
    wkT = proj_layout(w_k)
    wvT = proj_layout(w_v)
    woT = f32arr(w_out.T.reshape(8, 128, 1024).transpose(1, 0, 2).reshape(128, 8192))

    ii = np.arange(64)
    primeof = np.zeros(64, dtype=np.int64)
    primeof[orig] = np.arange(64)
    diag_all = np.zeros((64, NH * NR * 4 * 64), dtype=np.float32)
    for h in range(NH):
        for r in range(NR):
            base = ((h * NR) + r) * 256
            dgm = np.zeros((64, 64), dtype=np.float32)
            dgm[ii, ii] = scales[h, r]
            d2m = np.zeros((64, 64), dtype=np.float32)
            d2m[ii, ii] = signs[h, r, 1]
            d3m = np.zeros((64, 64), dtype=np.float32)
            d3m[ii, ii] = signs[h, r, 2]
            d1ptm = np.zeros((64, 64), dtype=np.float32)
            d1ptm[ii, primeof] = signs[h, r, 0]
            diag_all[:, base:base + 64] = dgm
            diag_all[:, base + 64:base + 128] = d2m
            diag_all[:, base + 128:base + 192] = d3m
            diag_all[:, base + 192:base + 256] = d1ptm

    gate_wT = f32arr(gate_w.T.reshape(8, 128, 8).transpose(1, 0, 2).reshape(128, 64))
    a1catT = f32arr(A1.transpose(2, 0, 1).reshape(8, 128, 128).transpose(1, 0, 2).reshape(128, 1024))
    a3catT = f32arr(A3.transpose(2, 0, 1).reshape(8, 128, 128).transpose(1, 0, 2).reshape(128, 1024))
    b2cat = f32arr(B2.transpose(0, 2, 1).reshape(128, 1024))

    wg_t = w_gate.T.reshape(8, 128, 32, 128)
    wu_t = w_up.T.reshape(8, 128, 32, 128)
    b1l = B1.transpose(0, 2, 1).reshape(128, 32, 128)
    b3l = B3.transpose(0, 2, 1).reshape(128, 32, 128)
    a2l = A2.transpose(2, 0, 1).reshape(32, 128, 128)
    wd_r = w_down.T.reshape(32, 128, 1024)
    stream = np.zeros((32, 128, 3456), dtype=np.float32)
    for fc in range(32):
        stream[fc, :, 0:1024] = wg_t[:, :, fc, :].transpose(1, 0, 2).reshape(128, 1024)
        stream[fc, :, 1024:2048] = wu_t[:, :, fc, :].transpose(1, 0, 2).reshape(128, 1024)
        stream[fc, :, 2048:2176] = b1l[:, fc, :]
        stream[fc, :, 2176:2304] = b3l[:, fc, :]
        stream[fc, :, 2304:2432] = a2l[fc]
        stream[fc, :, 2432:3456] = wd_r[fc]
    stream = f32arr(stream)

    shared = dict(
        wqT=wqT, wkT=wkT, wvT=wvT, woT=woT, sorf_diag_all=diag_all,
        gate_wT=gate_wT, a1catT=a1catT, a3catT=a3catT, b2cat=b2cat,
        ffn_stream=stream,
        hmat=bfarr(_hadamard64()), jmat=_jmat(), sel=_selectors(),
        exp8=_expand8(), ident=bfarr(np.eye(128)),
        onescol=bfarr(np.ones((1, 128))), onescol2=bfarr(np.ones((128, 1))),
    )
    in_maps = []
    for i in range(NCORES):
        xs = np.concatenate([x[0, i * SEQ_SLICE:(i + 1) * SEQ_SLICE],
                             x[1, i * SEQ_SLICE:(i + 1) * SEQ_SLICE]], axis=0)
        ctab, stab = _rope_tables(i)
        m = dict(shared)
        xgT = xs.T.reshape(8, 128, NAG, AG).transpose(1, 2, 0, 3).reshape(128, NAG, 8 * AG)
        m.update(xg=f32arr(xs), xgT=f32arr(xgT), ctab=ctab, stab=stab)
        in_maps.append(m)
    return in_maps



SEM_NAMES = ["sHc", "sHa0", "sHa1", "sHm", "sHf0", "sHf1", "sGc", "sGw", "sGs", "sT", "sV", "sS", "sC", "sO"]


class DummyInstr:
    def then_inc(self, sem, n):
        return self


class DryEng:
    def __getattr__(self, name):
        def f(*a, **k):
            return DummyInstr()
        return f


class Env:
    def __init__(self):
        self.reset()
        self.marks = {}
        self.dry = True
        self.sems = None

    def reset(self):
        self.vals = {s: 0 for s in SEM_NAMES}
        self.log = []

    def begin_real(self, sems):
        self.dry_log = self.log
        self.reset()
        self.dry = False
        self.sems = sems

    def inc(self, instr, sem, n=1):
        self.vals[sem] += n
        self.log.append((sem, n))
        if not self.dry:
            instr.then_inc(self.sems[sem], n)

    def check_passes_match(self):
        assert self.dry_log == self.log, (
            f"dry/real inc divergence at index "
            f"{next(i for i, (a, b) in enumerate(zip(self.dry_log, self.log)) if a != b) if self.dry_log != self.log and len(self.dry_log) == len(self.log) else (len(self.log), len(self.dry_log))}")

    def mark(self, name, sem):
        if self.dry:
            assert name not in self.marks, f"dup mark {name}"
            self.marks[name] = (sem, self.vals[sem])

    def wait(self, eng, name):
        if not self.dry:
            sem, v = self.marks[name]
            eng.wait_ge(self.sems[sem], v)



def build_program(debug=(), phases=("P0", "A", "AR", "B", "C"), alv=7, stub_y=True, nag=NAG, sub=3):
    nc = bass.Bass("TRN2", target_bir_lowering=False)

    xg_d = nc.dram_tensor("xg", [TCORE, D_MODEL], F32, kind="ExternalInput")
    xgT_d = nc.dram_tensor("xgT", [128, NAG, 8 * AG], F32, kind="ExternalInput")
    wqT_d = nc.dram_tensor("wqT", [128, 8192], F32, kind="ExternalInput")
    wkT_d = nc.dram_tensor("wkT", [128, 8192], F32, kind="ExternalInput")
    wvT_d = nc.dram_tensor("wvT", [128, 8192], F32, kind="ExternalInput")
    woT_d = nc.dram_tensor("woT", [128, 8192], F32, kind="ExternalInput")
    diag_d = nc.dram_tensor("sorf_diag_all", [64, NH * NR * 4 * 64], F32, kind="ExternalInput")
    gate_wT_d = nc.dram_tensor("gate_wT", [128, 64], F32, kind="ExternalInput")
    a1catT_d = nc.dram_tensor("a1catT", [128, 1024], F32, kind="ExternalInput")
    a3catT_d = nc.dram_tensor("a3catT", [128, 1024], F32, kind="ExternalInput")
    b2cat_d = nc.dram_tensor("b2cat", [128, 1024], F32, kind="ExternalInput")
    stream_d = nc.dram_tensor("ffn_stream", [NFC, 128, 3456], F32, kind="ExternalInput")
    hmat_d = nc.dram_tensor("hmat", [64, 64], BF16, kind="ExternalInput")
    jmat_d = nc.dram_tensor("jmat", [128, 128], BF16, kind="ExternalInput")
    sel_d = nc.dram_tensor("sel", [128, 128], BF16, kind="ExternalInput")
    exp8_d = nc.dram_tensor("exp8", [8, 128], BF16, kind="ExternalInput")
    ident_d = nc.dram_tensor("ident", [128, 128], BF16, kind="ExternalInput")
    ones_d = nc.dram_tensor("onescol", [1, 128], BF16, kind="ExternalInput")
    ones2_d = nc.dram_tensor("onescol2", [128, 1], BF16, kind="ExternalInput")
    ctab_d = nc.dram_tensor("ctab", [128, TCORE], BF16, kind="ExternalInput")
    stab_d = nc.dram_tensor("stab", [128, TCORE], BF16, kind="ExternalInput")
    y_d = nc.dram_tensor("y", [128, 128, 256], F32, kind="ExternalOutput")
    streambf_d = nc.dram_tensor("streambf", [NFC, 128, 3456], BF16)
    cc_in_d = nc.dram_tensor("cc_in", [128, 4160], F32)
    cc_out_d = nc.dram_tensor("cc_out", [128, 4160], F32)

    dbg = {}
    def dbg_t(name, shape, dtype):
        if name in debug:
            dbg[name] = nc.dram_tensor("dbg_" + name, shape, dtype, kind="ExternalOutput")
    dbg_t("fpt", [128, 2048], BF16)
    dbg_t("sacc", [128, 4160], F32)
    dbg_t("kphi", [128, NH * 4 * 256], BF16)
    dbg_t("x1t", [128, 8192], F32)
    dbg_t("attnt", [128, 4096], BF16)
    dbg_t("fsc", [128, 64], F32)
    dbg_t("ohb", [128, 4096], BF16)
    dbg_t("m2r", [1, 512], BF16)
    dbg_t("r2r", [1, 512], BF16)
    dbg_t("t2b", [128, 4096], BF16)

    es = ExitStack()
    with es:
        SB = lambda n, shp, dt=F32: es.enter_context(nc.sbuf_tensor(n, shp, dt))
        PSUM = lambda n, shp, dt=F32: es.enter_context(nc.psum_tensor(n, shp, dt))

        fpt = SB("fptS", [128, 2048], BF16)
        s_acc = SB("s_acc", [128, 4160], F32)
        s_bf = SB("s_bf", [128, 4160], BF16)
        mrow = SB("mrow", [1, TCORE], BF16)
        rrow = SB("rrow", [1, TCORE], BF16)
        ident = SB("identS", [128, 128], BF16)
        onescol = SB("onescolS", [1, 128], BF16)
        onescol2 = SB("onescol2S", [128, 1], BF16)
        jmat = SB("jmatS", [128, 128], BF16)
        sel = SB("selS", [128, 8 * 16], BF16)

        sorf_tmp = None

        xTbuf = [SB(f"xTbuf{p}", [128, 8 * AG], F32) for p in range(2)]
        ctbuf = [SB(f"ctbuf{p}", [128, AG], BF16) for p in range(2)]
        stbuf = [SB(f"stbuf{p}", [128, AG], BF16) for p in range(2)]
        xh = [SB(f"xh{p}", [128, 8 * AG], BF16) for p in range(2)]
        wkT = SB("wkTs", [128, 8 * 1024], BF16)
        wvT = SB("wvTs", [128, 8 * 1024], BF16)
        kT = SB("kTs", [128, 8 * AG], BF16)
        t1b = SB("t1b", [128, 8 * AG], BF16)
        t2b = SB("t2b", [128, 8 * AG], BF16)
        sqb = SB("sqb", [128, 8 * AG], BF16)
        kphi = SB("kphiS", [128, NH * 4 * 256], BF16)
        diag_all = kphi
        vaug = SB("vaug", [128, 4 * 1040], BF16)
        stat4m = SB("stat4m", [128, 4], F32)
        stat4s = SB("stat4s", [128, 4], F32)
        tmp4 = SB("tmp4", [128, 4], F32)
        rv4 = SB("rv4", [128, 4], F32)
        rstd4 = SB("rstd4", [128, 4], F32)
        mcol_bf = SB("mcol_bf", [128, 8], BF16)
        cvec = SB("cvec", [128, 4], F32)
        fsc = SB("fsc", [128, 64], F32)
        nsb = fsc
        tmp8a = SB("tmp8a", [128, 8], F32)
        a1catTs = SB("a1catTs", [128, 1024], BF16)
        a3catTs = SB("a3catTs", [128, 1024], BF16)

        tmp8b = SB("tmp8b", [128, 8], F32)


        K = [PSUM(f"bankpair{i}", [128, 1024]) for i in range(4)]
        half = lambda i, hh: K[i][:, 512 * hh:512 * (hh + 1)]
        fstage = kT
        sorf_tmp = [kT[0:64, 2048:2112], kT[0:64, 2112:2176]]
        hmat = kT[0:64, 2176:2240]
        b2cats = None
        ohbuf = vaug
        sacc_bf = lambda: s_acc[:].bitcast(BF16)
        oTbuf_ap = lambda: s_acc[:].bitcast(BF16)[:, 0:4096]
        qphib_ap = lambda: s_acc[:].bitcast(BF16)[:, 4096:6144]
        b2cats = s_acc[:].bitcast(BF16)[:, 0:1024]
        gatewT = s_acc[:].bitcast(BF16)[:, 6912:6976]
        exp8s = s_acc[:].bitcast(BF16)[0:8, 6976:7104]
        mask12 = s_acc[:].bitcast(BF16)[:, 7104:7168]
        rwb = s_acc[:].bitcast(BF16)[:, 7168:7176]
        m2row = s_acc[:].bitcast(BF16)[0:1, 7176:7688]
        r2row = s_acc[:].bitcast(BF16)[0:1, 7688:8200]
        zhat = s_acc[:].bitcast(BF16)[:, 4096:4608]
        def mixT_ap(g, fc):
            if fc < 16:
                return xTbuf[g % 2][:].bitcast(BF16)[:, 512 * fc:512 * (fc + 1)]
            if fc < 24:
                return kT[:, 512 * (fc - 16):512 * (fc - 15)]
            return xh[1 - g % 2][:, 512 * (fc - 24):512 * (fc - 23)]

        env = Env()

        def prog_sync(s):
            e = env
            e.inc(s.dma_start(out=ident[:], in_=ident_d.ap()), "sHc", 16)
            e.inc(s.dma_start(out=onescol[:], in_=ones_d.ap()), "sHc", 16)
            e.inc(s.dma_start(out=onescol2[:], in_=ones2_d.ap()), "sHc", 16)
            e.inc(s.dma_start(out=jmat[:], in_=jmat_d.ap()), "sHc", 16)
            e.inc(s.dma_start(out=hmat, in_=hmat_d.ap()), "sHc", 16)
            e.inc(s.dma_start(out=sel[:], in_=sel_d.ap()), "sHc", 16)
            e.mark("H_consts", "sHc")
            if "P0" in phases:
                e.wait(s, "S_fpt_evac")
                e.inc(s.dma_start(out=fpt[64:128, :], in_=fstage[0:64, 0:2048]), "sHm", 16)
                e.mark("H_fpt", "sHm")
            if "A" in phases:
                for g in range(nag):
                    p = g % 2
                    if g >= 2:
                        e.wait(s, f"A_vfree_{g-2}")
                    sp = f"sHa{g % 2}"
                    e.inc(s.dma_start(out=xTbuf[p][:], in_=xgT_d.ap()[:, g]), sp, 16)
                    e.inc(s.dma_start(out=ctbuf[p][:], in_=ctab_d.ap()[:, AG * g:AG * (g + 1)]), sp, 16)
                    e.inc(s.dma_start(out=stbuf[p][:], in_=stab_d.ap()[:, AG * g:AG * (g + 1)]), sp, 16)
                    e.mark(f"A_h_{g}", sp)
            if "AR" in phases:
                e.wait(s, "A_sacc_done")
                e.inc(s.dma_start(out=cc_in_d.ap(), in_=s_acc[:]), "sHm", 16)
                e.mark("H_ccin", "sHm")
            if "B" in phases:
                def bload(g):
                    p = g % 2
                    if g >= 2:
                        e.wait(s, f"C_done_{g-2}" if "C" in phases else f"B_v_done_{g-2}")
                    else:
                        e.wait(s, f"A_vfree_{NAG-2+g}")
                    if g == 3 and "x1t" in dbg:
                        e.inc(s.dma_start(out=dbg["x1t"].ap(), in_=kphi[:].bitcast(F32)), "sO", 16)
                        e.mark("H_x1dump", "sO")
                    sp = f"sHa{p}"
                    e.inc(s.dma_start(out=xTbuf[p][:], in_=xgT_d.ap()[:, g]), sp, 16)
                    e.inc(s.dma_start(out=ctbuf[p][:], in_=ctab_d.ap()[:, AG * g:AG * (g + 1)]), sp, 16)
                    e.inc(s.dma_start(out=stbuf[p][:], in_=stab_d.ap()[:, AG * g:AG * (g + 1)]), sp, 16)
                    e.mark(f"B_h_{g}", sp)
                if "C" not in phases:
                    for g in range(NAG):
                        bload(g)
            if "sacc" in dbg:
                if "AR" in phases:
                    e.wait(s, "G_sbf")
                    e.inc(s.dma_start(out=dbg["sacc"].ap(), in_=cc_out_d.ap()), "sO", 16)
                else:
                    e.wait(s, "A_sacc_done")
                    e.inc(s.dma_start(out=dbg["sacc"].ap(), in_=s_acc[:]), "sO", 16)
            if "fpt" in dbg:
                e.wait(s, "H_fpt")
                e.inc(s.dma_start(out=dbg["fpt"].ap(), in_=fpt[:]), "sO", 16)
            if "kphi" in dbg:
                e.wait(s, "A_sacc_done")
                e.inc(s.dma_start(out=dbg["kphi"].ap(), in_=kphi[:]), "sO", 16)
            if "C" in phases:
                wdbuf = lambda q: s_acc[:].bitcast(BF16)[:, 3456 * q + 2432:3456 * q + 3456]
                bload(0)
                bload(1)
                for g in range(NAG):
                    for ps in range(2):
                        for fc in range(NFC):
                            q = fc % 2
                            sem = "sHf0" if q == 0 else "sHf1"
                            if ps == 0 and fc < 2:
                                e.wait(s, f"C_c1done_{g}")
                            elif ps == 1 and fc < 2:
                                e.wait(s, f"C_dpsum_{g}_0")
                            else:
                                e.wait(s, f"C_wfree_{g}_{ps}_{fc-2}")
                            e.inc(s.dma_start(out=wdbuf(q),
                                              in_=streambf_d.ap()[fc][:, 2432:3456]), sem, 16)
                            e.mark(f"C_wd_{g}_{ps}_{fc}", sem)
                        e.wait(s, f"C_x2_{g}_{ps}")
                        bi = g * 16 + ps * 8
                        e.inc(s.dma_start(
                            out=y_d.ap().rearrange("a b c -> a (b c)")[:, 256 * bi:256 * (bi + 8)],
                            in_=sqb[:].bitcast(F32)[:, 0:2048]), "sO", 16)
                        e.mark(f"C_y_{g}_{ps}", "sO")
                    if g + 2 < NAG:
                        bload(g + 2)
            if "C" not in phases and stub_y:
                e.inc(s.dma_start(out=y_d.ap().rearrange("a b c -> (a b c)"),
                                  in_=xg_d.ap().rearrange("a b -> (a b)")), "sO", 16)
            e.mark("final_out", "sO")
            e.wait(s, "final_out")

        def prog_gpsimd(g_):
            e = env
            if "P0" in phases:
                e.inc(g_.dma_start(out=diag_all[0:64, :], in_=diag_d.ap()), "sGc", 16)
                e.mark("G_diag", "sGc")
            if "A" in phases:
                if alv >= 2:
                    e.inc(g_.dma_start(out=wkT[:], in_=wkT_d.ap()), "sGw", 16)
                    e.inc(g_.dma_start(out=wvT[:], in_=wvT_d.ap()), "sGw", 16)
                e.mark("G_wkv", "sGw")
                if alv >= 6:
                    i2 = g_.memset(vaug[:].rearrange("p (t h c) -> p t h c", t=4, c=65)[:, :, :, 64:65], 1.0)
                    e.inc(i2, "sC", 1)
                else:
                    e.inc(g_.drain(), "sC", 1)
                e.mark("G_vaug_ones", "sC")
            if "C" in phases:
                for fc in range(NFC):
                    e.inc(g_.dma_start(out=streambf_d.ap()[fc], in_=stream_d.ap()[fc]), "sGs", 16)
                e.mark("G_cache", "sGs")
            if "B" in phases:
                e.wait(g_, f"A_te_done_{NAG-1}")
                e.inc(g_.dma_start(out=wkT[:], in_=wqT_d.ap()), "sGw", 16)
                e.inc(g_.dma_start(out=wvT[:], in_=woT_d.ap()), "sGw", 16)
                e.mark("G_wq", "sGw")
            if "AR" in phases:
                e.wait(g_, "H_ccin")
                cc = g_.collective_compute(
                    "AllReduce", ALU.add,
                    replica_groups=[list(range(NCORES))],
                    ins=[cc_in_d.ap().opt()],
                    outs=[cc_out_d.ap().opt()],
                )
                e.inc(cc, "sC", 1)
                e.mark("C_ccdone", "sC")
                e.wait(g_, "C_ccdone")
                e.inc(g_.dma_start(out=s_bf[:], in_=cc_out_d.ap()), "sGs", 16)
                e.mark("G_sbf", "sGs")
            if "AR" not in phases and "B" in phases:
                e.wait(g_, "A_sacc_done")
                e.inc(g_.dma_start(out=s_bf[:], in_=s_acc[:]), "sGs", 16)
                e.mark("G_sbf", "sGs")
            if "C" in phases:
                if "AR" in phases:
                    e.wait(g_, "H_ccin")
                else:
                    e.wait(g_, "G_sbf")
                e.inc(g_.dma_start(out=gatewT, in_=gate_wT_d.ap()), "sGc", 16)
                e.inc(g_.dma_start(out=a1catTs[:], in_=a1catT_d.ap()), "sGc", 16)
                e.inc(g_.dma_start(out=a3catTs[:], in_=a3catT_d.ap()), "sGc", 16)
                e.inc(g_.dma_start(out=exp8s, in_=exp8_d.ap()), "sGc", 16)
                e.mark("G_cconsts", "sGc")
            if "C" in phases:
                bundle = lambda q: s_acc[:].bitcast(BF16)[:, 3456 * q:3456 * (q + 1)]
                for g in range(NAG):
                    for fc in range(NFC):
                        q = fc % 2
                        sem = "sGc" if q == 0 else "sGw"
                        if fc < 2:
                            e.wait(g_, f"B_te_done_{g}")
                            if g == 0:
                                e.wait(g_, "G_cconsts")
                            else:
                                e.wait(g_, f"C_bfree_{g-1}_{NFC-2+fc}")
                        else:
                            e.wait(g_, f"C_bfree_{g}_{fc-2}")
                        e.inc(g_.dma_start(out=bundle(q), in_=streambf_d.ap()[fc]), sem, 16)
                        e.mark(f"C_bun_{g}_{fc}", sem)
                    e.wait(g_, f"C_c1done_{g}")
                    e.inc(g_.dma_start(out=b2cats, in_=b2cat_d.ap()), "sGc", 16)
                    e.mark(f"C_b2_{g}", "sGc")
                e.mark("G_cdone", "sGc")

        def prog_tensor(t):
            e = env
            if "P0" in phases:
                e.wait(t, "H_consts")
                e.wait(t, "G_diag")
                pf = [K[3][0:64, 0:64], K[3][0:64, 512:576]]
                for h in range(NH):
                    for r in range(NR):
                        base = ((h * NR) + r) * 256
                        dgm = diag_all[0:64, base:base + 64]
                        d2m = diag_all[0:64, base + 64:base + 128]
                        d3m = diag_all[0:64, base + 128:base + 192]
                        d1ptm = diag_all[0:64, base + 192:base + 256]
                        lhs_seq = [hmat, d3m, hmat, d2m, hmat, d1ptm]
                        for si in range(6):
                            rhs = dgm if si == 0 else sorf_tmp[(si + 1) % 2]
                            if si >= 1:
                                e.wait(t, f"S_sorf_{h}_{r}_{si-1}_done")
                            elif not (h == 0 and r == 0):
                                ph, pr = (h, r - 1) if r > 0 else (h - 1, NR - 1)
                                e.wait(t, f"S_sorf_{ph}_{pr}_5_done")
                            mm = t.matmul(pf[si % 2], lhs_seq[si], rhs, start=True, stop=True)
                            e.inc(mm, "sT", 1)
                            e.mark(f"T_sorf_{h}_{r}_{si}", "sT")
                e.mark("T_fpt_mm", "sT")

            if "A" in phases:
                if alv >= 2:
                    e.wait(t, "G_wkv")
                if "P0" in phases:
                    e.wait(t, "H_fpt")
                for g in range(nag):
                    p = g % 2
                    e.wait(t, f"A_h_{g}")
                    if g > 0:
                        if alv >= 5:
                            e.wait(t, f"A_sexp_done_{g-1}")
                        elif alv >= 3:
                            e.wait(t, f"A_t2_{g-1}_7")
                        if alv >= 4:
                            e.wait(t, f"A_nsb_{g-1}")
                        else:
                            e.wait(t, f"A_xh_{g-1}")
                    e.wait(t, f"A_xc_{g}")
                    e.wait(t, f"A_sqx_{g}")
                    for tt in range(4):
                        for d in range(8):
                            mm = t.matmul(K[0][:, 128 * tt:128 * tt + 1],
                                          xh[p][:, AG * d + 128 * tt:AG * d + 128 * (tt + 1)],
                                          onescol2[:],
                                          start=(d == 0), stop=(d == 7))
                        for d in range(8):
                            mm = t.matmul(K[0][:, 512 + 128 * tt:512 + 128 * tt + 1],
                                          sqb[:, AG * d + 128 * tt:AG * d + 128 * (tt + 1)],
                                          onescol2[:],
                                          start=(d == 0), stop=(d == 7))
                    e.inc(mm, "sT", 1)
                    e.mark(f"A_statmm_{g}", "sT")
                    e.wait(t, f"A_statbf_{g}")
                    for j in range(8):
                        dst = K[2][0:1, 128 * j:128 * j + 128] if j < 4 else \
                              K[2][0:1, 512 + 128 * (j - 4):512 + 128 * (j - 4) + 128]
                        mm = t.matmul(dst, mcol_bf[:, j:j + 1], ident[:],
                                      start=True, stop=True)
                    e.inc(mm, "sT", 1)
                    e.mark(f"A_stT_{g}", "sT")
                    e.wait(t, f"A_strow_{g}")
                    mm = t.matmul(half(0, 0), onescol[:], mrow[0:1, AG * g:AG * (g + 1)],
                                  start=True, stop=True)
                    mm = t.matmul(half(0, 1), onescol[:], rrow[0:1, AG * g:AG * (g + 1)],
                                  start=True, stop=True)
                    e.inc(mm, "sT", 1)
                    e.mark(f"A_mbrb_{g}", "sT")
                    e.wait(t, f"A_xh_{g}")
                    for c in range(8 if alv >= 2 else 0):
                        if c >= 2:
                            e.wait(t, f"A_kt_{g}_{c-2}")
                        elif g > 0 and c < 2:
                            e.wait(t, f"A_kt_{g-1}_{6+c}")
                        dst = half(1, c % 2)
                        for d in range(8):
                            mm = t.matmul(dst, wkT[:, 1024 * d + 128 * c:1024 * d + 128 * (c + 1)],
                                          xh[p][:, AG * d:AG * (d + 1)],
                                          start=(d == 0), stop=(d == 7))
                        e.inc(mm, "sT", 1)
                        e.mark(f"A_kp_{g}_{c}", "sT")
                    for c in range(8 if alv >= 3 else 0):
                        if c == 0:
                            e.wait(t, f"A_strow_{g}")
                        e.wait(t, f"A_kt_{g}_{c}")
                        if c >= 2:
                            e.wait(t, f"A_t2_{g}_{c-2}")
                        mm = t.matmul(half(2, c % 2), jmat[:], kT[:, AG * c:AG * (c + 1)],
                                      start=True, stop=True)
                        e.inc(mm, "sT", 1)
                        e.mark(f"A_jp_{g}_{c}", "sT")
                    e.wait(t, "G_vaug_ones") if g == 0 else None
                    for tt in range(4 if alv >= 6 else 0):
                        if tt >= 1:
                            e.wait(t, f"A_vaug_{g}_{tt-1}")
                        elif g > 0:
                            e.wait(t, f"A_vaug_{g-1}_3")
                            if alv >= 7:
                                e.wait(t, f"A_spadd_{g-1}_31")
                        for nh2 in range(2):
                            dst = half(3, nh2)
                            for d in range(8):
                                mm = t.matmul(dst,
                                              xh[p][:, AG * d + 128 * tt:AG * d + 128 * (tt + 1)],
                                              wvT[:, 1024 * d + 512 * nh2:1024 * d + 512 * (nh2 + 1)],
                                              start=(d == 0), stop=(d == 7))
                        e.inc(mm, "sT", 1)
                        e.mark(f"A_vp_{g}_{tt}", "sT")
                    e.wait(t, f"A_xh_{g}")
                    if alv >= 4:
                        e.wait(t, f"A_sq_{g}")
                    for tt in range(4 if alv >= 4 else 0):
                        for c in range(8):
                            mm = t.matmul(K[0][:, 128 * tt:128 * tt + 16],
                                          sqb[:, AG * c + 128 * tt:AG * c + 128 * (tt + 1)],
                                          sel[:, 16 * c:16 * (c + 1)],
                                          start=(c == 0), stop=(c == 7))
                        e.inc(mm, "sT", 1)
                    e.mark(f"A_ns_{g}", "sT")
                    if alv >= 5:
                        e.wait(t, f"A_t2_{g}_7")
                    cnt = 0
                    for h in range(NH if alv >= 5 else 0):
                        c, par = h // 2, (h % 2) * 64
                        for tt in range(4):
                            if cnt >= 2:
                                e.wait(t, f"A_expk_{g}_{cnt-2}")
                            dst = half(2, cnt % 2)[:, 0:256]
                            sl = slice(AG * c + 128 * tt, AG * c + 128 * (tt + 1))
                            mm = t.matmul(dst, t1b[par:par + 64, sl],
                                          fpt[par:par + 64, 256 * c:256 * (c + 1)],
                                          start=True, stop=False)
                            mm = t.matmul(dst, t2b[par:par + 64, sl],
                                          fpt[par:par + 64, 256 * c:256 * (c + 1)],
                                          start=False, stop=True)
                            e.inc(mm, "sT", 1)
                            e.mark(f"A_fk_{g}_{cnt}", "sT")
                            cnt += 1
                    if alv >= 7:
                        e.wait(t, f"A_vaug_{g}_3")
                        e.wait(t, f"A_expk_{g}_63")
                    scnt = 0
                    slot_cols = [0, 512, 128, 640]
                    for h in range(NH if alv >= 7 else 0):
                        for m in range(2):
                            if scnt >= 2:
                                e.wait(t, f"A_spadd_{g}_{scnt-2}")
                            dst = K[3][:, slot_cols[scnt % 4]:slot_cols[scnt % 4] + 65]
                            for tt in range(4):
                                kbase = (h * 4 + tt) * 256 + 128 * m
                                mm = t.matmul(dst, kphi[:, kbase:kbase + 128],
                                              vaug[:, 1040 * tt + 65 * h:1040 * tt + 65 * (h + 1)],
                                              start=(tt == 0), stop=(tt == 3))
                            e.inc(mm, "sT", 1)
                            e.mark(f"A_sp_{g}_{scnt}", "sT")
                            scnt += 1
                    e.mark(f"A_te_done_{g}", "sT")

            if "B" in phases:
                qT = kT
                woTb = wvT
                e.wait(t, "G_wq")
                e.wait(t, "G_sbf")
                e.wait(t, f"A_sexp_done_{NAG-1}")
                e.wait(t, f"A_nsb_{NAG-1}")
                for g in range(NAG):
                    p = g % 2
                    cc = g // (NAG // 2)
                    e.wait(t, f"B_h_{g}")
                    if g > 0:
                        e.wait(t, f"B_t2_{g-1}_7")
                    mm = t.matmul(half(0, 0), onescol[:], mrow[0:1, AG * g:AG * (g + 1)],
                                  start=True, stop=True)
                    mm = t.matmul(half(0, 1), onescol[:], rrow[0:1, AG * g:AG * (g + 1)],
                                  start=True, stop=True)
                    e.inc(mm, "sT", 1)
                    e.mark(f"B_mbrb_{g}", "sT")
                    e.wait(t, f"B_xh_{g}")
                    for c in range(8):
                        if c >= 2:
                            e.wait(t, f"B_qt_{g}_{c-2}")
                        elif g > 0:
                            e.wait(t, f"B_qt_{g-1}_{6+c}")
                        dst = half(1, c % 2)
                        for d in range(8):
                            mm = t.matmul(dst, wkT[:, 1024 * d + 128 * c:1024 * d + 128 * (c + 1)],
                                          xh[p][:, AG * d:AG * (d + 1)],
                                          start=(d == 0), stop=(d == 7))
                        e.inc(mm, "sT", 1)
                        e.mark(f"B_qp_{g}_{c}", "sT")
                    for c in range(8):
                        e.wait(t, f"B_qt_{g}_{c}")
                        if c >= 2:
                            e.wait(t, f"B_t2_{g}_{c-2}")
                        mm = t.matmul(half(0, c % 2), jmat[:], qT[:, AG * c:AG * (c + 1)],
                                      start=True, stop=True)
                        e.inc(mm, "sT", 1)
                        e.mark(f"B_jp_{g}_{c}", "sT")
                    e.wait(t, f"B_sq_{g}")
                    if g > 0:
                        e.wait(t, f"B_div_{g-1}_7")
                        e.wait(t, f"B_div_{g-1}_6")
                    for tt in range(4):
                        for c in range(8):
                            mm = t.matmul(K[2][:, 128 * tt:128 * tt + 16],
                                          sqb[:, AG * c + 128 * tt:AG * c + 128 * (tt + 1)],
                                          sel[:, 16 * c:16 * (c + 1)],
                                          start=(c == 0), stop=(c == 7))
                    e.inc(mm, "sT", 1)
                    e.mark(f"B_ns_{g}", "sT")
                    e.wait(t, f"B_t2_{g}_7")
                    if g > 0:
                        e.wait(t, f"B_v_done_{g-1}")
                    for b in range(8):
                        if b >= 2:
                            e.wait(t, f"B_div_{g}_{b-2}")
                        npsum = K[2] if b % 2 == 0 else K[3]
                        for hh in range(2):
                            h = 2 * b + hh
                            c2, par = h // 2, (h % 2) * 64
                            if h >= 2:
                                e.wait(t, f"B_exp_{g}_{h-2}")
                            for m in range(2):
                                dst = half(hh, m) if False else half(0 if hh == 0 else 1, m)
                                sl = slice(AG * c2, AG * (c2 + 1))
                                mm = t.matmul(dst, fpt[par:par + 64, 256 * c2 + 128 * m:256 * c2 + 128 * (m + 1)].tensor[par:par + 64, 256 * c2 + 128 * m:256 * c2 + 128 * (m + 1)] if False else fpt[par:par + 64, 256 * c2 + 128 * m:256 * c2 + 128 * (m + 1)],
                                              t1b[par:par + 64, sl], start=True, stop=False)
                                mm = t.matmul(dst, fpt[par:par + 64, 256 * c2 + 128 * m:256 * c2 + 128 * (m + 1)],
                                              t2b[par:par + 64, sl], start=False, stop=True)
                            e.inc(mm, "sT", 1)
                            e.mark(f"B_ft_{g}_{h}", "sT")
                            e.wait(t, f"B_exp_{g}_{h}")
                            for tt in range(4):
                                dst = npsum[:, 128 * (4 * hh + tt):128 * (4 * hh + tt) + 65]
                                for m in range(2):
                                    off = 65 * ((cc * 16 + h) * 2 + m)
                                    mm = t.matmul(dst,
                                                  qphib_ap()[:, 1024 * (h % 2) + 512 * m + 128 * tt:1024 * (h % 2) + 512 * m + 128 * (tt + 1)],
                                                  s_bf[:, off:off + 65],
                                                  start=(m == 0), stop=(m == 1))
                            e.inc(mm, "sT", 1)
                        e.mark(f"B_nmm_{g}_{b}", "sT")
                    e.wait(t, f"B_div_{g}_7")
                    e.wait(t, f"B_div_{g}_6")
                    tslots = [K[2][:, 0:64].bitcast(BF16), K[2][:, 512:576].bitcast(BF16),
                              K[3][:, 0:64].bitcast(BF16), K[3][:, 512:576].bitcast(BF16)]
                    ti = 0
                    for tt in range(4):
                        for rnd in range(2):
                            if ti >= 1:
                                e.wait(t, f"B_oTev_{g}_{ti-1}")
                            for j in range(4):
                                c = 4 * rnd + j
                                mm = t.transpose(tslots[j][:, 0:128],
                                                 ohbuf[:, 1024 * tt + 128 * c:1024 * tt + 128 * (c + 1)],
                                                 ident[:])
                            e.inc(mm, "sT", 1)
                            e.mark(f"B_oTt_{g}_{ti}", "sT")
                            ti += 1
                    e.wait(t, f"B_oTev_{g}_7")
                    for dc in range(8):
                        if dc >= 2:
                            e.wait(t, f"B_at_{g}_{dc-2}")
                        dst = half(0, dc % 2)
                        for ec in range(8):
                            mm = t.matmul(dst, woTb[:, 1024 * ec + 128 * dc:1024 * ec + 128 * (dc + 1)],
                                          oTbuf_ap()[:, 512 * ec:512 * (ec + 1)],
                                          start=(ec == 0), stop=(ec == 7))
                        e.inc(mm, "sT", 1)
                        e.mark(f"B_pj_{g}_{dc}", "sT")
                    e.wait(t, f"B_atsq_{g}")
                    for tt in range(4):
                        for d in range(8):
                            mm = t.matmul(K[1][:, 128 * tt:128 * tt + 1],
                                          sqb[:, 512 * d + 128 * tt:512 * d + 128 * (tt + 1)],
                                          onescol2[:], start=(d == 0), stop=(d == 7))
                        for d in range(8):
                            mm = t.matmul(K[1][:, 512 + 128 * tt:512 + 128 * tt + 1],
                                          t2b[:, 512 * d + 128 * tt:512 * d + 128 * (tt + 1)],
                                          onescol2[:], start=(d == 0), stop=(d == 7))
                    e.inc(mm, "sT", 1)
                    e.mark(f"B_statmm_{g}", "sT")
                    e.wait(t, f"B_statbf_{g}")
                    for j in range(8):
                        dst = K[2][0:1, 128 * j:128 * j + 128] if j < 4 else \
                              K[2][0:1, 512 + 128 * (j - 4):512 + 128 * (j - 4) + 128]
                        mm = t.matmul(dst, mcol_bf[:, j:j + 1], ident[:], start=True, stop=True)
                    e.inc(mm, "sT", 1)
                    e.mark(f"B_stT_{g}", "sT")
                    e.wait(t, f"B_strow_{g}")
                    mm = t.matmul(half(3, 0), onescol[:], m2row, start=True, stop=True)
                    mm = t.matmul(half(3, 1), onescol[:], r2row, start=True, stop=True)
                    e.inc(mm, "sT", 1)
                    e.mark(f"B_mbrb2_{g}", "sT")
                    e.mark(f"B_te_done_{g}", "sT")

                    if "C" not in phases:
                        continue
                    x1T = kphi[:].bitcast(F32)
                    x1off = lambda c: 1024 * c + 512 * (g % 2)
                    bundle = lambda qq: s_acc[:].bitcast(BF16)[:, 3456 * qq:3456 * (qq + 1)]
                    wdbuf = lambda qq: s_acc[:].bitcast(BF16)[:, 3456 * qq + 2432:3456 * qq + 3456]
                    if g == 0:
                        e.wait(t, "G_cconsts")
                    e.wait(t, f"C_xb_{g}")
                    e.wait(t, f"C_sqx3_{g}")
                    for tt in range(4):
                        for d in range(8):
                            mm = t.matmul(K[0][:, 128 * tt:128 * tt + 1],
                                          xh[g % 2][:, 512 * d + 128 * tt:512 * d + 128 * (tt + 1)],
                                          onescol2[:], start=(d == 0), stop=(d == 7))
                        for d in range(8):
                            mm = t.matmul(K[0][:, 512 + 128 * tt:512 + 128 * tt + 1],
                                          t2b[:, 512 * d + 128 * tt:512 * d + 128 * (tt + 1)],
                                          onescol2[:], start=(d == 0), stop=(d == 7))
                    e.inc(mm, "sT", 1)
                    e.mark(f"C_statmm_{g}", "sT")
                    e.wait(t, f"C_statbf_{g}")
                    for j in range(8):
                        dst = K[2][0:1, 128 * j:128 * j + 128] if j < 4 else \
                              K[2][0:1, 512 + 128 * (j - 4):512 + 128 * (j - 4) + 128]
                        mm = t.matmul(dst, mcol_bf[:, j:j + 1], ident[:], start=True, stop=True)
                    e.inc(mm, "sT", 1)
                    e.mark(f"C_stT_{g}", "sT")
                    e.wait(t, f"C_strow_{g}")
                    mm = t.matmul(half(0, 0), onescol[:], m2row, start=True, stop=True)
                    mm = t.matmul(half(0, 1), onescol[:], r2row, start=True, stop=True)
                    e.inc(mm, "sT", 1)
                    e.mark(f"C_mbrb3_{g}", "sT")
                    e.wait(t, f"C_x2h_{g}")
                    for tt in range(4):
                        for c in range(8):
                            mm = t.matmul(K[1][:, 128 * tt:128 * tt + 8],
                                          t1b[:, 512 * c + 128 * tt:512 * c + 128 * (tt + 1)],
                                          gatewT[:, 8 * c:8 * (c + 1)],
                                          start=(c == 0), stop=(c == 7))
                    e.inc(mm, "sT", 1)
                    e.mark(f"C_gate_{g}", "sT")
                    e.wait(t, f"C_top2_{g}")
                    for tt in range(4):
                        mm = t.matmul(K[2][0:8, 128 * tt:128 * (tt + 1)],
                                      mask12[:, 16 * tt:16 * tt + 8], ident[:],
                                      start=True, stop=True)
                        mm = t.matmul(K[2][0:8, 512 + 128 * tt:512 + 128 * (tt + 1)],
                                      mask12[:, 16 * tt + 8:16 * (tt + 1)], ident[:],
                                      start=True, stop=True)
                        mm = t.matmul(K[3][0:1, 128 * tt:128 * (tt + 1)],
                                      rwb[:, 2 * tt:2 * tt + 1], ident[:],
                                      start=True, stop=True)
                        mm = t.matmul(K[3][0:1, 512 + 128 * tt:512 + 128 * (tt + 1)],
                                      rwb[:, 2 * tt + 1:2 * tt + 2], ident[:],
                                      start=True, stop=True)
                    e.inc(mm, "sT", 1)
                    e.mark(f"C_maskT_{g}", "sT")
                    e.wait(t, f"C_mrows_{g}")
                    mm = t.matmul(half(3, 0), exp8s, ctbuf[g % 2][0:8, :], start=True, stop=True)
                    mm = t.matmul(half(3, 1), exp8s, stbuf[g % 2][0:8, :], start=True, stop=True)
                    e.inc(mm, "sT", 1)
                    e.mark(f"C_mer_{g}", "sT")
                    e.wait(t, f"C_merc_{g}")
                    mm = t.matmul(half(3, 0), onescol[:], m2row, start=True, stop=True)
                    mm = t.matmul(half(3, 1), onescol[:], r2row, start=True, stop=True)
                    e.inc(mm, "sT", 1)
                    e.mark(f"C_wb_{g}", "sT")
                    e.wait(t, f"C_wbc_{g}")
                    for d in range(8):
                        mm = t.matmul(half(3, 0), a1catTs[:, 128 * d:128 * (d + 1)],
                                      t1b[:, 512 * d:512 * (d + 1)],
                                      start=(d == 0), stop=(d == 7))
                    e.inc(mm, "sT", 1)
                    e.mark(f"C_u1_{g}", "sT")
                    for d in range(8):
                        mm = t.matmul(half(3, 1), a3catTs[:, 128 * d:128 * (d + 1)],
                                      t1b[:, 512 * d:512 * (d + 1)],
                                      start=(d == 0), stop=(d == 7))
                    e.inc(mm, "sT", 1)
                    e.mark(f"C_u3_{g}", "sT")
                    for fc in range(NFC):
                        qq = fc % 2
                        e.wait(t, f"C_bun_{g}_{fc}")
                        if fc == 0:
                            e.wait(t, f"C_mu3_{g}")
                        if fc > 0:
                            e.wait(t, f"S_silu_{g}_{fc-1}")
                            e.wait(t, f"C_mix_{g}_{fc-1}")
                        for d in range(8):
                            mm = t.matmul(half(0, 0), bundle(qq)[:, 128 * d:128 * (d + 1)],
                                          t1b[:, 512 * d:512 * (d + 1)],
                                          start=(d == 0), stop=(d == 7))
                        for d in range(8):
                            mm = t.matmul(half(1, 0), bundle(qq)[:, 1024 + 128 * d:1024 + 128 * (d + 1)],
                                          t1b[:, 512 * d:512 * (d + 1)],
                                          start=(d == 0), stop=(d == 7))
                        e.inc(mm, "sT", 1)
                        e.mark(f"C_cc_{g}_{fc}", "sT")
                        e.wait(t, f"S_cs_{g}_{fc}")
                        mm = t.matmul(half(0, 0), bundle(qq)[:, 2048:2176],
                                      sqb[:, 0:512], start=False, stop=True,
                                      skip_group_check=True)
                        mm = t.matmul(half(1, 0), bundle(qq)[:, 2176:2304],
                                      sqb[:, 1024:1536], start=False, stop=True,
                                      skip_group_check=True)
                        mm = t.matmul(half(0, 1), bundle(qq)[:, 2048:2176],
                                      sqb[:, 512:1024], start=True, stop=True)
                        mm = t.matmul(half(1, 1), bundle(qq)[:, 2176:2304],
                                      sqb[:, 1536:2048], start=True, stop=True)
                        e.inc(mm, "sT", 1)
                        e.mark(f"C_dd_{g}_{fc}", "sT")
                        e.wait(t, f"C_w1_{g}_{fc}")
                        mm = t.matmul(half(2, 0), bundle(qq)[:, 2304:2432],
                                      t2b[:, 3072:3584],
                                      start=(fc == 0), stop=(fc == NFC - 1))
                        e.wait(t, f"C_mix_{g}_{fc}")
                        mm = t.matmul(half(2, 1), bundle(qq)[:, 2304:2432],
                                      mixT_ap(g, fc), start=(fc == 0), stop=(fc == NFC - 1))
                        e.inc(mm, "sT", 1)
                        e.mark(f"C_z_{g}_{fc}", "sT")
                        e.mark(f"C_bfree_{g}_{fc}", "sT")
                    e.mark(f"C_c1done_{g}", "sT")
                    for ps in range(2):
                        for fc in range(NFC):
                            qq = fc % 2
                            e.wait(t, f"C_wd_{g}_{ps}_{fc}")
                            if ps == 0 and fc == 0:
                                e.wait(t, f"C_zhat_{g}")
                            if ps == 1 and fc == 0:
                                e.wait(t, f"C_ffnt_{g}_0")
                            for dc in range(8):
                                dst = (K[2] if dc < 4 else K[3])[:, 256 * (dc % 4):256 * (dc % 4 + 1)]
                                mm = t.matmul(dst, wdbuf(qq)[:, 128 * dc:128 * (dc + 1)],
                                              mixT_ap(g, fc)[:, 256 * ps:256 * (ps + 1)],
                                              start=(fc == 0 and dc % 2 == 0), stop=True,
                                              skip_group_check=True)
                            e.inc(mm, "sT", 1)
                            e.mark(f"C_wfree_{g}_{ps}_{fc}", "sT")
                        if ps == 0:
                            e.wait(t, f"C_b2_{g}")
                        for dc in range(8):
                            dst = (K[2] if dc < 4 else K[3])[:, 256 * (dc % 4):256 * (dc % 4 + 1)]
                            mm = t.matmul(dst, b2cats[:, 128 * dc:128 * (dc + 1)],
                                          zhat[:, 256 * ps:256 * (ps + 1)],
                                          start=False, stop=True, skip_group_check=True)
                        e.inc(mm, "sT", 1)
                        e.mark(f"C_dpsum_{g}_{ps}", "sT")
                        e.wait(t, f"C_ffnt_{g}_{ps}")
                        for q in range(2):
                            for dc in range(8):
                                mm = t.matmul(K[0][:, 128 * q:128 * q + 1],
                                              xh[g % 2][:, 256 * dc + 128 * q:256 * dc + 128 * q + 128],
                                              onescol2[:], start=(dc == 0), stop=(dc == 7))
                            for dc in range(8):
                                mm = t.matmul(K[0][:, 512 + 128 * q:512 + 128 * q + 1],
                                              vaug[:, 256 * dc + 128 * q:256 * dc + 128 * q + 128],
                                              onescol2[:], start=(dc == 0), stop=(dc == 7))
                        e.inc(mm, "sT", 1)
                        e.mark(f"C_statmm4_{g}_{ps}", "sT")
                        e.wait(t, f"C_statbf4_{g}_{ps}")
                        for (j, col) in [(0, 0), (1, 128), (4, 512), (5, 640)]:
                            mm = t.matmul(K[1][0:1, col:col + 128], mcol_bf[:, j:j + 1],
                                          ident[:], start=True, stop=True)
                        e.inc(mm, "sT", 1)
                        e.mark(f"C_stT4_{g}_{ps}", "sT")
                        e.wait(t, f"C_strow4_{g}_{ps}")
                        mm = t.matmul(K[1][:, 0:256], onescol[:], m2row[0:1, 0:256],
                                      start=True, stop=True)
                        mm = t.matmul(K[1][:, 512:768], onescol[:], r2row[0:1, 0:256],
                                      start=True, stop=True)
                        e.inc(mm, "sT", 1)
                        e.mark(f"C_mbrb4_{g}_{ps}", "sT")
                    e.mark(f"C_te_done_{g}", "sT")

        def prog_scalar(sc):
            e = env
            if "P0" in phases:
                pf = [K[3][0:64, 0:64], K[3][0:64, 512:576]]
                for h in range(NH):
                    for r in range(NR):
                        for si in range(6):
                            e.wait(sc, f"T_sorf_{h}_{r}_{si}")
                            src = pf[si % 2]
                            if si < 5:
                                i2 = sc.activation(sorf_tmp[si % 2], src, AF.Identity,
                                                   bias=0.0, scale=1.0)
                            else:
                                if h % 2 == 0:
                                    dst = fpt[0:64, 256 * (h // 2) + 64 * r:256 * (h // 2) + 64 * (r + 1)]
                                else:
                                    dst = fstage[0:64, 256 * (h // 2) + 64 * r:256 * (h // 2) + 64 * (r + 1)]
                                i2 = sc.activation(dst, src, AF.Identity, bias=0.0, scale=INV3)
                            e.inc(i2, "sS", 1)
                            e.mark(f"S_sorf_{h}_{r}_{si}_done", "sS")
                e.mark("S_fpt_evac", "sS")

            if "A" in phases:
                for g in range(nag):
                    e.wait(sc, f"A_xc_{g}")
                    if g > 0:
                        e.wait(sc, f"A_ns_{g-1}")
                    if sub & 8:
                        i2 = sc.drain()
                    else:
                        i2 = sc.activation(sqb[:], xh[g % 2][:], AF.Square)
                    e.inc(i2, "sS", 1)
                    e.mark(f"A_sqx_{g}", "sS")
                    e.wait(sc, f"A_stat_{g}")
                    i2 = sc.activation(rstd4[:], rv4[:], AF.Sqrt)
                    e.inc(i2, "sS", 1)
                    e.mark(f"A_sqrt_{g}", "sS")
                    e.wait(sc, f"A_stT_{g}")
                    col = 512 * g
                    i2 = sc.activation(mrow[0:1, col:col + 512], K[2][0:1, 0:512],
                                       AF.Identity, bias=0.0, scale=1.0)
                    i2 = sc.activation(rrow[0:1, col:col + 512], K[2][0:1, 512:1024],
                                       AF.Identity, bias=0.0, scale=1.0)
                    e.inc(i2, "sS", 1)
                    e.mark(f"A_strow_{g}", "sS")
                    for c in range(8 if alv >= 2 else 0):
                        e.wait(sc, f"A_kp_{g}_{c}")
                        if c == 0:
                            e.wait(sc, f"A_statmm_{g}")
                        if g > 0 and c == 0:
                            e.wait(sc, f"A_te_done_{g-1}")
                        i2 = sc.copy(kT[:, AG * c:AG * (c + 1)], half(1, c % 2))
                        e.inc(i2, "sS", 1)
                        e.mark(f"A_kt_{g}_{c}", "sS")
                        sc.drain()
                        i2 = sc.activation(sqb[:, AG * c:AG * (c + 1)],
                                           kT[:, AG * c:AG * (c + 1)], AF.Square)
                        e.inc(i2, "sS", 1)
                    e.mark(f"A_sq_{g}", "sS")
                    for tt in range(4 if alv >= 6 else 0):
                        e.wait(sc, f"A_vp_{g}_{tt}")
                        base = 1040 * tt
                        dst = vaug[:, base:base + 1040].rearrange("p (h c) -> p h c", c=65)
                        i2 = sc.copy(dst[:, 0:8, 0:64],
                                     half(3, 0).rearrange("p (h c) -> p h c", c=64))
                        i2 = sc.copy(dst[:, 8:16, 0:64],
                                     half(3, 1).rearrange("p (h c) -> p h c", c=64))
                        e.inc(i2, "sS", 1)
                        e.mark(f"A_vaug_{g}_{tt}", "sS")
                    if alv >= 4:
                        e.wait(sc, f"A_ns_{g}")
                        e.wait(sc, "V_cvec")
                    for tt in range(4 if alv >= 4 else 0):
                        i2 = sc.activation(nsb[:, 16 * tt:16 * (tt + 1)],
                                           K[0][:, 128 * tt:128 * tt + 16],
                                           AF.Identity, bias=cvec[:, 0:1], scale=-0.5)
                        e.inc(i2, "sS", 1)
                    sc.drain()
                    e.mark(f"A_nsb_{g}", "sS")
                    cnt = 0
                    for h in range(NH if alv >= 5 else 0):
                        for tt in range(4):
                            e.wait(sc, f"A_fk_{g}_{cnt}")
                            i2 = sc.activation(kphi[:, (h * 4 + tt) * 256:(h * 4 + tt + 1) * 256],
                                               half(2, cnt % 2)[:, 0:256], AF.Exp,
                                               bias=nsb[:, 16 * tt + h:16 * tt + h + 1],
                                               scale=1.0)
                            e.inc(i2, "sS", 1)
                            e.mark(f"A_expk_{g}_{cnt}", "sS")
                            cnt += 1
                    e.mark(f"A_sexp_done_{g}", "sS")

            if "B" in phases:
                qT = kT
                for g in range(NAG):
                    p = g % 2
                    for c in range(8):
                        e.wait(sc, f"B_qp_{g}_{c}")
                        if g == 0 and c == 0:
                            e.wait(sc, f"A_te_done_{NAG-1}")
                        elif g > 0 and c == 0:
                            e.wait(sc, f"B_te_done_{g-1}")
                        if g > 0 and c == 0 and "C" in phases:
                            e.wait(sc, f"C_y_{g-1}_1")
                        if g == 1 and c == 0 and "attnt" in dbg:
                            e.wait(sc, "H_atdump")
                        i2 = sc.copy(qT[:, AG * c:AG * (c + 1)], half(1, c % 2))
                        e.inc(i2, "sS", 1)
                        e.mark(f"B_qt_{g}_{c}", "sS")
                        sc.drain()
                        i2 = sc.activation(sqb[:, AG * c:AG * (c + 1)],
                                           qT[:, AG * c:AG * (c + 1)], AF.Square)
                        e.inc(i2, "sS", 1)
                    e.mark(f"B_sq_{g}", "sS")
                    e.wait(sc, f"B_ns_{g}")
                    e.wait(sc, "V_cvec")
                    for tt in range(4):
                        i2 = sc.activation(fsc[:, 16 * tt:16 * (tt + 1)],
                                           K[2][:, 128 * tt:128 * tt + 16],
                                           AF.Exp, bias=cvec[:, 0:1], scale=-0.5)
                        e.inc(i2, "sS", 1)
                    sc.drain()
                    e.mark(f"B_fsc_{g}", "sS")
                    for h in range(NH):
                        e.wait(sc, f"B_ft_{g}_{h}")
                        hh = h % 2
                        base = 1024 * (h % 2)
                        for m in range(2):
                            i2 = sc.activation(qphib_ap()[:, base + 512 * m:base + 512 * (m + 1)],
                                               half(0 if hh == 0 else 1, m), AF.Exp)
                        e.inc(i2, "sS", 1)
                        e.mark(f"B_exp_{g}_{h}", "sS")
                    tslots = [K[2][:, 0:64].bitcast(BF16), K[2][:, 512:576].bitcast(BF16),
                              K[3][:, 0:64].bitcast(BF16), K[3][:, 512:576].bitcast(BF16)]
                    ti = 0
                    for tt in range(4):
                        for rnd in range(2):
                            e.wait(sc, f"B_oTt_{g}_{ti}")
                            for j in range(4):
                                c = 4 * rnd + j
                                i2 = sc.copy(oTbuf_ap()[:, 512 * c + 128 * tt:512 * c + 128 * (tt + 1)],
                                             tslots[j][:, 0:128])
                            e.inc(i2, "sS", 1)
                            e.mark(f"B_oTev_{g}_{ti}", "sS")
                            ti += 1
                    for dc in range(8):
                        e.wait(sc, f"B_pj_{g}_{dc}")
                        i2 = sc.copy(sqb[:, 512 * dc:512 * (dc + 1)], half(0, dc % 2))
                        e.inc(i2, "sS", 1)
                        e.mark(f"B_at_{g}_{dc}", "sS")
                    sc.drain()
                    i2 = sc.activation(t2b[:], sqb[:], AF.Square)
                    e.inc(i2, "sS", 1)
                    e.mark(f"B_atsq_{g}", "sS")
                    e.wait(sc, f"B_stat_{g}")
                    i2 = sc.activation(rstd4[:], rv4[:], AF.Sqrt)
                    e.inc(i2, "sS", 1)
                    e.mark(f"B_sqrt_{g}", "sS")
                    e.wait(sc, f"B_stT_{g}")
                    i2 = sc.activation(m2row, K[2][0:1, 0:512],
                                       AF.Identity, bias=0.0, scale=1.0)
                    i2 = sc.activation(r2row, K[2][0:1, 512:1024],
                                       AF.Identity, bias=0.0, scale=1.0)
                    e.inc(i2, "sS", 1)
                    e.mark(f"B_strow_{g}", "sS")

                    if "C" not in phases:
                        continue
                    sqbf32 = sqb[:].bitcast(F32)
                    e.wait(sc, f"C_xb_{g}")
                    i2 = sc.activation(t2b[:], xh[g % 2][:], AF.Square)
                    e.inc(i2, "sS", 1)
                    e.mark(f"C_sqx3_{g}", "sS")
                    e.wait(sc, f"C_stat_{g}")
                    i2 = sc.activation(rstd4[:], rv4[:], AF.Sqrt)
                    e.inc(i2, "sS", 1)
                    e.mark(f"C_sqrt_{g}", "sS")
                    e.wait(sc, f"C_stT_{g}")
                    i2 = sc.activation(m2row, K[2][0:1, 0:512], AF.Identity,
                                       bias=0.0, scale=1.0)
                    i2 = sc.activation(r2row, K[2][0:1, 512:1024], AF.Identity,
                                       bias=0.0, scale=1.0)
                    e.inc(i2, "sS", 1)
                    e.mark(f"C_strow_{g}", "sS")
                    e.wait(sc, f"C_top2m_{g}")
                    i2 = sc.activation(fsc[:, 0:4], tmp8a[:, 4:8], AF.Sigmoid)
                    e.inc(i2, "sS", 1)
                    e.mark(f"S_sig_{g}", "sS")
                    e.wait(sc, f"C_maskT_{g}")
                    i2 = sc.copy(ctbuf[g % 2][0:8, :], K[2][0:8, 0:512])
                    i2 = sc.copy(stbuf[g % 2][0:8, :], K[2][0:8, 512:1024])
                    i2 = sc.activation(m2row, K[3][0:1, 0:512], AF.Identity,
                                       bias=0.0, scale=1.0)
                    i2 = sc.activation(r2row, K[3][0:1, 512:1024], AF.Identity,
                                       bias=0.0, scale=1.0)
                    e.inc(i2, "sS", 1)
                    e.mark(f"C_mrows_{g}", "sS")
                    for fc in range(NFC):
                        e.wait(sc, f"C_cc_{g}_{fc}")
                        i2 = sc.copy(sqbf32[:, 1024:1536], half(0, 0))
                        i2 = sc.copy(sqbf32[:, 1536:2048], half(1, 0))
                        e.inc(i2, "sS", 1)
                        e.mark(f"S_cs_{g}_{fc}", "sS")
                        e.wait(sc, f"C_dd_{g}_{fc}")
                        i2 = sc.activation(vaug[:].bitcast(F32)[:, 0:512], half(0, 0), AF.Silu)
                        e.inc(i2, "sS", 1)
                        e.mark(f"S_silu1_{g}_{fc}", "sS")
                        e.wait(sc, f"C_add_{g}_{fc}")
                        i2 = sc.activation(vaug[:].bitcast(F32)[:, 512:1024],
                                           vaug[:].bitcast(F32)[:, 1024:1536], AF.Silu)
                        e.inc(i2, "sS", 1)
                        e.mark(f"S_silu_{g}_{fc}", "sS")
                    for ps in range(2):
                        e.wait(sc, f"C_dpsum_{g}_{ps}")
                        i2 = sc.copy(xh[g % 2][:, 0:1024], K[2][:])
                        i2 = sc.copy(xh[g % 2][:, 1024:2048], K[3][:])
                        sc.drain()
                        i2 = sc.activation(vaug[:, 0:2048], xh[g % 2][:, 0:2048], AF.Square)
                        e.inc(i2, "sS", 1)
                        e.mark(f"C_ffnt_{g}_{ps}", "sS")
                        e.wait(sc, f"C_stat4_{g}_{ps}")
                        i2 = sc.activation(rstd4[:, 0:2], rv4[:, 0:2], AF.Sqrt)
                        e.inc(i2, "sS", 1)
                        e.mark(f"C_sqrt4_{g}_{ps}", "sS")
                        e.wait(sc, f"C_stT4_{g}_{ps}")
                        i2 = sc.activation(m2row[0:1, 0:256], K[1][0:1, 0:256], AF.Identity,
                                           bias=0.0, scale=1.0)
                        i2 = sc.activation(r2row[0:1, 0:256], K[1][0:1, 512:768], AF.Identity,
                                           bias=0.0, scale=1.0)
                        e.inc(i2, "sS", 1)
                        e.mark(f"C_strow4_{g}_{ps}", "sS")

        def prog_vector(v):
            e = env
            if "A" in phases:
                v.memset(s_acc[:], 0.0)
                i2 = v.memset(cvec[:, 0:1], -LN16)
                v.drain()
                e.inc(i2, "sV", 1)
                e.mark("V_cvec", "sV")
                for g in range(nag):
                    p = g % 2
                    e.wait(v, f"A_h_{g}")
                    if g > 0:
                        if alv >= 4:
                            e.wait(v, f"A_ns_{g-1}")
                        if alv >= 2:
                            e.wait(v, f"A_kp_{g-1}_7")
                        if alv >= 6:
                            e.wait(v, f"A_vp_{g-1}_3")
                        e.wait(v, f"A_statmm_{g-1}")
                    i2 = v.tensor_copy(xh[p][:], xTbuf[p][:])
                    v.drain()
                    e.inc(i2, "sV", 1)
                    e.mark(f"A_xc_{g}", "sV")
                    e.wait(v, f"A_statmm_{g}")
                    v.tensor_scalar(stat4m[:], K[0][:, 0:512].rearrange("p (tt o) -> p tt o", o=128)[:, :, 0:1].squeeze(2),
                                    1.0 / D_MODEL, None, ALU.mult)
                    v.tensor_scalar(stat4s[:], K[0][:, 512:1024].rearrange("p (tt o) -> p tt o", o=128)[:, :, 0:1].squeeze(2),
                                    1.0 / D_MODEL, None, ALU.mult)
                    v.drain()
                    v.tensor_tensor(tmp4[:], stat4m[:], stat4m[:], ALU.mult)
                    v.drain()
                    v.tensor_sub(stat4s[:], stat4s[:], tmp4[:])
                    v.drain()
                    v.tensor_scalar(stat4s[:], stat4s[:], EPS_LN, None, ALU.add)
                    v.drain()
                    i2 = v.reciprocal(rv4[:], stat4s[:])
                    e.inc(i2, "sV", 1)
                    e.mark(f"A_stat_{g}", "sV")
                    e.wait(v, f"A_sqrt_{g}")
                    if g > 0:
                        e.wait(v, f"A_stT_{g-1}")
                    v.tensor_copy(mcol_bf[:, 0:4], stat4m[:])
                    i2 = v.tensor_copy(mcol_bf[:, 4:8], rstd4[:])
                    v.drain()
                    e.inc(i2, "sV", 1)
                    e.mark(f"A_statbf_{g}", "sV")
                    e.wait(v, f"A_mbrb_{g}")
                    pass
                    for c in range(8 if (sub & 2) else 0):
                        v.tensor_sub(xTbuf[p][:, AG * c:AG * (c + 1)],
                                     xTbuf[p][:, AG * c:AG * (c + 1)], half(0, 0))
                        v.drain()
                        i2 = v.tensor_mul(xh[p][:, AG * c:AG * (c + 1)],
                                          xTbuf[p][:, AG * c:AG * (c + 1)], half(0, 1))
                        e.inc(i2, "sV", 1) if c == 7 else None
                    if not (sub & 2):
                        i2 = v.drain()
                        e.inc(i2, "sV", 1)
                    e.mark(f"A_xh_{g}", "sV")
                    for c in range(8 if alv >= 3 else 0):
                        e.wait(v, f"A_kt_{g}_{c}")
                        if g > 0 and c == 0:
                            e.wait(v, f"A_te_done_{g-1}")
                        i2 = v.tensor_mul(t1b[:, AG * c:AG * (c + 1)], ctbuf[p][:],
                                          kT[:, AG * c:AG * (c + 1)])
                        e.wait(v, f"A_jp_{g}_{c}")
                        i2 = v.tensor_mul(t2b[:, AG * c:AG * (c + 1)], stbuf[p][:],
                                          half(2, c % 2))
                        e.inc(i2, "sV", 1)
                        e.mark(f"A_t2_{g}_{c}", "sV")
                    e.mark(f"A_vfree_{g}", "sV")
                    if alv < 3:
                        pass
                    slot_cols = [0, 512, 128, 640]
                    scnt = 0
                    for h in range(NH if alv >= 7 else 0):
                        for m in range(2):
                            e.wait(v, f"A_sp_{g}_{scnt}")
                            cc = (g // (NAG // 2))
                            off = 65 * ((cc * 16 + h) * 2 + m)
                            i2 = v.tensor_add(s_acc[:, off:off + 65], s_acc[:, off:off + 65],
                                              K[3][:, slot_cols[scnt % 4]:slot_cols[scnt % 4] + 65])
                            e.inc(i2, "sV", 1)
                            e.mark(f"A_spadd_{g}_{scnt}", "sV")
                            scnt += 1
                    e.mark(f"A_v_done_{g}", "sV")
                e.mark("A_sacc_done", "sV")

            if "B" in phases:
                x1T = kphi[:].bitcast(F32)
                for g in range(NAG):
                    p = g % 2
                    e.wait(v, f"B_h_{g}")
                    e.wait(v, f"B_mbrb_{g}")
                    if g == 0:
                        e.wait(v, f"A_te_done_{NAG-1}")
                    for c in range(8):
                        v.tensor_sub(xh[p][:, AG * c:AG * (c + 1)],
                                     xTbuf[p][:, AG * c:AG * (c + 1)], half(0, 0))
                        v.drain()
                        i2 = v.tensor_mul(xh[p][:, AG * c:AG * (c + 1)],
                                          xh[p][:, AG * c:AG * (c + 1)], half(0, 1))
                        v.drain()
                        e.inc(i2, "sV", 1) if c == 7 else None
                    e.mark(f"B_xh_{g}", "sV")
                    for c in range(8):
                        e.wait(v, f"B_qt_{g}_{c}")
                        i2 = v.tensor_mul(t1b[:, AG * c:AG * (c + 1)], ctbuf[p][:],
                                          qT[:, AG * c:AG * (c + 1)] if False else kT[:, AG * c:AG * (c + 1)])
                        e.wait(v, f"B_jp_{g}_{c}")
                        i2 = v.tensor_mul(t2b[:, AG * c:AG * (c + 1)], stbuf[p][:],
                                          half(0, c % 2))
                        e.inc(i2, "sV", 1)
                        e.mark(f"B_t2_{g}_{c}", "sV")
                    e.mark(f"B_vfree_{g}", "sV")
                    e.wait(v, f"B_fsc_{g}")
                    for b in range(8):
                        e.wait(v, f"B_nmm_{g}_{b}")
                        npsum = K[2] if b % 2 == 0 else K[3]
                        d8 = npsum[:, 64:].rearrange("p (x) -> p x") if False else None
                        din2 = npsum[:].rearrange("p (hh tt o) -> p tt hh o", hh=2, tt=4)[:, :, :, 64:65]
                        t8a = tmp8a[:].rearrange("p (tt hh) -> p tt hh", tt=4)
                        t8b = tmp8b[:].rearrange("p (tt hh) -> p tt hh", tt=4)
                        fslice = fsc[:].rearrange("p (tt h) -> p tt h", tt=4)[:, :, 2 * b:2 * b + 2]
                        v.tensor_copy(t8a, din2.squeeze(3))
                        v.drain()
                        v.tensor_mul(t8b, t8a, fslice)
                        v.drain()
                        v.tensor_scalar_max(tmp8b[:], tmp8b[:], EPS_ATT)
                        v.drain()
                        v.reciprocal(tmp8b[:], tmp8b[:])
                        v.drain()
                        v.tensor_mul(t8a, t8b, fslice)
                        v.drain()
                        nin = npsum[:].rearrange("p (hh tt o) -> p tt hh o", hh=2, tt=4)[:, :, :, 0:64]
                        fbc = tmp8a[:].rearrange("p (tt hh) -> p tt hh", tt=4).unsqueeze(3).broadcast_to((128, 4, 2, 64))
                        oout = ohbuf[:, 0:4096].rearrange(
                            "p (tt hh2 e) -> p tt hh2 e", tt=4, e=64)[:, :, 2 * b:2 * b + 2, :]
                        i2 = v.tensor_mul(oout, nin, fbc)
                        v.drain()
                        e.inc(i2, "sV", 1)
                        e.mark(f"B_div_{g}_{b}", "sV")
                    e.wait(v, f"B_statmm_{g}")
                    v.tensor_scalar(stat4m[:], K[1][:, 0:512].rearrange("p (tt o) -> p tt o", o=128)[:, :, 0:1].squeeze(2),
                                    1.0 / D_MODEL, None, ALU.mult)
                    v.tensor_scalar(stat4s[:], K[1][:, 512:1024].rearrange("p (tt o) -> p tt o", o=128)[:, :, 0:1].squeeze(2),
                                    1.0 / D_MODEL, None, ALU.mult)
                    v.drain()
                    v.tensor_tensor(tmp4[:], stat4m[:], stat4m[:], ALU.mult)
                    v.drain()
                    v.tensor_sub(stat4s[:], stat4s[:], tmp4[:])
                    v.drain()
                    v.tensor_scalar(stat4s[:], stat4s[:], EPS_LN, None, ALU.add)
                    v.drain()
                    i2 = v.reciprocal(rv4[:], stat4s[:])
                    e.inc(i2, "sV", 1)
                    e.mark(f"B_stat_{g}", "sV")
                    e.wait(v, f"B_sqrt_{g}")
                    if g > 0:
                        e.wait(v, f"B_stT_{g-1}")
                    v.tensor_copy(mcol_bf[:, 0:4], stat4m[:])
                    i2 = v.tensor_copy(mcol_bf[:, 4:8], rstd4[:])
                    v.drain()
                    e.inc(i2, "sV", 1)
                    e.mark(f"B_statbf_{g}", "sV")
                    e.wait(v, f"B_mbrb2_{g}")
                    if g == 2 and "x1t" in dbg:
                        e.wait(v, "H_x1dump")
                    for c in range(8):
                        dst = x1T[:, 1024 * c + 512 * (g % 2):1024 * c + 512 * (g % 2) + 512]
                        v.tensor_sub(dst, sqb[:, 512 * c:512 * (c + 1)], half(3, 0))
                        v.drain()
                        v.tensor_mul(dst, dst, half(3, 1))
                        v.drain()
                        i2 = v.tensor_add(dst, dst, xTbuf[p][:, AG * c:AG * (c + 1)])
                        v.drain()
                        e.inc(i2, "sV", 1) if c == 7 else None
                    e.mark(f"B_v_done_{g}", "sV")

                    if "C" not in phases:
                        continue
                    x1off = lambda c: 1024 * c + 512 * (g % 2)
                    vf = vaug[:].bitcast(F32)
                    sqbf32 = sqb[:].bitcast(F32)
                    t2f = t2b[:].bitcast(F32)
                    for c in range(8):
                        i2 = v.tensor_copy(xh[g % 2][:, 512 * c:512 * (c + 1)],
                                           x1T[:, x1off(c):x1off(c) + 512])
                    v.drain()
                    e.inc(i2, "sV", 1)
                    e.mark(f"C_xb_{g}", "sV")
                    e.wait(v, f"C_statmm_{g}")
                    v.tensor_scalar(stat4m[:], K[0][:, 0:512].rearrange("p (tt o) -> p tt o", o=128)[:, :, 0:1].squeeze(2),
                                    1.0 / D_MODEL, None, ALU.mult)
                    v.tensor_scalar(stat4s[:], K[0][:, 512:1024].rearrange("p (tt o) -> p tt o", o=128)[:, :, 0:1].squeeze(2),
                                    1.0 / D_MODEL, None, ALU.mult)
                    v.drain()
                    v.tensor_tensor(tmp4[:], stat4m[:], stat4m[:], ALU.mult)
                    v.drain()
                    v.tensor_sub(stat4s[:], stat4s[:], tmp4[:])
                    v.drain()
                    v.tensor_scalar(stat4s[:], stat4s[:], EPS_LN, None, ALU.add)
                    v.drain()
                    i2 = v.reciprocal(rv4[:], stat4s[:])
                    e.inc(i2, "sV", 1)
                    e.mark(f"C_stat_{g}", "sV")
                    e.wait(v, f"C_sqrt_{g}")
                    v.tensor_copy(mcol_bf[:, 0:4], stat4m[:])
                    i2 = v.tensor_copy(mcol_bf[:, 4:8], rstd4[:])
                    v.drain()
                    e.inc(i2, "sV", 1)
                    e.mark(f"C_statbf_{g}", "sV")
                    e.wait(v, f"C_mbrb3_{g}")
                    for c in range(8):
                        v.tensor_sub(t1b[:, 512 * c:512 * (c + 1)],
                                     x1T[:, x1off(c):x1off(c) + 512], half(0, 0))
                        v.drain()
                        i2 = v.tensor_mul(t1b[:, 512 * c:512 * (c + 1)],
                                          t1b[:, 512 * c:512 * (c + 1)], half(0, 1))
                        v.drain()
                    e.inc(i2, "sV", 1)
                    e.mark(f"C_x2h_{g}", "sV")
                    e.wait(v, f"C_gate_{g}")
                    for tt in range(4):
                        sl = K[1][:, 128 * tt:128 * tt + 8]
                        v.reduce_max(tmp8a[:, 0:1], sl, AX.X)
                        v.drain()
                        v.tensor_scalar(mask12[:, 16 * tt:16 * tt + 8], sl, tmp8a[:, 0:1], 2.0,
                                        ALU.is_equal, ALU.mult)
                        v.drain()
                        v.scalar_tensor_tensor(tmp8b[:], mask12[:, 16 * tt:16 * tt + 8], -5e29,
                                               sl, ALU.mult, ALU.add)
                        v.drain()
                        v.reduce_max(tmp8a[:, 1:2], tmp8b[:], AX.X)
                        v.drain()
                        v.tensor_scalar(mask12[:, 16 * tt + 8:16 * (tt + 1)], tmp8b[:],
                                        tmp8a[:, 1:2], 2.0, ALU.is_equal, ALU.mult)
                        i2 = v.tensor_tensor(tmp8a[:, 4 + tt:5 + tt], tmp8a[:, 0:1],
                                             tmp8a[:, 1:2], ALU.subtract)
                        v.drain()
                    e.inc(i2, "sV", 1)
                    e.mark(f"C_top2m_{g}", "sV")
                    e.wait(v, f"S_sig_{g}")
                    v.tensor_copy(rwb[:].rearrange("p (t two) -> p t two", two=2)[:, :, 0:1].squeeze(2),
                                  fsc[:, 0:4])
                    v.drain()
                    i2 = v.tensor_scalar(rwb[:].rearrange("p (t two) -> p t two", two=2)[:, :, 1:2].squeeze(2),
                                         fsc[:, 0:4], -1.0, 1.0, ALU.mult, ALU.add)
                    v.drain()
                    e.inc(i2, "sV", 1)
                    e.mark(f"C_top2_{g}", "sV")
                    e.wait(v, f"C_mer_{g}")
                    v.tensor_copy(t2b[:, 0:512], half(3, 0))
                    i2 = v.tensor_copy(t2b[:, 512:1024], half(3, 1))
                    v.drain()
                    e.inc(i2, "sV", 1)
                    e.mark(f"C_merc_{g}", "sV")
                    e.wait(v, f"C_wb_{g}")
                    v.tensor_copy(t2f[:, 512:1024], half(3, 0))
                    i2 = v.tensor_copy(t2f[:, 1024:1536], half(3, 1))
                    v.drain()
                    e.inc(i2, "sV", 1)
                    e.mark(f"C_wbc_{g}", "sV")
                    if g > 0:
                        e.wait(v, f"C_y_{g-1}_1")
                    e.wait(v, f"C_u1_{g}")
                    v.tensor_mul(sqb[:, 0:512], half(3, 0), t2b[:, 0:512])
                    i2 = v.tensor_mul(sqb[:, 512:1024], half(3, 0), t2b[:, 512:1024])
                    v.drain()
                    e.inc(i2, "sV", 1)
                    e.mark(f"C_mu1_{g}", "sV")
                    e.wait(v, f"C_u3_{g}")
                    v.tensor_mul(sqb[:, 1024:1536], half(3, 1), t2b[:, 0:512])
                    i2 = v.tensor_mul(sqb[:, 1536:2048], half(3, 1), t2b[:, 512:1024])
                    v.drain()
                    e.inc(i2, "sV", 1)
                    e.mark(f"C_mu3_{g}", "sV")
                    for fc in range(NFC):
                        e.wait(v, f"C_dd_{g}_{fc}")
                        v.tensor_add(vf[:, 1024:1536], half(0, 1), sqbf32[:, 1024:1536])
                        i2 = v.tensor_add(vf[:, 1536:2048], half(1, 1), sqbf32[:, 1536:2048])
                        v.drain()
                        e.inc(i2, "sV", 1)
                        e.mark(f"C_add_{g}_{fc}", "sV")
                        e.wait(v, f"S_silu_{g}_{fc}")
                        v.tensor_mul(sqbf32[:, 1024:1536], vf[:, 0:512], half(1, 0))
                        v.drain()
                        v.tensor_mul(sqbf32[:, 1536:2048], vf[:, 512:1024], vf[:, 1536:2048])
                        v.drain()
                        i2 = v.tensor_mul(t2b[:, 3072:3584], sqbf32[:, 1024:1536], t2f[:, 512:1024])
                        v.drain()
                        e.inc(i2, "sV", 1)
                        e.mark(f"C_w1_{g}_{fc}", "sV")
                        v.tensor_mul(vf[:, 0:512], sqbf32[:, 1536:2048], t2f[:, 1024:1536])
                        v.drain()
                        i2 = v.tensor_add(mixT_ap(g, fc), t2b[:, 3072:3584], vf[:, 0:512])
                        v.drain()
                        e.inc(i2, "sV", 1)
                        e.mark(f"C_mix_{g}_{fc}", "sV")
                    e.wait(v, f"C_z_{g}_{NFC-1}")
                    v.tensor_copy(vf[:, 0:512], half(2, 1))
                    v.drain()
                    v.tensor_sub(vf[:, 0:512], vf[:, 0:512], half(2, 0))
                    v.drain()
                    v.tensor_mul(vf[:, 0:512], vf[:, 0:512], t2b[:, 512:1024])
                    v.drain()
                    v.tensor_mul(vf[:, 512:1024], half(2, 0), t2b[:, 0:512])
                    v.drain()
                    i2 = v.tensor_add(zhat, vf[:, 512:1024], vf[:, 0:512])
                    v.drain()
                    e.inc(i2, "sV", 1)
                    e.mark(f"C_zhat_{g}", "sV")
                    for ps in range(2):
                        e.wait(v, f"C_statmm4_{g}_{ps}")
                        v.tensor_scalar(stat4m[:, 0:2], K[0][:, 0:256].rearrange("p (q o) -> p q o", o=128)[:, :, 0:1].squeeze(2),
                                        1.0 / D_MODEL, None, ALU.mult)
                        v.tensor_scalar(stat4s[:, 0:2], K[0][:, 512:768].rearrange("p (q o) -> p q o", o=128)[:, :, 0:1].squeeze(2),
                                        1.0 / D_MODEL, None, ALU.mult)
                        v.drain()
                        v.tensor_tensor(tmp4[:, 0:2], stat4m[:, 0:2], stat4m[:, 0:2], ALU.mult)
                        v.drain()
                        v.tensor_sub(stat4s[:, 0:2], stat4s[:, 0:2], tmp4[:, 0:2])
                        v.drain()
                        v.tensor_scalar(stat4s[:, 0:2], stat4s[:, 0:2], EPS_LN, None, ALU.add)
                        v.drain()
                        i2 = v.reciprocal(rv4[:, 0:2], stat4s[:, 0:2])
                        e.inc(i2, "sV", 1)
                        e.mark(f"C_stat4_{g}_{ps}", "sV")
                        e.wait(v, f"C_sqrt4_{g}_{ps}")
                        v.tensor_copy(mcol_bf[:, 0:2], stat4m[:, 0:2])
                        i2 = v.tensor_copy(mcol_bf[:, 4:6], rstd4[:, 0:2])
                        v.drain()
                        e.inc(i2, "sV", 1)
                        e.mark(f"C_statbf4_{g}_{ps}", "sV")
                        e.wait(v, f"C_mbrb4_{g}_{ps}")
                        if ps == 1:
                            e.wait(v, f"C_y_{g}_0")
                        for dc in range(8):
                            fT = xh[g % 2][:, 256 * dc:256 * (dc + 1)]
                            mb4 = K[1][:, 0:256]
                            rb4 = K[1][:, 512:768]
                            xt = vf[:, 2048 - 256:2048]
                            v.tensor_sub(xt, fT, mb4)
                            v.drain()
                            v.tensor_mul(xt, xt, rb4)
                            v.drain()
                            x1in = x1T[:, x1off(dc) + 256 * ps:x1off(dc) + 256 * ps + 256]
                            i2 = v.tensor_add(sqbf32[:, 256 * dc:256 * (dc + 1)],
                                              vf[:, 2048 - 256:2048], x1in)
                            v.drain()
                        e.inc(i2, "sV", 1)
                        e.mark(f"C_x2_{g}_{ps}", "sV")
                    e.mark(f"C_done_{g}", "sV")

        d = DryEng()
        for f in (prog_sync, prog_gpsimd, prog_tensor, prog_scalar, prog_vector):
            f(d)

        sems = {n: es.enter_context(nc.semaphore(n)) for n in SEM_NAMES}
        env.begin_real(sems)
        with nc.Block() as block:
            @block.sync
            def _(s):
                prog_sync(s)

            @block.gpsimd
            def _(g_):
                prog_gpsimd(g_)

            @block.tensor
            def _(t):
                prog_tensor(t)

            @block.scalar
            def _(sc):
                prog_scalar(sc)

            @block.vector
            def _(v):
                prog_vector(v)

        env.check_passes_match()
        nc._dbg_env = env

    return nc


def assemble_y(y):
    yb = y.reshape(128, 8, 2, 8, 256)
    xcore = yb.transpose(1, 2, 4, 3, 0).reshape(TCORE, D_MODEL)
    return xcore


def kernel(**inputs):
    in_maps = make_in_maps(inputs)
    nc = build_program()
    res = run_bass_kernel_spmd(nc, in_maps, core_ids=list(range(NCORES)))
    out = np.zeros((C_SEQ, L_SEQ, D_MODEL), dtype=np.float32)
    for i in range(NCORES):
        xcore = assemble_y(res.results[i]["y"])
        out[0, i * SEQ_SLICE:(i + 1) * SEQ_SLICE] = xcore[0:SEQ_SLICE]
        out[1, i * SEQ_SLICE:(i + 1) * SEQ_SLICE] = xcore[SEQ_SLICE:TCORE]
    return out


# version 06 (final)
# speedup vs baseline: 1.0179x; 1.0028x over previous
import math
import numpy as np
import ml_dtypes
from contextlib import ExitStack

import concourse.bass as bass
import concourse.mybir as mybir
from concourse.bass_utils import run_bass_kernel_spmd

F32 = mybir.dt.float32
BF16 = mybir.dt.bfloat16
AF = mybir.ActivationFunctionType
ALU = mybir.AluOpType
AX = mybir.AxisListType

NCORES = 8
C_SEQ = 2
L_SEQ = 16384
D_MODEL = 1024
NH = 16
HD = 64
NFEAT = 256
NR = 4
TCORE = 4096
SEQ_SLICE = 2048
AG = 512
NAG = TCORE // AG
BG = 1024
NBG = TCORE // BG
FFD = 4096
NFC = 32
EPS_ATT = 1e-6
EPS_LN = 1e-5
LN16 = math.log(16.0)
INV3 = 1.0 / 512.0

bfarr = lambda a: np.ascontiguousarray(np.asarray(a), dtype=ml_dtypes.bfloat16)
f32arr = lambda a: np.ascontiguousarray(np.asarray(a), dtype=np.float32)



def _hadamard64():
    h = np.array([[1.0]])
    while h.shape[0] < 64:
        h = np.block([[h, h], [h, -h]])
    return h


def _origidx():
    o = np.zeros(64, dtype=np.int64)
    o[:32] = 2 * np.arange(32)
    o[32:] = 2 * np.arange(32) + 1
    return o


def _rope_tables(core_id):
    t = np.arange(TCORE)
    l = core_id * SEQ_SLICE + (t % SEQ_SLICE)
    pos = np.maximum(l - 1, 0).astype(np.float64)
    j = np.arange(32)
    freq = 10000.0 ** (-2.0 * j / 64.0)
    ang = pos[None, :] * freq[:, None]
    cos, sin = np.cos(ang), np.sin(ang)
    ctab = np.zeros((128, TCORE))
    stab = np.zeros((128, TCORE))
    for hf in range(2):
        b = 64 * hf
        ctab[b:b + 32] = cos
        ctab[b + 32:b + 64] = cos
        stab[b:b + 32] = -sin
        stab[b + 32:b + 64] = sin
    return bfarr(ctab), bfarr(stab)


def _jmat():
    j64 = np.zeros((64, 64))
    j64[np.arange(32), 32 + np.arange(32)] = 1.0
    j64[32 + np.arange(32), np.arange(32)] = 1.0
    out = np.zeros((128, 128))
    out[0:64, 0:64] = j64
    out[64:128, 64:128] = j64
    return bfarr(out)


def _selectors():
    sel = np.zeros((128, 8, 16))
    for c in range(8):
        sel[0:64, c, 2 * c] = 1.0
        sel[64:128, c, 2 * c + 1] = 1.0
    return bfarr(sel.reshape(128, 128))


def _expand8():
    e = np.zeros((8, 128))
    for i in range(8):
        e[i, 16 * i:16 * i + 16] = 1.0
    return bfarr(e)


def make_in_maps(inp):
    x = np.asarray(inp["L0_attn_x"])
    w_qkv = np.asarray(inp["L0_attn_w_qkv"])
    w_out = np.asarray(inp["L0_attn_w_out"])
    signs = np.asarray(inp["L0_attn_sorf_signs"])
    scales = np.asarray(inp["L0_attn_sorf_scales"])
    gate_w = np.asarray(inp["L0_ffn_gate_w"])
    w_gate = np.asarray(inp["L0_ffn_w_gate"])
    w_up = np.asarray(inp["L0_ffn_w_up"])
    w_down = np.asarray(inp["L0_ffn_w_down"])
    A1 = np.asarray(inp["L0_ffn_A1"])
    B1 = np.asarray(inp["L0_ffn_B1"])
    A3 = np.asarray(inp["L0_ffn_A3"])
    B3 = np.asarray(inp["L0_ffn_B3"])
    A2 = np.asarray(inp["L0_ffn_A2"])
    B2 = np.asarray(inp["L0_ffn_B2"])

    orig = _origidx()
    perm = np.concatenate([64 * h + orig for h in range(NH)])
    w_q = w_qkv[0:1024][perm]
    w_k = w_qkv[1024:2048][perm]
    w_v = w_qkv[2048:3072]

    proj_layout = lambda w: f32arr(w.T.reshape(8, 128, w.shape[0]).transpose(1, 0, 2).reshape(128, 8 * w.shape[0]))
    wqT = proj_layout(w_q)
    wkT = proj_layout(w_k)
    wvT = proj_layout(w_v)
    woT = f32arr(w_out.T.reshape(8, 128, 1024).transpose(1, 0, 2).reshape(128, 8192))

    ii = np.arange(64)
    primeof = np.zeros(64, dtype=np.int64)
    primeof[orig] = np.arange(64)
    diag_all = np.zeros((64, NH * NR * 4 * 64), dtype=np.float32)
    for h in range(NH):
        for r in range(NR):
            base = ((h * NR) + r) * 256
            dgm = np.zeros((64, 64), dtype=np.float32)
            dgm[ii, ii] = scales[h, r]
            d2m = np.zeros((64, 64), dtype=np.float32)
            d2m[ii, ii] = signs[h, r, 1]
            d3m = np.zeros((64, 64), dtype=np.float32)
            d3m[ii, ii] = signs[h, r, 2]
            d1ptm = np.zeros((64, 64), dtype=np.float32)
            d1ptm[ii, primeof] = signs[h, r, 0]
            diag_all[:, base:base + 64] = dgm
            diag_all[:, base + 64:base + 128] = d2m
            diag_all[:, base + 128:base + 192] = d3m
            diag_all[:, base + 192:base + 256] = d1ptm

    gate_wT = f32arr(gate_w.T.reshape(8, 128, 8).transpose(1, 0, 2).reshape(128, 64))
    a1catT = f32arr(A1.transpose(2, 0, 1).reshape(8, 128, 128).transpose(1, 0, 2).reshape(128, 1024))
    a3catT = f32arr(A3.transpose(2, 0, 1).reshape(8, 128, 128).transpose(1, 0, 2).reshape(128, 1024))
    b2cat = f32arr(B2.transpose(0, 2, 1).reshape(128, 1024))

    wg_t = w_gate.T.reshape(8, 128, 32, 128)
    wu_t = w_up.T.reshape(8, 128, 32, 128)
    b1l = B1.transpose(0, 2, 1).reshape(128, 32, 128)
    b3l = B3.transpose(0, 2, 1).reshape(128, 32, 128)
    a2l = A2.transpose(2, 0, 1).reshape(32, 128, 128)
    wd_r = w_down.T.reshape(32, 128, 1024)
    stream = np.zeros((32, 128, 3456), dtype=np.float32)
    for fc in range(32):
        stream[fc, :, 0:1024] = wg_t[:, :, fc, :].transpose(1, 0, 2).reshape(128, 1024)
        stream[fc, :, 1024:2048] = wu_t[:, :, fc, :].transpose(1, 0, 2).reshape(128, 1024)
        stream[fc, :, 2048:2176] = b1l[:, fc, :]
        stream[fc, :, 2176:2304] = b3l[:, fc, :]
        stream[fc, :, 2304:2432] = a2l[fc]
        stream[fc, :, 2432:3456] = wd_r[fc]
    stream = f32arr(stream)

    shared = dict(
        wqT=wqT, wkT=wkT, wvT=wvT, woT=woT, sorf_diag_all=diag_all,
        gate_wT=gate_wT, a1catT=a1catT, a3catT=a3catT, b2cat=b2cat,
        ffn_stream=stream,
        hmat=bfarr(_hadamard64()), jmat=_jmat(), sel=_selectors(),
        exp8=_expand8(), ident=bfarr(np.eye(128)),
        onescol=bfarr(np.ones((1, 128))), onescol2=bfarr(np.ones((128, 1))),
    )
    in_maps = []
    for i in range(NCORES):
        xs = np.concatenate([x[0, i * SEQ_SLICE:(i + 1) * SEQ_SLICE],
                             x[1, i * SEQ_SLICE:(i + 1) * SEQ_SLICE]], axis=0)
        ctab, stab = _rope_tables(i)
        m = dict(shared)
        xgT = xs.T.reshape(8, 128, NAG, AG).transpose(1, 2, 0, 3).reshape(128, NAG, 8 * AG)
        m.update(xg=f32arr(xs), xgT=f32arr(xgT), ctab=ctab, stab=stab)
        in_maps.append(m)
    return in_maps



SEM_NAMES = ["sHc", "sHa0", "sHa1", "sHm", "sHf0", "sHf1", "sGc", "sGw", "sGs", "sT", "sV", "sS", "sC", "sO"]


class DummyInstr:
    def then_inc(self, sem, n):
        return self


class DryEng:
    def __getattr__(self, name):
        def f(*a, **k):
            return DummyInstr()
        return f


class Env:
    def __init__(self):
        self.reset()
        self.marks = {}
        self.dry = True
        self.sems = None

    def reset(self):
        self.vals = {s: 0 for s in SEM_NAMES}
        self.log = []

    def begin_real(self, sems):
        self.dry_log = self.log
        self.reset()
        self.dry = False
        self.sems = sems

    def inc(self, instr, sem, n=1):
        self.vals[sem] += n
        self.log.append((sem, n))
        if not self.dry:
            instr.then_inc(self.sems[sem], n)

    def check_passes_match(self):
        assert self.dry_log == self.log, (
            f"dry/real inc divergence at index "
            f"{next(i for i, (a, b) in enumerate(zip(self.dry_log, self.log)) if a != b) if self.dry_log != self.log and len(self.dry_log) == len(self.log) else (len(self.log), len(self.dry_log))}")

    def mark(self, name, sem):
        if self.dry:
            assert name not in self.marks, f"dup mark {name}"
            self.marks[name] = (sem, self.vals[sem])

    def wait(self, eng, name):
        if not self.dry:
            sem, v = self.marks[name]
            eng.wait_ge(self.sems[sem], v)



def build_program(debug=(), phases=("P0", "A", "AR", "B", "C"), alv=7, stub_y=True, nag=NAG, sub=3):
    nc = bass.Bass("TRN2", target_bir_lowering=False)

    xg_d = nc.dram_tensor("xg", [TCORE, D_MODEL], F32, kind="ExternalInput")
    xgT_d = nc.dram_tensor("xgT", [128, NAG, 8 * AG], F32, kind="ExternalInput")
    wqT_d = nc.dram_tensor("wqT", [128, 8192], F32, kind="ExternalInput")
    wkT_d = nc.dram_tensor("wkT", [128, 8192], F32, kind="ExternalInput")
    wvT_d = nc.dram_tensor("wvT", [128, 8192], F32, kind="ExternalInput")
    woT_d = nc.dram_tensor("woT", [128, 8192], F32, kind="ExternalInput")
    diag_d = nc.dram_tensor("sorf_diag_all", [64, NH * NR * 4 * 64], F32, kind="ExternalInput")
    gate_wT_d = nc.dram_tensor("gate_wT", [128, 64], F32, kind="ExternalInput")
    a1catT_d = nc.dram_tensor("a1catT", [128, 1024], F32, kind="ExternalInput")
    a3catT_d = nc.dram_tensor("a3catT", [128, 1024], F32, kind="ExternalInput")
    b2cat_d = nc.dram_tensor("b2cat", [128, 1024], F32, kind="ExternalInput")
    stream_d = nc.dram_tensor("ffn_stream", [NFC, 128, 3456], F32, kind="ExternalInput")
    hmat_d = nc.dram_tensor("hmat", [64, 64], BF16, kind="ExternalInput")
    jmat_d = nc.dram_tensor("jmat", [128, 128], BF16, kind="ExternalInput")
    sel_d = nc.dram_tensor("sel", [128, 128], BF16, kind="ExternalInput")
    exp8_d = nc.dram_tensor("exp8", [8, 128], BF16, kind="ExternalInput")
    ident_d = nc.dram_tensor("ident", [128, 128], BF16, kind="ExternalInput")
    ones_d = nc.dram_tensor("onescol", [1, 128], BF16, kind="ExternalInput")
    ones2_d = nc.dram_tensor("onescol2", [128, 1], BF16, kind="ExternalInput")
    ctab_d = nc.dram_tensor("ctab", [128, TCORE], BF16, kind="ExternalInput")
    stab_d = nc.dram_tensor("stab", [128, TCORE], BF16, kind="ExternalInput")
    y_d = nc.dram_tensor("y", [128, 128, 256], F32, kind="ExternalOutput")
    streambf_d = nc.dram_tensor("streambf", [NFC, 128, 3456], BF16)
    cc_in_d = nc.dram_tensor("cc_in", [128, 4160], F32)
    cc_out_d = nc.dram_tensor("cc_out", [128, 4160], F32)

    dbg = {}
    def dbg_t(name, shape, dtype):
        if name in debug:
            dbg[name] = nc.dram_tensor("dbg_" + name, shape, dtype, kind="ExternalOutput")
    dbg_t("fpt", [128, 2048], BF16)
    dbg_t("sacc", [128, 4160], F32)
    dbg_t("kphi", [128, NH * 4 * 256], BF16)
    dbg_t("x1t", [128, 8192], F32)
    dbg_t("attnt", [128, 4096], BF16)
    dbg_t("fsc", [128, 64], F32)
    dbg_t("ohb", [128, 4096], BF16)
    dbg_t("m2r", [1, 512], BF16)
    dbg_t("r2r", [1, 512], BF16)
    dbg_t("t2b", [128, 4096], BF16)

    es = ExitStack()
    with es:
        SB = lambda n, shp, dt=F32: es.enter_context(nc.sbuf_tensor(n, shp, dt))
        PSUM = lambda n, shp, dt=F32: es.enter_context(nc.psum_tensor(n, shp, dt))

        fpt = SB("fptS", [128, 2048], BF16)
        s_acc = SB("s_acc", [128, 4160], F32)
        s_bf = SB("s_bf", [128, 4160], BF16)
        mrow = SB("mrow", [1, TCORE], BF16)
        rrow = SB("rrow", [1, TCORE], BF16)
        ident = SB("identS", [128, 128], BF16)
        onescol = SB("onescolS", [1, 128], BF16)
        onescol2 = SB("onescol2S", [128, 1], BF16)
        jmat = SB("jmatS", [128, 128], BF16)
        sel = SB("selS", [128, 8 * 16], BF16)

        sorf_tmp = None

        xTbuf = [SB(f"xTbuf{p}", [128, 8 * AG], F32) for p in range(2)]
        ctbuf = [SB(f"ctbuf{p}", [128, AG], BF16) for p in range(2)]
        stbuf = [SB(f"stbuf{p}", [128, AG], BF16) for p in range(2)]
        xh = [SB(f"xh{p}", [128, 8 * AG], BF16) for p in range(2)]
        wkT = SB("wkTs", [128, 8 * 1024], BF16)
        wvT = SB("wvTs", [128, 8 * 1024], BF16)
        kT = SB("kTs", [128, 8 * AG], BF16)
        t1b = SB("t1b", [128, 8 * AG], BF16)
        t2b = SB("t2b", [128, 8 * AG], BF16)
        sqb = SB("sqb", [128, 8 * AG], BF16)
        kphi = SB("kphiS", [128, NH * 4 * 256], BF16)
        diag_all = kphi
        vaug = SB("vaug", [128, 4 * 1040], BF16)
        stat4m = SB("stat4m", [128, 4], F32)
        stat4s = SB("stat4s", [128, 4], F32)
        tmp4 = SB("tmp4", [128, 4], F32)
        rv4 = SB("rv4", [128, 4], F32)
        rstd4 = SB("rstd4", [128, 4], F32)
        mcol_bf = SB("mcol_bf", [128, 8], BF16)
        cvec = SB("cvec", [128, 4], F32)
        fsc = SB("fsc", [128, 64], F32)
        nsb = fsc
        tmp8a = SB("tmp8a", [128, 8], F32)
        a1catTs = SB("a1catTs", [128, 1024], BF16)
        a3catTs = SB("a3catTs", [128, 1024], BF16)

        tmp8b = SB("tmp8b", [128, 8], F32)


        K = [PSUM(f"bankpair{i}", [128, 1024]) for i in range(4)]
        half = lambda i, hh: K[i][:, 512 * hh:512 * (hh + 1)]
        fstage = kT
        sorf_tmp = [kT[0:64, 2048:2112], kT[0:64, 2112:2176]]
        hmat = kT[0:64, 2176:2240]
        b2cats = None
        ohbuf = vaug
        sacc_bf = lambda: s_acc[:].bitcast(BF16)
        oTbuf_ap = lambda: s_acc[:].bitcast(BF16)[:, 0:4096]
        qphib_ap = lambda: s_acc[:].bitcast(BF16)[:, 4096:6144]
        b2cats = s_acc[:].bitcast(BF16)[:, 0:1024]
        gatewT = s_acc[:].bitcast(BF16)[:, 6912:6976]
        exp8s = s_acc[:].bitcast(BF16)[0:8, 6976:7104]
        mask12 = s_acc[:].bitcast(BF16)[:, 7104:7168]
        rwb = s_acc[:].bitcast(BF16)[:, 7168:7176]
        m2row = s_acc[:].bitcast(BF16)[0:1, 7176:7688]
        r2row = s_acc[:].bitcast(BF16)[0:1, 7688:8200]
        zhat = s_acc[:].bitcast(BF16)[:, 4096:4608]
        def mixT_ap(g, fc):
            if fc < 16:
                return xTbuf[g % 2][:].bitcast(BF16)[:, 512 * fc:512 * (fc + 1)]
            if fc < 24:
                return kT[:, 512 * (fc - 16):512 * (fc - 15)]
            return xh[1 - g % 2][:, 512 * (fc - 24):512 * (fc - 23)]

        env = Env()

        def prog_sync(s):
            e = env
            e.inc(s.dma_start(out=ident[:], in_=ident_d.ap()), "sHc", 16)
            e.inc(s.dma_start(out=onescol[:], in_=ones_d.ap()), "sHc", 16)
            e.inc(s.dma_start(out=onescol2[:], in_=ones2_d.ap()), "sHc", 16)
            e.inc(s.dma_start(out=jmat[:], in_=jmat_d.ap()), "sHc", 16)
            e.inc(s.dma_start(out=hmat, in_=hmat_d.ap()), "sHc", 16)
            e.inc(s.dma_start(out=sel[:], in_=sel_d.ap()), "sHc", 16)
            e.mark("H_consts", "sHc")
            if "P0" in phases:
                e.wait(s, "S_fpt_evac")
                e.inc(s.dma_start(out=fpt[64:128, :], in_=fstage[0:64, 0:2048]), "sHm", 16)
                e.mark("H_fpt", "sHm")
            if "A" in phases:
                for g in range(nag):
                    p = g % 2
                    if g >= 2:
                        e.wait(s, f"A_vfree_{g-2}")
                    sp = f"sHa{g % 2}"
                    e.inc(s.dma_start(out=xTbuf[p][:], in_=xgT_d.ap()[:, g]), sp, 16)
                    e.inc(s.dma_start(out=ctbuf[p][:], in_=ctab_d.ap()[:, AG * g:AG * (g + 1)]), sp, 16)
                    e.inc(s.dma_start(out=stbuf[p][:], in_=stab_d.ap()[:, AG * g:AG * (g + 1)]), sp, 16)
                    e.mark(f"A_h_{g}", sp)
            if "AR" in phases:
                e.wait(s, "A_sacc_done")
                e.inc(s.dma_start(out=cc_in_d.ap(), in_=s_acc[:]), "sHm", 16)
                e.mark("H_ccin", "sHm")
            if "B" in phases:
                def bload(g):
                    p = g % 2
                    if g >= 2:
                        e.wait(s, f"C_done_{g-2}" if "C" in phases else f"B_v_done_{g-2}")
                    else:
                        e.wait(s, f"A_vfree_{NAG-2+g}")
                    if g == 3 and "x1t" in dbg:
                        e.inc(s.dma_start(out=dbg["x1t"].ap(), in_=kphi[:].bitcast(F32)), "sO", 16)
                        e.mark("H_x1dump", "sO")
                    sp = f"sHa{p}"
                    e.inc(s.dma_start(out=xTbuf[p][:], in_=xgT_d.ap()[:, g]), sp, 16)
                    e.inc(s.dma_start(out=ctbuf[p][:], in_=ctab_d.ap()[:, AG * g:AG * (g + 1)]), sp, 16)
                    e.inc(s.dma_start(out=stbuf[p][:], in_=stab_d.ap()[:, AG * g:AG * (g + 1)]), sp, 16)
                    e.mark(f"B_h_{g}", sp)
                if "C" not in phases:
                    for g in range(NAG):
                        bload(g)
            if "sacc" in dbg:
                if "AR" in phases:
                    e.wait(s, "G_sbf")
                    e.inc(s.dma_start(out=dbg["sacc"].ap(), in_=cc_out_d.ap()), "sO", 16)
                else:
                    e.wait(s, "A_sacc_done")
                    e.inc(s.dma_start(out=dbg["sacc"].ap(), in_=s_acc[:]), "sO", 16)
            if "fpt" in dbg:
                e.wait(s, "H_fpt")
                e.inc(s.dma_start(out=dbg["fpt"].ap(), in_=fpt[:]), "sO", 16)
            if "kphi" in dbg:
                e.wait(s, "A_sacc_done")
                e.inc(s.dma_start(out=dbg["kphi"].ap(), in_=kphi[:]), "sO", 16)
            if "C" in phases:
                wdbuf = lambda q: s_acc[:].bitcast(BF16)[:, 3456 * q + 2432:3456 * q + 3456]
                bload(0)
                bload(1)
                for g in range(NAG):
                    for ps in range(2):
                        for fc in range(NFC):
                            q = fc % 2
                            sem = "sHf0" if q == 0 else "sHf1"
                            if ps == 0 and fc < 2:
                                e.wait(s, f"C_c1done_{g}")
                            elif ps == 1 and fc < 2:
                                e.wait(s, f"C_dpsum_{g}_0")
                            else:
                                e.wait(s, f"C_wfree_{g}_{ps}_{fc-2}")
                            e.inc(s.dma_start(out=wdbuf(q),
                                              in_=streambf_d.ap()[fc][:, 2432:3456]), sem, 16)
                            e.mark(f"C_wd_{g}_{ps}_{fc}", sem)
                        e.wait(s, f"C_x2_{g}_{ps}")
                        bi = g * 16 + ps * 8
                        e.inc(s.dma_start(
                            out=y_d.ap().rearrange("a b c -> a (b c)")[:, 256 * bi:256 * (bi + 8)],
                            in_=sqb[:].bitcast(F32)[:, 0:2048]), "sO", 16)
                        e.mark(f"C_y_{g}_{ps}", "sO")
                    if g + 2 < NAG:
                        bload(g + 2)
            if "C" not in phases and stub_y:
                e.inc(s.dma_start(out=y_d.ap().rearrange("a b c -> (a b c)"),
                                  in_=xg_d.ap().rearrange("a b -> (a b)")), "sO", 16)
            e.mark("final_out", "sO")
            e.wait(s, "final_out")

        def prog_gpsimd(g_):
            e = env
            if "P0" in phases:
                e.inc(g_.dma_start(out=diag_all[0:64, :], in_=diag_d.ap()), "sGc", 16)
                e.mark("G_diag", "sGc")
            if "A" in phases:
                if alv >= 2:
                    e.inc(g_.dma_start(out=wkT[:], in_=wkT_d.ap()), "sGw", 16)
                    e.inc(g_.dma_start(out=wvT[:], in_=wvT_d.ap()), "sGw", 16)
                e.mark("G_wkv", "sGw")
                if alv >= 6:
                    i2 = g_.memset(vaug[:].rearrange("p (t h c) -> p t h c", t=4, c=65)[:, :, :, 64:65], 1.0)
                    e.inc(i2, "sC", 1)
                else:
                    e.inc(g_.drain(), "sC", 1)
                e.mark("G_vaug_ones", "sC")
            if "C" in phases:
                for fc in range(NFC):
                    e.inc(g_.dma_start(out=streambf_d.ap()[fc], in_=stream_d.ap()[fc]), "sGs", 16)
                e.mark("G_cache", "sGs")
            if "B" in phases:
                e.wait(g_, f"A_te_done_{NAG-1}")
                e.inc(g_.dma_start(out=wkT[:], in_=wqT_d.ap()), "sGw", 16)
                e.inc(g_.dma_start(out=wvT[:], in_=woT_d.ap()), "sGw", 16)
                e.mark("G_wq", "sGw")
            if "AR" in phases:
                e.wait(g_, "H_ccin")
                cc = g_.collective_compute(
                    "AllReduce", ALU.add,
                    replica_groups=[list(range(NCORES))],
                    ins=[cc_in_d.ap().opt()],
                    outs=[cc_out_d.ap().opt()],
                )
                e.inc(cc, "sC", 1)
                e.mark("C_ccdone", "sC")
                e.wait(g_, "C_ccdone")
                e.inc(g_.dma_start(out=s_bf[:], in_=cc_out_d.ap()), "sGs", 16)
                e.mark("G_sbf", "sGs")
            if "AR" not in phases and "B" in phases:
                e.wait(g_, "A_sacc_done")
                e.inc(g_.dma_start(out=s_bf[:], in_=s_acc[:]), "sGs", 16)
                e.mark("G_sbf", "sGs")
            if "C" in phases:
                if "AR" in phases:
                    e.wait(g_, "H_ccin")
                else:
                    e.wait(g_, "G_sbf")
                e.inc(g_.dma_start(out=gatewT, in_=gate_wT_d.ap()), "sGc", 16)
                e.inc(g_.dma_start(out=a1catTs[:], in_=a1catT_d.ap()), "sGc", 16)
                e.inc(g_.dma_start(out=a3catTs[:], in_=a3catT_d.ap()), "sGc", 16)
                e.inc(g_.dma_start(out=exp8s, in_=exp8_d.ap()), "sGc", 16)
                e.mark("G_cconsts", "sGc")
            if "C" in phases:
                bundle = lambda q: s_acc[:].bitcast(BF16)[:, 3456 * q:3456 * (q + 1)]
                for g in range(NAG):
                    for fc in range(NFC):
                        q = fc % 2
                        sem = "sGc" if q == 0 else "sGw"
                        if fc < 2:
                            e.wait(g_, f"B_te_done_{g}")
                            if g == 0:
                                e.wait(g_, "G_cconsts")
                            else:
                                e.wait(g_, f"C_bfree_{g-1}_{NFC-2+fc}")
                        else:
                            e.wait(g_, f"C_bfree_{g}_{fc-2}")
                        e.inc(g_.dma_start(out=bundle(q), in_=streambf_d.ap()[fc]), sem, 16)
                        e.mark(f"C_bun_{g}_{fc}", sem)
                    e.wait(g_, f"C_c1done_{g}")
                    e.inc(g_.dma_start(out=b2cats, in_=b2cat_d.ap()), "sGc", 16)
                    e.mark(f"C_b2_{g}", "sGc")
                e.mark("G_cdone", "sGc")

        def prog_tensor(t):
            e = env
            if "P0" in phases:
                e.wait(t, "H_consts")
                e.wait(t, "G_diag")
                pf = [K[3][0:64, 0:64], K[3][0:64, 512:576]]
                for h in range(NH):
                    for r in range(NR):
                        base = ((h * NR) + r) * 256
                        dgm = diag_all[0:64, base:base + 64]
                        d2m = diag_all[0:64, base + 64:base + 128]
                        d3m = diag_all[0:64, base + 128:base + 192]
                        d1ptm = diag_all[0:64, base + 192:base + 256]
                        lhs_seq = [hmat, d3m, hmat, d2m, hmat, d1ptm]
                        for si in range(6):
                            rhs = dgm if si == 0 else sorf_tmp[(si + 1) % 2]
                            if si >= 1:
                                e.wait(t, f"S_sorf_{h}_{r}_{si-1}_done")
                            elif not (h == 0 and r == 0):
                                ph, pr = (h, r - 1) if r > 0 else (h - 1, NR - 1)
                                e.wait(t, f"S_sorf_{ph}_{pr}_5_done")
                            mm = t.matmul(pf[si % 2], lhs_seq[si], rhs, start=True, stop=True)
                            e.inc(mm, "sT", 1)
                            e.mark(f"T_sorf_{h}_{r}_{si}", "sT")
                e.mark("T_fpt_mm", "sT")

            if "A" in phases:
                if alv >= 2:
                    e.wait(t, "G_wkv")
                if "P0" in phases:
                    e.wait(t, "H_fpt")
                for g in range(nag):
                    p = g % 2
                    e.wait(t, f"A_h_{g}")
                    if g > 0:
                        if alv >= 5:
                            e.wait(t, f"A_sexp_done_{g-1}")
                        elif alv >= 3:
                            e.wait(t, f"A_t2_{g-1}_7")
                        if alv >= 4:
                            e.wait(t, f"A_nsb_{g-1}")
                        else:
                            e.wait(t, f"A_xh_{g-1}")
                    e.wait(t, f"A_xc_{g}")
                    e.wait(t, f"A_sqx_{g}")
                    for tt in range(4):
                        for d in range(8):
                            mm = t.matmul(K[0][:, 128 * tt:128 * tt + 1],
                                          xh[p][:, AG * d + 128 * tt:AG * d + 128 * (tt + 1)],
                                          onescol2[:],
                                          start=(d == 0), stop=(d == 7))
                        for d in range(8):
                            mm = t.matmul(K[0][:, 512 + 128 * tt:512 + 128 * tt + 1],
                                          sqb[:, AG * d + 128 * tt:AG * d + 128 * (tt + 1)],
                                          onescol2[:],
                                          start=(d == 0), stop=(d == 7))
                    e.inc(mm, "sT", 1)
                    e.mark(f"A_statmm_{g}", "sT")
                    e.wait(t, f"A_statbf_{g}")
                    for j in range(8):
                        dst = K[2][0:1, 128 * j:128 * j + 128] if j < 4 else \
                              K[2][0:1, 512 + 128 * (j - 4):512 + 128 * (j - 4) + 128]
                        mm = t.matmul(dst, mcol_bf[:, j:j + 1], ident[:],
                                      start=True, stop=True)
                    e.inc(mm, "sT", 1)
                    e.mark(f"A_stT_{g}", "sT")
                    e.wait(t, f"A_strow_{g}")
                    mm = t.matmul(half(0, 0), onescol[:], mrow[0:1, AG * g:AG * (g + 1)],
                                  start=True, stop=True)
                    mm = t.matmul(half(0, 1), onescol[:], rrow[0:1, AG * g:AG * (g + 1)],
                                  start=True, stop=True)
                    e.inc(mm, "sT", 1)
                    e.mark(f"A_mbrb_{g}", "sT")
                    e.wait(t, f"A_xh_{g}")
                    for c in range(8 if alv >= 2 else 0):
                        if c >= 2:
                            e.wait(t, f"A_kt_{g}_{c-2}")
                        elif g > 0 and c < 2:
                            e.wait(t, f"A_kt_{g-1}_{6+c}")
                        dst = half(1, c % 2)
                        for d in range(8):
                            mm = t.matmul(dst, wkT[:, 1024 * d + 128 * c:1024 * d + 128 * (c + 1)],
                                          xh[p][:, AG * d:AG * (d + 1)],
                                          start=(d == 0), stop=(d == 7))
                        e.inc(mm, "sT", 1)
                        e.mark(f"A_kp_{g}_{c}", "sT")
                    for c in range(8 if alv >= 3 else 0):
                        if c == 0:
                            e.wait(t, f"A_strow_{g}")
                        e.wait(t, f"A_kt_{g}_{c}")
                        if c >= 2:
                            e.wait(t, f"A_t2_{g}_{c-2}")
                        mm = t.matmul(half(2, c % 2), jmat[:], kT[:, AG * c:AG * (c + 1)],
                                      start=True, stop=True)
                        e.inc(mm, "sT", 1)
                        e.mark(f"A_jp_{g}_{c}", "sT")
                    e.wait(t, "G_vaug_ones") if g == 0 else None
                    for tt in range(4 if alv >= 6 else 0):
                        if tt >= 1:
                            e.wait(t, f"A_vaug_{g}_{tt-1}")
                        elif g > 0:
                            e.wait(t, f"A_vaug_{g-1}_3")
                            if alv >= 7:
                                e.wait(t, f"A_spadd_{g-1}_31")
                        for nh2 in range(2):
                            dst = half(3, nh2)
                            for d in range(8):
                                mm = t.matmul(dst,
                                              xh[p][:, AG * d + 128 * tt:AG * d + 128 * (tt + 1)],
                                              wvT[:, 1024 * d + 512 * nh2:1024 * d + 512 * (nh2 + 1)],
                                              start=(d == 0), stop=(d == 7))
                        e.inc(mm, "sT", 1)
                        e.mark(f"A_vp_{g}_{tt}", "sT")
                    e.wait(t, f"A_xh_{g}")
                    if alv >= 4:
                        e.wait(t, f"A_sq_{g}")
                    for tt in range(4 if alv >= 4 else 0):
                        for c in range(8):
                            mm = t.matmul(K[0][:, 128 * tt:128 * tt + 16],
                                          sqb[:, AG * c + 128 * tt:AG * c + 128 * (tt + 1)],
                                          sel[:, 16 * c:16 * (c + 1)],
                                          start=(c == 0), stop=(c == 7))
                        e.inc(mm, "sT", 1)
                    e.mark(f"A_ns_{g}", "sT")
                    if alv >= 5:
                        e.wait(t, f"A_t2_{g}_7")
                    cnt = 0
                    for h in range(NH if alv >= 5 else 0):
                        c, par = h // 2, (h % 2) * 64
                        for tt in range(4):
                            if cnt >= 2:
                                e.wait(t, f"A_expk_{g}_{cnt-2}")
                            dst = half(2, cnt % 2)[:, 0:256]
                            sl = slice(AG * c + 128 * tt, AG * c + 128 * (tt + 1))
                            mm = t.matmul(dst, t1b[par:par + 64, sl],
                                          fpt[par:par + 64, 256 * c:256 * (c + 1)],
                                          start=True, stop=False)
                            mm = t.matmul(dst, t2b[par:par + 64, sl],
                                          fpt[par:par + 64, 256 * c:256 * (c + 1)],
                                          start=False, stop=True)
                            e.inc(mm, "sT", 1)
                            e.mark(f"A_fk_{g}_{cnt}", "sT")
                            cnt += 1
                    if alv >= 7:
                        e.wait(t, f"A_vaug_{g}_3")
                        e.wait(t, f"A_expk_{g}_63")
                    scnt = 0
                    slot_cols = [0, 512, 128, 640]
                    for h in range(NH if alv >= 7 else 0):
                        for m in range(2):
                            if scnt >= 2:
                                e.wait(t, f"A_spadd_{g}_{scnt-2}")
                            dst = K[3][:, slot_cols[scnt % 4]:slot_cols[scnt % 4] + 65]
                            for tt in range(4):
                                kbase = (h * 4 + tt) * 256 + 128 * m
                                mm = t.matmul(dst, kphi[:, kbase:kbase + 128],
                                              vaug[:, 1040 * tt + 65 * h:1040 * tt + 65 * (h + 1)],
                                              start=(tt == 0), stop=(tt == 3))
                            e.inc(mm, "sT", 1)
                            e.mark(f"A_sp_{g}_{scnt}", "sT")
                            scnt += 1
                    e.mark(f"A_te_done_{g}", "sT")

            if "B" in phases:
                qT = kT
                woTb = wvT
                e.wait(t, "G_wq")
                e.wait(t, "G_sbf")
                e.wait(t, f"A_sexp_done_{NAG-1}")
                e.wait(t, f"A_nsb_{NAG-1}")
                for g in range(NAG):
                    p = g % 2
                    cc = g // (NAG // 2)
                    e.wait(t, f"B_h_{g}")
                    if g > 0:
                        e.wait(t, f"B_t2_{g-1}_7")
                    mm = t.matmul(half(0, 0), onescol[:], mrow[0:1, AG * g:AG * (g + 1)],
                                  start=True, stop=True)
                    mm = t.matmul(half(0, 1), onescol[:], rrow[0:1, AG * g:AG * (g + 1)],
                                  start=True, stop=True)
                    e.inc(mm, "sT", 1)
                    e.mark(f"B_mbrb_{g}", "sT")
                    e.wait(t, f"B_xh_{g}")
                    for c in range(8):
                        if c >= 2:
                            e.wait(t, f"B_qt_{g}_{c-2}")
                        elif g > 0:
                            e.wait(t, f"B_qt_{g-1}_{6+c}")
                        dst = half(1, c % 2)
                        for d in range(8):
                            mm = t.matmul(dst, wkT[:, 1024 * d + 128 * c:1024 * d + 128 * (c + 1)],
                                          xh[p][:, AG * d:AG * (d + 1)],
                                          start=(d == 0), stop=(d == 7))
                        e.inc(mm, "sT", 1)
                        e.mark(f"B_qp_{g}_{c}", "sT")
                    for c in range(8):
                        e.wait(t, f"B_qt_{g}_{c}")
                        if c >= 2:
                            e.wait(t, f"B_t2_{g}_{c-2}")
                        mm = t.matmul(half(0, c % 2), jmat[:], qT[:, AG * c:AG * (c + 1)],
                                      start=True, stop=True)
                        e.inc(mm, "sT", 1)
                        e.mark(f"B_jp_{g}_{c}", "sT")
                    e.wait(t, f"B_sq_{g}")
                    if g > 0:
                        e.wait(t, f"B_div_{g-1}_7")
                        e.wait(t, f"B_div_{g-1}_6")
                    for tt in range(4):
                        for c in range(8):
                            mm = t.matmul(K[2][:, 128 * tt:128 * tt + 16],
                                          sqb[:, AG * c + 128 * tt:AG * c + 128 * (tt + 1)],
                                          sel[:, 16 * c:16 * (c + 1)],
                                          start=(c == 0), stop=(c == 7))
                    e.inc(mm, "sT", 1)
                    e.mark(f"B_ns_{g}", "sT")
                    e.wait(t, f"B_t2_{g}_7")
                    if g > 0:
                        e.wait(t, f"B_v_done_{g-1}")
                    for b in range(8):
                        if b >= 2:
                            e.wait(t, f"B_div_{g}_{b-2}")
                        npsum = K[2] if b % 2 == 0 else K[3]
                        for hh in range(2):
                            h = 2 * b + hh
                            c2, par = h // 2, (h % 2) * 64
                            if h >= 2:
                                e.wait(t, f"B_exp_{g}_{h-2}")
                            for m in range(2):
                                dst = half(hh, m) if False else half(0 if hh == 0 else 1, m)
                                sl = slice(AG * c2, AG * (c2 + 1))
                                mm = t.matmul(dst, fpt[par:par + 64, 256 * c2 + 128 * m:256 * c2 + 128 * (m + 1)].tensor[par:par + 64, 256 * c2 + 128 * m:256 * c2 + 128 * (m + 1)] if False else fpt[par:par + 64, 256 * c2 + 128 * m:256 * c2 + 128 * (m + 1)],
                                              t1b[par:par + 64, sl], start=True, stop=False)
                                mm = t.matmul(dst, fpt[par:par + 64, 256 * c2 + 128 * m:256 * c2 + 128 * (m + 1)],
                                              t2b[par:par + 64, sl], start=False, stop=True)
                            e.inc(mm, "sT", 1)
                            e.mark(f"B_ft_{g}_{h}", "sT")
                            e.wait(t, f"B_exp_{g}_{h}")
                            for tt in range(4):
                                dst = npsum[:, 128 * (4 * hh + tt):128 * (4 * hh + tt) + 65]
                                for m in range(2):
                                    off = 65 * ((cc * 16 + h) * 2 + m)
                                    mm = t.matmul(dst,
                                                  qphib_ap()[:, 1024 * (h % 2) + 512 * m + 128 * tt:1024 * (h % 2) + 512 * m + 128 * (tt + 1)],
                                                  s_bf[:, off:off + 65],
                                                  start=(m == 0), stop=(m == 1))
                            e.inc(mm, "sT", 1)
                        e.mark(f"B_nmm_{g}_{b}", "sT")
                    e.wait(t, f"B_div_{g}_7")
                    e.wait(t, f"B_div_{g}_6")
                    tslots = [K[2][:, 0:64].bitcast(BF16), K[2][:, 512:576].bitcast(BF16),
                              K[3][:, 0:64].bitcast(BF16), K[3][:, 512:576].bitcast(BF16)]
                    ti = 0
                    for tt in range(4):
                        for rnd in range(2):
                            if ti >= 1:
                                e.wait(t, f"B_oTev_{g}_{ti-1}")
                            for j in range(4):
                                c = 4 * rnd + j
                                mm = t.transpose(tslots[j][:, 0:128],
                                                 ohbuf[:, 1024 * tt + 128 * c:1024 * tt + 128 * (c + 1)],
                                                 ident[:])
                            e.inc(mm, "sT", 1)
                            e.mark(f"B_oTt_{g}_{ti}", "sT")
                            ti += 1
                    e.wait(t, f"B_oTev_{g}_7")
                    for dc in range(8):
                        if dc >= 2:
                            e.wait(t, f"B_at_{g}_{dc-2}")
                        dst = half(0, dc % 2)
                        for ec in range(8):
                            mm = t.matmul(dst, woTb[:, 1024 * ec + 128 * dc:1024 * ec + 128 * (dc + 1)],
                                          oTbuf_ap()[:, 512 * ec:512 * (ec + 1)],
                                          start=(ec == 0), stop=(ec == 7))
                        e.inc(mm, "sT", 1)
                        e.mark(f"B_pj_{g}_{dc}", "sT")
                    e.wait(t, f"B_atsq_{g}")
                    for tt in range(4):
                        for d in range(8):
                            mm = t.matmul(K[1][:, 128 * tt:128 * tt + 1],
                                          sqb[:, 512 * d + 128 * tt:512 * d + 128 * (tt + 1)],
                                          onescol2[:], start=(d == 0), stop=(d == 7))
                        for d in range(8):
                            mm = t.matmul(K[1][:, 512 + 128 * tt:512 + 128 * tt + 1],
                                          t2b[:, 512 * d + 128 * tt:512 * d + 128 * (tt + 1)],
                                          onescol2[:], start=(d == 0), stop=(d == 7))
                    e.inc(mm, "sT", 1)
                    e.mark(f"B_statmm_{g}", "sT")
                    e.wait(t, f"B_statbf_{g}")
                    for j in range(8):
                        dst = K[2][0:1, 128 * j:128 * j + 128] if j < 4 else \
                              K[2][0:1, 512 + 128 * (j - 4):512 + 128 * (j - 4) + 128]
                        mm = t.matmul(dst, mcol_bf[:, j:j + 1], ident[:], start=True, stop=True)
                    e.inc(mm, "sT", 1)
                    e.mark(f"B_stT_{g}", "sT")
                    e.wait(t, f"B_strow_{g}")
                    mm = t.matmul(half(3, 0), onescol[:], m2row, start=True, stop=True)
                    mm = t.matmul(half(3, 1), onescol[:], r2row, start=True, stop=True)
                    e.inc(mm, "sT", 1)
                    e.mark(f"B_mbrb2_{g}", "sT")
                    e.mark(f"B_te_done_{g}", "sT")

                    if "C" not in phases:
                        continue
                    x1T = kphi[:].bitcast(F32)
                    x1off = lambda c: 1024 * c + 512 * (g % 2)
                    bundle = lambda qq: s_acc[:].bitcast(BF16)[:, 3456 * qq:3456 * (qq + 1)]
                    wdbuf = lambda qq: s_acc[:].bitcast(BF16)[:, 3456 * qq + 2432:3456 * qq + 3456]
                    if g == 0:
                        e.wait(t, "G_cconsts")
                    e.wait(t, f"C_xb_{g}")
                    e.wait(t, f"C_sqx3_{g}")
                    for tt in range(4):
                        for d in range(8):
                            mm = t.matmul(K[0][:, 128 * tt:128 * tt + 1],
                                          xh[g % 2][:, 512 * d + 128 * tt:512 * d + 128 * (tt + 1)],
                                          onescol2[:], start=(d == 0), stop=(d == 7))
                        for d in range(8):
                            mm = t.matmul(K[0][:, 512 + 128 * tt:512 + 128 * tt + 1],
                                          t2b[:, 512 * d + 128 * tt:512 * d + 128 * (tt + 1)],
                                          onescol2[:], start=(d == 0), stop=(d == 7))
                    e.inc(mm, "sT", 1)
                    e.mark(f"C_statmm_{g}", "sT")
                    e.wait(t, f"C_statbf_{g}")
                    for j in range(8):
                        dst = K[2][0:1, 128 * j:128 * j + 128] if j < 4 else \
                              K[2][0:1, 512 + 128 * (j - 4):512 + 128 * (j - 4) + 128]
                        mm = t.matmul(dst, mcol_bf[:, j:j + 1], ident[:], start=True, stop=True)
                    e.inc(mm, "sT", 1)
                    e.mark(f"C_stT_{g}", "sT")
                    e.wait(t, f"C_strow_{g}")
                    mm = t.matmul(half(0, 0), onescol[:], m2row, start=True, stop=True)
                    mm = t.matmul(half(0, 1), onescol[:], r2row, start=True, stop=True)
                    e.inc(mm, "sT", 1)
                    e.mark(f"C_mbrb3_{g}", "sT")
                    e.wait(t, f"C_x2h_{g}")
                    for tt in range(4):
                        for c in range(8):
                            mm = t.matmul(K[1][:, 128 * tt:128 * tt + 8],
                                          t1b[:, 512 * c + 128 * tt:512 * c + 128 * (tt + 1)],
                                          gatewT[:, 8 * c:8 * (c + 1)],
                                          start=(c == 0), stop=(c == 7))
                    e.inc(mm, "sT", 1)
                    e.mark(f"C_gate_{g}", "sT")
                    e.wait(t, f"C_top2_{g}")
                    for tt in range(4):
                        mm = t.matmul(K[2][0:8, 128 * tt:128 * (tt + 1)],
                                      mask12[:, 16 * tt:16 * tt + 8], ident[:],
                                      start=True, stop=True)
                        mm = t.matmul(K[2][0:8, 512 + 128 * tt:512 + 128 * (tt + 1)],
                                      mask12[:, 16 * tt + 8:16 * (tt + 1)], ident[:],
                                      start=True, stop=True)
                        mm = t.matmul(K[3][0:1, 128 * tt:128 * (tt + 1)],
                                      rwb[:, 2 * tt:2 * tt + 1], ident[:],
                                      start=True, stop=True)
                        mm = t.matmul(K[3][0:1, 512 + 128 * tt:512 + 128 * (tt + 1)],
                                      rwb[:, 2 * tt + 1:2 * tt + 2], ident[:],
                                      start=True, stop=True)
                    e.inc(mm, "sT", 1)
                    e.mark(f"C_maskT_{g}", "sT")
                    e.wait(t, f"C_mrows_{g}")
                    mm = t.matmul(half(3, 0), exp8s, ctbuf[g % 2][0:8, :], start=True, stop=True)
                    mm = t.matmul(half(3, 1), exp8s, stbuf[g % 2][0:8, :], start=True, stop=True)
                    e.inc(mm, "sT", 1)
                    e.mark(f"C_mer_{g}", "sT")
                    e.wait(t, f"C_merc_{g}")
                    mm = t.matmul(half(3, 0), onescol[:], m2row, start=True, stop=True)
                    mm = t.matmul(half(3, 1), onescol[:], r2row, start=True, stop=True)
                    e.inc(mm, "sT", 1)
                    e.mark(f"C_wb_{g}", "sT")
                    e.wait(t, f"C_wbc_{g}")
                    for d in range(8):
                        mm = t.matmul(half(3, 0), a1catTs[:, 128 * d:128 * (d + 1)],
                                      t1b[:, 512 * d:512 * (d + 1)],
                                      start=(d == 0), stop=(d == 7))
                    e.inc(mm, "sT", 1)
                    e.mark(f"C_u1_{g}", "sT")
                    for d in range(8):
                        mm = t.matmul(half(3, 1), a3catTs[:, 128 * d:128 * (d + 1)],
                                      t1b[:, 512 * d:512 * (d + 1)],
                                      start=(d == 0), stop=(d == 7))
                    e.inc(mm, "sT", 1)
                    e.mark(f"C_u3_{g}", "sT")
                    for fc in range(NFC):
                        qq = fc % 2
                        e.wait(t, f"C_bun_{g}_{fc}")
                        if fc == 0:
                            e.wait(t, f"C_mu3_{g}")
                        if fc > 0:
                            e.wait(t, f"S_silu1_{g}_{fc-1}")
                            e.wait(t, f"C_k1f_{g}_{fc-1}")
                        for d in range(8):
                            mm = t.matmul(half(0, 0), bundle(qq)[:, 128 * d:128 * (d + 1)],
                                          t1b[:, 512 * d:512 * (d + 1)],
                                          start=(d == 0), stop=(d == 7))
                        for d in range(8):
                            mm = t.matmul(half(1, 0), bundle(qq)[:, 1024 + 128 * d:1024 + 128 * (d + 1)],
                                          t1b[:, 512 * d:512 * (d + 1)],
                                          start=(d == 0), stop=(d == 7))
                        e.inc(mm, "sT", 1)
                        e.mark(f"C_cc_{g}_{fc}", "sT")
                        e.wait(t, f"S_cs_{g}_{fc}")
                        mm = t.matmul(half(0, 0), bundle(qq)[:, 2048:2176],
                                      sqb[:, 0:512], start=False, stop=True,
                                      skip_group_check=True)
                        mm = t.matmul(half(1, 0), bundle(qq)[:, 2176:2304],
                                      sqb[:, 1024:1536], start=False, stop=True,
                                      skip_group_check=True)
                        mm = t.matmul(half(0, 1), bundle(qq)[:, 2048:2176],
                                      sqb[:, 512:1024], start=True, stop=True)
                        mm = t.matmul(half(1, 1), bundle(qq)[:, 2176:2304],
                                      sqb[:, 1536:2048], start=True, stop=True)
                        e.inc(mm, "sT", 1)
                        e.mark(f"C_dd_{g}_{fc}", "sT")
                        e.wait(t, f"C_w1_{g}_{fc}")
                        mm = t.matmul(half(2, 0), bundle(qq)[:, 2304:2432],
                                      t2b[:, 3072:3584],
                                      start=(fc == 0), stop=(fc == NFC - 1))
                        e.wait(t, f"C_mix_{g}_{fc}")
                        mm = t.matmul(half(2, 1), bundle(qq)[:, 2304:2432],
                                      mixT_ap(g, fc), start=(fc == 0), stop=(fc == NFC - 1))
                        e.inc(mm, "sT", 1)
                        e.mark(f"C_z_{g}_{fc}", "sT")
                        e.mark(f"C_bfree_{g}_{fc}", "sT")
                    e.mark(f"C_c1done_{g}", "sT")
                    for ps in range(2):
                        for fc in range(NFC):
                            qq = fc % 2
                            e.wait(t, f"C_wd_{g}_{ps}_{fc}")
                            if ps == 0 and fc == 0:
                                e.wait(t, f"C_zhat_{g}")
                            if ps == 1 and fc == 0:
                                e.wait(t, f"C_ffnt_{g}_0")
                            for dc in range(8):
                                dst = (K[2] if dc < 4 else K[3])[:, 256 * (dc % 4):256 * (dc % 4 + 1)]
                                mm = t.matmul(dst, wdbuf(qq)[:, 128 * dc:128 * (dc + 1)],
                                              mixT_ap(g, fc)[:, 256 * ps:256 * (ps + 1)],
                                              start=(fc == 0 and dc % 2 == 0), stop=True,
                                              skip_group_check=True)
                            e.inc(mm, "sT", 1)
                            e.mark(f"C_wfree_{g}_{ps}_{fc}", "sT")
                        if ps == 0:
                            e.wait(t, f"C_b2_{g}")
                        for dc in range(8):
                            dst = (K[2] if dc < 4 else K[3])[:, 256 * (dc % 4):256 * (dc % 4 + 1)]
                            mm = t.matmul(dst, b2cats[:, 128 * dc:128 * (dc + 1)],
                                          zhat[:, 256 * ps:256 * (ps + 1)],
                                          start=False, stop=True, skip_group_check=True)
                        e.inc(mm, "sT", 1)
                        e.mark(f"C_dpsum_{g}_{ps}", "sT")
                        e.wait(t, f"C_ffnt_{g}_{ps}")
                        for q in range(2):
                            for dc in range(8):
                                mm = t.matmul(K[0][:, 128 * q:128 * q + 1],
                                              xh[g % 2][:, 256 * dc + 128 * q:256 * dc + 128 * q + 128],
                                              onescol2[:], start=(dc == 0), stop=(dc == 7))
                            for dc in range(8):
                                mm = t.matmul(K[0][:, 512 + 128 * q:512 + 128 * q + 1],
                                              vaug[:, 256 * dc + 128 * q:256 * dc + 128 * q + 128],
                                              onescol2[:], start=(dc == 0), stop=(dc == 7))
                        e.inc(mm, "sT", 1)
                        e.mark(f"C_statmm4_{g}_{ps}", "sT")
                        e.wait(t, f"C_statbf4_{g}_{ps}")
                        for (j, col) in [(0, 0), (1, 128), (4, 512), (5, 640)]:
                            mm = t.matmul(K[1][0:1, col:col + 128], mcol_bf[:, j:j + 1],
                                          ident[:], start=True, stop=True)
                        e.inc(mm, "sT", 1)
                        e.mark(f"C_stT4_{g}_{ps}", "sT")
                        e.wait(t, f"C_strow4_{g}_{ps}")
                        mm = t.matmul(K[1][:, 0:256], onescol[:], m2row[0:1, 0:256],
                                      start=True, stop=True)
                        mm = t.matmul(K[1][:, 512:768], onescol[:], r2row[0:1, 0:256],
                                      start=True, stop=True)
                        e.inc(mm, "sT", 1)
                        e.mark(f"C_mbrb4_{g}_{ps}", "sT")
                    e.mark(f"C_te_done_{g}", "sT")

        def prog_scalar(sc):
            e = env
            if "P0" in phases:
                pf = [K[3][0:64, 0:64], K[3][0:64, 512:576]]
                for h in range(NH):
                    for r in range(NR):
                        for si in range(6):
                            e.wait(sc, f"T_sorf_{h}_{r}_{si}")
                            src = pf[si % 2]
                            if si < 5:
                                i2 = sc.activation(sorf_tmp[si % 2], src, AF.Identity,
                                                   bias=0.0, scale=1.0)
                            else:
                                if h % 2 == 0:
                                    dst = fpt[0:64, 256 * (h // 2) + 64 * r:256 * (h // 2) + 64 * (r + 1)]
                                else:
                                    dst = fstage[0:64, 256 * (h // 2) + 64 * r:256 * (h // 2) + 64 * (r + 1)]
                                i2 = sc.activation(dst, src, AF.Identity, bias=0.0, scale=INV3)
                            e.inc(i2, "sS", 1)
                            e.mark(f"S_sorf_{h}_{r}_{si}_done", "sS")
                e.mark("S_fpt_evac", "sS")

            if "A" in phases:
                for g in range(nag):
                    e.wait(sc, f"A_xc_{g}")
                    if g > 0:
                        e.wait(sc, f"A_ns_{g-1}")
                    if sub & 8:
                        i2 = sc.drain()
                    else:
                        i2 = sc.activation(sqb[:], xh[g % 2][:], AF.Square)
                    e.inc(i2, "sS", 1)
                    e.mark(f"A_sqx_{g}", "sS")
                    e.wait(sc, f"A_stat_{g}")
                    i2 = sc.activation(rstd4[:], rv4[:], AF.Sqrt)
                    e.inc(i2, "sS", 1)
                    e.mark(f"A_sqrt_{g}", "sS")
                    e.wait(sc, f"A_stT_{g}")
                    col = 512 * g
                    i2 = sc.activation(mrow[0:1, col:col + 512], K[2][0:1, 0:512],
                                       AF.Identity, bias=0.0, scale=1.0)
                    i2 = sc.activation(rrow[0:1, col:col + 512], K[2][0:1, 512:1024],
                                       AF.Identity, bias=0.0, scale=1.0)
                    e.inc(i2, "sS", 1)
                    e.mark(f"A_strow_{g}", "sS")
                    for c in range(8 if alv >= 2 else 0):
                        e.wait(sc, f"A_kp_{g}_{c}")
                        if c == 0:
                            e.wait(sc, f"A_statmm_{g}")
                        if g > 0 and c == 0:
                            e.wait(sc, f"A_te_done_{g-1}")
                        i2 = sc.copy(kT[:, AG * c:AG * (c + 1)], half(1, c % 2))
                        e.inc(i2, "sS", 1)
                        e.mark(f"A_kt_{g}_{c}", "sS")
                        sc.drain()
                        i2 = sc.activation(sqb[:, AG * c:AG * (c + 1)],
                                           kT[:, AG * c:AG * (c + 1)], AF.Square)
                        e.inc(i2, "sS", 1)
                    e.mark(f"A_sq_{g}", "sS")
                    for tt in range(4 if alv >= 6 else 0):
                        e.wait(sc, f"A_vp_{g}_{tt}")
                        base = 1040 * tt
                        dst = vaug[:, base:base + 1040].rearrange("p (h c) -> p h c", c=65)
                        i2 = sc.copy(dst[:, 0:8, 0:64],
                                     half(3, 0).rearrange("p (h c) -> p h c", c=64))
                        i2 = sc.copy(dst[:, 8:16, 0:64],
                                     half(3, 1).rearrange("p (h c) -> p h c", c=64))
                        e.inc(i2, "sS", 1)
                        e.mark(f"A_vaug_{g}_{tt}", "sS")
                    if alv >= 4:
                        e.wait(sc, f"A_ns_{g}")
                        e.wait(sc, "V_cvec")
                    for tt in range(4 if alv >= 4 else 0):
                        i2 = sc.activation(nsb[:, 16 * tt:16 * (tt + 1)],
                                           K[0][:, 128 * tt:128 * tt + 16],
                                           AF.Identity, bias=cvec[:, 0:1], scale=-0.5)
                        e.inc(i2, "sS", 1)
                    sc.drain()
                    e.mark(f"A_nsb_{g}", "sS")
                    cnt = 0
                    for h in range(NH if alv >= 5 else 0):
                        for tt in range(4):
                            e.wait(sc, f"A_fk_{g}_{cnt}")
                            i2 = sc.activation(kphi[:, (h * 4 + tt) * 256:(h * 4 + tt + 1) * 256],
                                               half(2, cnt % 2)[:, 0:256], AF.Exp,
                                               bias=nsb[:, 16 * tt + h:16 * tt + h + 1],
                                               scale=1.0)
                            e.inc(i2, "sS", 1)
                            e.mark(f"A_expk_{g}_{cnt}", "sS")
                            cnt += 1
                    e.mark(f"A_sexp_done_{g}", "sS")

            if "B" in phases:
                qT = kT
                for g in range(NAG):
                    p = g % 2
                    for c in range(8):
                        e.wait(sc, f"B_qp_{g}_{c}")
                        if g == 0 and c == 0:
                            e.wait(sc, f"A_te_done_{NAG-1}")
                        elif g > 0 and c == 0:
                            e.wait(sc, f"B_te_done_{g-1}")
                        if g > 0 and c == 0 and "C" in phases:
                            e.wait(sc, f"C_y_{g-1}_1")
                        if g == 1 and c == 0 and "attnt" in dbg:
                            e.wait(sc, "H_atdump")
                        i2 = sc.copy(qT[:, AG * c:AG * (c + 1)], half(1, c % 2))
                        e.inc(i2, "sS", 1)
                        e.mark(f"B_qt_{g}_{c}", "sS")
                        sc.drain()
                        i2 = sc.activation(sqb[:, AG * c:AG * (c + 1)],
                                           qT[:, AG * c:AG * (c + 1)], AF.Square)
                        e.inc(i2, "sS", 1)
                    e.mark(f"B_sq_{g}", "sS")
                    e.wait(sc, f"B_ns_{g}")
                    e.wait(sc, "V_cvec")
                    for tt in range(4):
                        i2 = sc.activation(fsc[:, 16 * tt:16 * (tt + 1)],
                                           K[2][:, 128 * tt:128 * tt + 16],
                                           AF.Exp, bias=cvec[:, 0:1], scale=-0.5)
                        e.inc(i2, "sS", 1)
                    sc.drain()
                    e.mark(f"B_fsc_{g}", "sS")
                    for h in range(NH):
                        e.wait(sc, f"B_ft_{g}_{h}")
                        hh = h % 2
                        base = 1024 * (h % 2)
                        for m in range(2):
                            i2 = sc.activation(qphib_ap()[:, base + 512 * m:base + 512 * (m + 1)],
                                               half(0 if hh == 0 else 1, m), AF.Exp)
                        e.inc(i2, "sS", 1)
                        e.mark(f"B_exp_{g}_{h}", "sS")
                    tslots = [K[2][:, 0:64].bitcast(BF16), K[2][:, 512:576].bitcast(BF16),
                              K[3][:, 0:64].bitcast(BF16), K[3][:, 512:576].bitcast(BF16)]
                    ti = 0
                    for tt in range(4):
                        for rnd in range(2):
                            e.wait(sc, f"B_oTt_{g}_{ti}")
                            for j in range(4):
                                c = 4 * rnd + j
                                i2 = sc.copy(oTbuf_ap()[:, 512 * c + 128 * tt:512 * c + 128 * (tt + 1)],
                                             tslots[j][:, 0:128])
                            e.inc(i2, "sS", 1)
                            e.mark(f"B_oTev_{g}_{ti}", "sS")
                            ti += 1
                    for dc in range(8):
                        e.wait(sc, f"B_pj_{g}_{dc}")
                        i2 = sc.copy(sqb[:, 512 * dc:512 * (dc + 1)], half(0, dc % 2))
                        e.inc(i2, "sS", 1)
                        e.mark(f"B_at_{g}_{dc}", "sS")
                    sc.drain()
                    i2 = sc.activation(t2b[:], sqb[:], AF.Square)
                    e.inc(i2, "sS", 1)
                    e.mark(f"B_atsq_{g}", "sS")
                    e.wait(sc, f"B_stat_{g}")
                    i2 = sc.activation(rstd4[:], rv4[:], AF.Sqrt)
                    e.inc(i2, "sS", 1)
                    e.mark(f"B_sqrt_{g}", "sS")
                    e.wait(sc, f"B_stT_{g}")
                    i2 = sc.activation(m2row, K[2][0:1, 0:512],
                                       AF.Identity, bias=0.0, scale=1.0)
                    i2 = sc.activation(r2row, K[2][0:1, 512:1024],
                                       AF.Identity, bias=0.0, scale=1.0)
                    e.inc(i2, "sS", 1)
                    e.mark(f"B_strow_{g}", "sS")

                    if "C" not in phases:
                        continue
                    sqbf32 = sqb[:].bitcast(F32)
                    e.wait(sc, f"C_xb_{g}")
                    i2 = sc.activation(t2b[:], xh[g % 2][:], AF.Square)
                    e.inc(i2, "sS", 1)
                    e.mark(f"C_sqx3_{g}", "sS")
                    e.wait(sc, f"C_stat_{g}")
                    i2 = sc.activation(rstd4[:], rv4[:], AF.Sqrt)
                    e.inc(i2, "sS", 1)
                    e.mark(f"C_sqrt_{g}", "sS")
                    e.wait(sc, f"C_stT_{g}")
                    i2 = sc.activation(m2row, K[2][0:1, 0:512], AF.Identity,
                                       bias=0.0, scale=1.0)
                    i2 = sc.activation(r2row, K[2][0:1, 512:1024], AF.Identity,
                                       bias=0.0, scale=1.0)
                    e.inc(i2, "sS", 1)
                    e.mark(f"C_strow_{g}", "sS")
                    e.wait(sc, f"C_top2m_{g}")
                    i2 = sc.activation(fsc[:, 0:4], tmp8a[:, 4:8], AF.Sigmoid)
                    e.inc(i2, "sS", 1)
                    e.mark(f"S_sig_{g}", "sS")
                    e.wait(sc, f"C_maskT_{g}")
                    i2 = sc.copy(ctbuf[g % 2][0:8, :], K[2][0:8, 0:512])
                    i2 = sc.copy(stbuf[g % 2][0:8, :], K[2][0:8, 512:1024])
                    i2 = sc.activation(m2row, K[3][0:1, 0:512], AF.Identity,
                                       bias=0.0, scale=1.0)
                    i2 = sc.activation(r2row, K[3][0:1, 512:1024], AF.Identity,
                                       bias=0.0, scale=1.0)
                    e.inc(i2, "sS", 1)
                    e.mark(f"C_mrows_{g}", "sS")
                    for fc in range(NFC):
                        e.wait(sc, f"C_cc_{g}_{fc}")
                        i2 = sc.copy(sqbf32[:, 1024:1536], half(0, 0))
                        i2 = sc.copy(sqbf32[:, 1536:2048], half(1, 0))
                        e.inc(i2, "sS", 1)
                        e.mark(f"S_cs_{g}_{fc}", "sS")
                        e.wait(sc, f"C_dd_{g}_{fc}")
                        i2 = sc.activation(vaug[:].bitcast(F32)[:, 0:512], half(0, 0), AF.Silu)
                        e.inc(i2, "sS", 1)
                        e.mark(f"S_silu1_{g}_{fc}", "sS")
                        e.wait(sc, f"C_add_{g}_{fc}")
                        i2 = sc.activation(vaug[:].bitcast(F32)[:, 512:1024],
                                           vaug[:].bitcast(F32)[:, 1024:1536], AF.Silu)
                        e.inc(i2, "sS", 1)
                        e.mark(f"S_silu_{g}_{fc}", "sS")
                    for ps in range(2):
                        e.wait(sc, f"C_dpsum_{g}_{ps}")
                        i2 = sc.copy(xh[g % 2][:, 0:1024], K[2][:])
                        i2 = sc.copy(xh[g % 2][:, 1024:2048], K[3][:])
                        sc.drain()
                        i2 = sc.activation(vaug[:, 0:2048], xh[g % 2][:, 0:2048], AF.Square)
                        e.inc(i2, "sS", 1)
                        e.mark(f"C_ffnt_{g}_{ps}", "sS")
                        e.wait(sc, f"C_stat4_{g}_{ps}")
                        i2 = sc.activation(rstd4[:, 0:2], rv4[:, 0:2], AF.Sqrt)
                        e.inc(i2, "sS", 1)
                        e.mark(f"C_sqrt4_{g}_{ps}", "sS")
                        e.wait(sc, f"C_stT4_{g}_{ps}")
                        i2 = sc.activation(m2row[0:1, 0:256], K[1][0:1, 0:256], AF.Identity,
                                           bias=0.0, scale=1.0)
                        i2 = sc.activation(r2row[0:1, 0:256], K[1][0:1, 512:768], AF.Identity,
                                           bias=0.0, scale=1.0)
                        e.inc(i2, "sS", 1)
                        e.mark(f"C_strow4_{g}_{ps}", "sS")

        def prog_vector(v):
            e = env
            if "A" in phases:
                v.memset(s_acc[:], 0.0)
                i2 = v.memset(cvec[:, 0:1], -LN16)
                v.drain()
                e.inc(i2, "sV", 1)
                e.mark("V_cvec", "sV")
                for g in range(nag):
                    p = g % 2
                    e.wait(v, f"A_h_{g}")
                    if g > 0:
                        if alv >= 4:
                            e.wait(v, f"A_ns_{g-1}")
                        if alv >= 2:
                            e.wait(v, f"A_kp_{g-1}_7")
                        if alv >= 6:
                            e.wait(v, f"A_vp_{g-1}_3")
                        e.wait(v, f"A_statmm_{g-1}")
                    i2 = v.tensor_copy(xh[p][:], xTbuf[p][:])
                    v.drain()
                    e.inc(i2, "sV", 1)
                    e.mark(f"A_xc_{g}", "sV")
                    e.wait(v, f"A_statmm_{g}")
                    v.tensor_scalar(stat4m[:], K[0][:, 0:512].rearrange("p (tt o) -> p tt o", o=128)[:, :, 0:1].squeeze(2),
                                    1.0 / D_MODEL, None, ALU.mult)
                    v.tensor_scalar(stat4s[:], K[0][:, 512:1024].rearrange("p (tt o) -> p tt o", o=128)[:, :, 0:1].squeeze(2),
                                    1.0 / D_MODEL, None, ALU.mult)
                    v.drain()
                    v.tensor_tensor(tmp4[:], stat4m[:], stat4m[:], ALU.mult)
                    v.drain()
                    v.tensor_sub(stat4s[:], stat4s[:], tmp4[:])
                    v.drain()
                    v.tensor_scalar(stat4s[:], stat4s[:], EPS_LN, None, ALU.add)
                    v.drain()
                    i2 = v.reciprocal(rv4[:], stat4s[:])
                    e.inc(i2, "sV", 1)
                    e.mark(f"A_stat_{g}", "sV")
                    e.wait(v, f"A_sqrt_{g}")
                    if g > 0:
                        e.wait(v, f"A_stT_{g-1}")
                    v.tensor_copy(mcol_bf[:, 0:4], stat4m[:])
                    i2 = v.tensor_copy(mcol_bf[:, 4:8], rstd4[:])
                    v.drain()
                    e.inc(i2, "sV", 1)
                    e.mark(f"A_statbf_{g}", "sV")
                    e.wait(v, f"A_mbrb_{g}")
                    pass
                    for c in range(8 if (sub & 2) else 0):
                        v.tensor_sub(xTbuf[p][:, AG * c:AG * (c + 1)],
                                     xTbuf[p][:, AG * c:AG * (c + 1)], half(0, 0))
                        v.drain()
                        i2 = v.tensor_mul(xh[p][:, AG * c:AG * (c + 1)],
                                          xTbuf[p][:, AG * c:AG * (c + 1)], half(0, 1))
                        e.inc(i2, "sV", 1) if c == 7 else None
                    if not (sub & 2):
                        i2 = v.drain()
                        e.inc(i2, "sV", 1)
                    e.mark(f"A_xh_{g}", "sV")
                    for c in range(8 if alv >= 3 else 0):
                        e.wait(v, f"A_kt_{g}_{c}")
                        if g > 0 and c == 0:
                            e.wait(v, f"A_te_done_{g-1}")
                        i2 = v.tensor_mul(t1b[:, AG * c:AG * (c + 1)], ctbuf[p][:],
                                          kT[:, AG * c:AG * (c + 1)])
                        e.wait(v, f"A_jp_{g}_{c}")
                        i2 = v.tensor_mul(t2b[:, AG * c:AG * (c + 1)], stbuf[p][:],
                                          half(2, c % 2))
                        e.inc(i2, "sV", 1)
                        e.mark(f"A_t2_{g}_{c}", "sV")
                    e.mark(f"A_vfree_{g}", "sV")
                    if alv < 3:
                        pass
                    slot_cols = [0, 512, 128, 640]
                    scnt = 0
                    for h in range(NH if alv >= 7 else 0):
                        for m in range(2):
                            e.wait(v, f"A_sp_{g}_{scnt}")
                            cc = (g // (NAG // 2))
                            off = 65 * ((cc * 16 + h) * 2 + m)
                            i2 = v.tensor_add(s_acc[:, off:off + 65], s_acc[:, off:off + 65],
                                              K[3][:, slot_cols[scnt % 4]:slot_cols[scnt % 4] + 65])
                            e.inc(i2, "sV", 1)
                            e.mark(f"A_spadd_{g}_{scnt}", "sV")
                            scnt += 1
                    e.mark(f"A_v_done_{g}", "sV")
                e.mark("A_sacc_done", "sV")

            if "B" in phases:
                x1T = kphi[:].bitcast(F32)
                for g in range(NAG):
                    p = g % 2
                    e.wait(v, f"B_h_{g}")
                    e.wait(v, f"B_mbrb_{g}")
                    if g == 0:
                        e.wait(v, f"A_te_done_{NAG-1}")
                    for c in range(8):
                        v.tensor_sub(xh[p][:, AG * c:AG * (c + 1)],
                                     xTbuf[p][:, AG * c:AG * (c + 1)], half(0, 0))
                        v.drain()
                        i2 = v.tensor_mul(xh[p][:, AG * c:AG * (c + 1)],
                                          xh[p][:, AG * c:AG * (c + 1)], half(0, 1))
                        v.drain()
                        e.inc(i2, "sV", 1) if c == 7 else None
                    e.mark(f"B_xh_{g}", "sV")
                    for c in range(8):
                        e.wait(v, f"B_qt_{g}_{c}")
                        i2 = v.tensor_mul(t1b[:, AG * c:AG * (c + 1)], ctbuf[p][:],
                                          qT[:, AG * c:AG * (c + 1)] if False else kT[:, AG * c:AG * (c + 1)])
                        e.wait(v, f"B_jp_{g}_{c}")
                        i2 = v.tensor_mul(t2b[:, AG * c:AG * (c + 1)], stbuf[p][:],
                                          half(0, c % 2))
                        e.inc(i2, "sV", 1)
                        e.mark(f"B_t2_{g}_{c}", "sV")
                    e.mark(f"B_vfree_{g}", "sV")
                    e.wait(v, f"B_fsc_{g}")
                    for b in range(8):
                        e.wait(v, f"B_nmm_{g}_{b}")
                        npsum = K[2] if b % 2 == 0 else K[3]
                        d8 = npsum[:, 64:].rearrange("p (x) -> p x") if False else None
                        din2 = npsum[:].rearrange("p (hh tt o) -> p tt hh o", hh=2, tt=4)[:, :, :, 64:65]
                        t8a = tmp8a[:].rearrange("p (tt hh) -> p tt hh", tt=4)
                        t8b = tmp8b[:].rearrange("p (tt hh) -> p tt hh", tt=4)
                        fslice = fsc[:].rearrange("p (tt h) -> p tt h", tt=4)[:, :, 2 * b:2 * b + 2]
                        v.tensor_copy(t8a, din2.squeeze(3))
                        v.drain()
                        v.tensor_mul(t8b, t8a, fslice)
                        v.drain()
                        v.tensor_scalar_max(tmp8b[:], tmp8b[:], EPS_ATT)
                        v.drain()
                        v.reciprocal(tmp8b[:], tmp8b[:])
                        v.drain()
                        v.tensor_mul(t8a, t8b, fslice)
                        v.drain()
                        nin = npsum[:].rearrange("p (hh tt o) -> p tt hh o", hh=2, tt=4)[:, :, :, 0:64]
                        fbc = tmp8a[:].rearrange("p (tt hh) -> p tt hh", tt=4).unsqueeze(3).broadcast_to((128, 4, 2, 64))
                        oout = ohbuf[:, 0:4096].rearrange(
                            "p (tt hh2 e) -> p tt hh2 e", tt=4, e=64)[:, :, 2 * b:2 * b + 2, :]
                        i2 = v.tensor_mul(oout, nin, fbc)
                        v.drain()
                        e.inc(i2, "sV", 1)
                        e.mark(f"B_div_{g}_{b}", "sV")
                    e.wait(v, f"B_statmm_{g}")
                    v.tensor_scalar(stat4m[:], K[1][:, 0:512].rearrange("p (tt o) -> p tt o", o=128)[:, :, 0:1].squeeze(2),
                                    1.0 / D_MODEL, None, ALU.mult)
                    v.tensor_scalar(stat4s[:], K[1][:, 512:1024].rearrange("p (tt o) -> p tt o", o=128)[:, :, 0:1].squeeze(2),
                                    1.0 / D_MODEL, None, ALU.mult)
                    v.drain()
                    v.tensor_tensor(tmp4[:], stat4m[:], stat4m[:], ALU.mult)
                    v.drain()
                    v.tensor_sub(stat4s[:], stat4s[:], tmp4[:])
                    v.drain()
                    v.tensor_scalar(stat4s[:], stat4s[:], EPS_LN, None, ALU.add)
                    v.drain()
                    i2 = v.reciprocal(rv4[:], stat4s[:])
                    e.inc(i2, "sV", 1)
                    e.mark(f"B_stat_{g}", "sV")
                    e.wait(v, f"B_sqrt_{g}")
                    if g > 0:
                        e.wait(v, f"B_stT_{g-1}")
                    v.tensor_copy(mcol_bf[:, 0:4], stat4m[:])
                    i2 = v.tensor_copy(mcol_bf[:, 4:8], rstd4[:])
                    v.drain()
                    e.inc(i2, "sV", 1)
                    e.mark(f"B_statbf_{g}", "sV")
                    e.wait(v, f"B_mbrb2_{g}")
                    if g == 2 and "x1t" in dbg:
                        e.wait(v, "H_x1dump")
                    for c in range(8):
                        dst = x1T[:, 1024 * c + 512 * (g % 2):1024 * c + 512 * (g % 2) + 512]
                        v.tensor_sub(dst, sqb[:, 512 * c:512 * (c + 1)], half(3, 0))
                        v.drain()
                        v.tensor_mul(dst, dst, half(3, 1))
                        v.drain()
                        i2 = v.tensor_add(dst, dst, xTbuf[p][:, AG * c:AG * (c + 1)])
                        v.drain()
                        e.inc(i2, "sV", 1) if c == 7 else None
                    e.mark(f"B_v_done_{g}", "sV")

                    if "C" not in phases:
                        continue
                    x1off = lambda c: 1024 * c + 512 * (g % 2)
                    vf = vaug[:].bitcast(F32)
                    sqbf32 = sqb[:].bitcast(F32)
                    t2f = t2b[:].bitcast(F32)
                    for c in range(8):
                        i2 = v.tensor_copy(xh[g % 2][:, 512 * c:512 * (c + 1)],
                                           x1T[:, x1off(c):x1off(c) + 512])
                    v.drain()
                    e.inc(i2, "sV", 1)
                    e.mark(f"C_xb_{g}", "sV")
                    e.wait(v, f"C_statmm_{g}")
                    v.tensor_scalar(stat4m[:], K[0][:, 0:512].rearrange("p (tt o) -> p tt o", o=128)[:, :, 0:1].squeeze(2),
                                    1.0 / D_MODEL, None, ALU.mult)
                    v.tensor_scalar(stat4s[:], K[0][:, 512:1024].rearrange("p (tt o) -> p tt o", o=128)[:, :, 0:1].squeeze(2),
                                    1.0 / D_MODEL, None, ALU.mult)
                    v.drain()
                    v.tensor_tensor(tmp4[:], stat4m[:], stat4m[:], ALU.mult)
                    v.drain()
                    v.tensor_sub(stat4s[:], stat4s[:], tmp4[:])
                    v.drain()
                    v.tensor_scalar(stat4s[:], stat4s[:], EPS_LN, None, ALU.add)
                    v.drain()
                    i2 = v.reciprocal(rv4[:], stat4s[:])
                    e.inc(i2, "sV", 1)
                    e.mark(f"C_stat_{g}", "sV")
                    e.wait(v, f"C_sqrt_{g}")
                    v.tensor_copy(mcol_bf[:, 0:4], stat4m[:])
                    i2 = v.tensor_copy(mcol_bf[:, 4:8], rstd4[:])
                    v.drain()
                    e.inc(i2, "sV", 1)
                    e.mark(f"C_statbf_{g}", "sV")
                    e.wait(v, f"C_mbrb3_{g}")
                    for c in range(8):
                        v.tensor_sub(t1b[:, 512 * c:512 * (c + 1)],
                                     x1T[:, x1off(c):x1off(c) + 512], half(0, 0))
                        v.drain()
                        i2 = v.tensor_mul(t1b[:, 512 * c:512 * (c + 1)],
                                          t1b[:, 512 * c:512 * (c + 1)], half(0, 1))
                        v.drain()
                    e.inc(i2, "sV", 1)
                    e.mark(f"C_x2h_{g}", "sV")
                    e.wait(v, f"C_gate_{g}")
                    for tt in range(4):
                        sl = K[1][:, 128 * tt:128 * tt + 8]
                        v.reduce_max(tmp8a[:, 0:1], sl, AX.X)
                        v.drain()
                        v.tensor_scalar(mask12[:, 16 * tt:16 * tt + 8], sl, tmp8a[:, 0:1], 2.0,
                                        ALU.is_equal, ALU.mult)
                        v.drain()
                        v.scalar_tensor_tensor(tmp8b[:], mask12[:, 16 * tt:16 * tt + 8], -5e29,
                                               sl, ALU.mult, ALU.add)
                        v.drain()
                        v.reduce_max(tmp8a[:, 1:2], tmp8b[:], AX.X)
                        v.drain()
                        v.tensor_scalar(mask12[:, 16 * tt + 8:16 * (tt + 1)], tmp8b[:],
                                        tmp8a[:, 1:2], 2.0, ALU.is_equal, ALU.mult)
                        i2 = v.tensor_tensor(tmp8a[:, 4 + tt:5 + tt], tmp8a[:, 0:1],
                                             tmp8a[:, 1:2], ALU.subtract)
                        v.drain()
                    e.inc(i2, "sV", 1)
                    e.mark(f"C_top2m_{g}", "sV")
                    e.wait(v, f"S_sig_{g}")
                    v.tensor_copy(rwb[:].rearrange("p (t two) -> p t two", two=2)[:, :, 0:1].squeeze(2),
                                  fsc[:, 0:4])
                    v.drain()
                    i2 = v.tensor_scalar(rwb[:].rearrange("p (t two) -> p t two", two=2)[:, :, 1:2].squeeze(2),
                                         fsc[:, 0:4], -1.0, 1.0, ALU.mult, ALU.add)
                    v.drain()
                    e.inc(i2, "sV", 1)
                    e.mark(f"C_top2_{g}", "sV")
                    e.wait(v, f"C_mer_{g}")
                    v.tensor_copy(t2b[:, 0:512], half(3, 0))
                    i2 = v.tensor_copy(t2b[:, 512:1024], half(3, 1))
                    v.drain()
                    e.inc(i2, "sV", 1)
                    e.mark(f"C_merc_{g}", "sV")
                    e.wait(v, f"C_wb_{g}")
                    v.tensor_copy(t2f[:, 512:1024], half(3, 0))
                    i2 = v.tensor_copy(t2f[:, 1024:1536], half(3, 1))
                    v.drain()
                    e.inc(i2, "sV", 1)
                    e.mark(f"C_wbc_{g}", "sV")
                    if g > 0:
                        e.wait(v, f"C_y_{g-1}_1")
                    e.wait(v, f"C_u1_{g}")
                    v.tensor_mul(sqb[:, 0:512], half(3, 0), t2b[:, 0:512])
                    i2 = v.tensor_mul(sqb[:, 512:1024], half(3, 0), t2b[:, 512:1024])
                    v.drain()
                    e.inc(i2, "sV", 1)
                    e.mark(f"C_mu1_{g}", "sV")
                    e.wait(v, f"C_u3_{g}")
                    v.tensor_mul(sqb[:, 1024:1536], half(3, 1), t2b[:, 0:512])
                    i2 = v.tensor_mul(sqb[:, 1536:2048], half(3, 1), t2b[:, 512:1024])
                    v.drain()
                    e.inc(i2, "sV", 1)
                    e.mark(f"C_mu3_{g}", "sV")
                    for fc in range(NFC):
                        e.wait(v, f"C_dd_{g}_{fc}")
                        v.tensor_add(vf[:, 1024:1536], half(0, 1), sqbf32[:, 1024:1536])
                        i2 = v.tensor_add(vf[:, 1536:2048], half(1, 1), sqbf32[:, 1536:2048])
                        v.drain()
                        e.inc(i2, "sV", 1)
                        e.mark(f"C_add_{g}_{fc}", "sV")
                        e.wait(v, f"S_silu_{g}_{fc}")
                        i2 = v.tensor_mul(sqbf32[:, 1024:1536], vf[:, 0:512], half(1, 0))
                        v.drain()
                        e.inc(i2, "sV", 1)
                        e.mark(f"C_k1f_{g}_{fc}", "sV")
                        v.tensor_mul(sqbf32[:, 1536:2048], vf[:, 512:1024], vf[:, 1536:2048])
                        v.drain()
                        i2 = v.tensor_mul(t2b[:, 3072:3584], sqbf32[:, 1024:1536], t2f[:, 512:1024])
                        v.drain()
                        e.inc(i2, "sV", 1)
                        e.mark(f"C_w1_{g}_{fc}", "sV")
                        v.tensor_mul(vf[:, 0:512], sqbf32[:, 1536:2048], t2f[:, 1024:1536])
                        v.drain()
                        i2 = v.tensor_add(mixT_ap(g, fc), t2b[:, 3072:3584], vf[:, 0:512])
                        v.drain()
                        e.inc(i2, "sV", 1)
                        e.mark(f"C_mix_{g}_{fc}", "sV")
                    e.wait(v, f"C_z_{g}_{NFC-1}")
                    v.tensor_copy(vf[:, 0:512], half(2, 1))
                    v.drain()
                    v.tensor_sub(vf[:, 0:512], vf[:, 0:512], half(2, 0))
                    v.drain()
                    v.tensor_mul(vf[:, 0:512], vf[:, 0:512], t2b[:, 512:1024])
                    v.drain()
                    v.tensor_mul(vf[:, 512:1024], half(2, 0), t2b[:, 0:512])
                    v.drain()
                    i2 = v.tensor_add(zhat, vf[:, 512:1024], vf[:, 0:512])
                    v.drain()
                    e.inc(i2, "sV", 1)
                    e.mark(f"C_zhat_{g}", "sV")
                    for ps in range(2):
                        e.wait(v, f"C_statmm4_{g}_{ps}")
                        v.tensor_scalar(stat4m[:, 0:2], K[0][:, 0:256].rearrange("p (q o) -> p q o", o=128)[:, :, 0:1].squeeze(2),
                                        1.0 / D_MODEL, None, ALU.mult)
                        v.tensor_scalar(stat4s[:, 0:2], K[0][:, 512:768].rearrange("p (q o) -> p q o", o=128)[:, :, 0:1].squeeze(2),
                                        1.0 / D_MODEL, None, ALU.mult)
                        v.drain()
                        v.tensor_tensor(tmp4[:, 0:2], stat4m[:, 0:2], stat4m[:, 0:2], ALU.mult)
                        v.drain()
                        v.tensor_sub(stat4s[:, 0:2], stat4s[:, 0:2], tmp4[:, 0:2])
                        v.drain()
                        v.tensor_scalar(stat4s[:, 0:2], stat4s[:, 0:2], EPS_LN, None, ALU.add)
                        v.drain()
                        i2 = v.reciprocal(rv4[:, 0:2], stat4s[:, 0:2])
                        e.inc(i2, "sV", 1)
                        e.mark(f"C_stat4_{g}_{ps}", "sV")
                        e.wait(v, f"C_sqrt4_{g}_{ps}")
                        v.tensor_copy(mcol_bf[:, 0:2], stat4m[:, 0:2])
                        i2 = v.tensor_copy(mcol_bf[:, 4:6], rstd4[:, 0:2])
                        v.drain()
                        e.inc(i2, "sV", 1)
                        e.mark(f"C_statbf4_{g}_{ps}", "sV")
                        e.wait(v, f"C_mbrb4_{g}_{ps}")
                        if ps == 1:
                            e.wait(v, f"C_y_{g}_0")
                        for dc in range(8):
                            fT = xh[g % 2][:, 256 * dc:256 * (dc + 1)]
                            mb4 = K[1][:, 0:256]
                            rb4 = K[1][:, 512:768]
                            xt = vf[:, 2048 - 256:2048]
                            v.tensor_sub(xt, fT, mb4)
                            v.drain()
                            v.tensor_mul(xt, xt, rb4)
                            v.drain()
                            x1in = x1T[:, x1off(dc) + 256 * ps:x1off(dc) + 256 * ps + 256]
                            i2 = v.tensor_add(sqbf32[:, 256 * dc:256 * (dc + 1)],
                                              vf[:, 2048 - 256:2048], x1in)
                            v.drain()
                        e.inc(i2, "sV", 1)
                        e.mark(f"C_x2_{g}_{ps}", "sV")
                    e.mark(f"C_done_{g}", "sV")

        d = DryEng()
        for f in (prog_sync, prog_gpsimd, prog_tensor, prog_scalar, prog_vector):
            f(d)

        sems = {n: es.enter_context(nc.semaphore(n)) for n in SEM_NAMES}
        env.begin_real(sems)
        with nc.Block() as block:
            @block.sync
            def _(s):
                prog_sync(s)

            @block.gpsimd
            def _(g_):
                prog_gpsimd(g_)

            @block.tensor
            def _(t):
                prog_tensor(t)

            @block.scalar
            def _(sc):
                prog_scalar(sc)

            @block.vector
            def _(v):
                prog_vector(v)

        env.check_passes_match()
        nc._dbg_env = env

    return nc


def assemble_y(y):
    yb = y.reshape(128, 8, 2, 8, 256)
    xcore = yb.transpose(1, 2, 4, 3, 0).reshape(TCORE, D_MODEL)
    return xcore


def kernel(**inputs):
    in_maps = make_in_maps(inputs)
    nc = build_program()
    res = run_bass_kernel_spmd(nc, in_maps, core_ids=list(range(NCORES)))
    out = np.zeros((C_SEQ, L_SEQ, D_MODEL), dtype=np.float32)
    for i in range(NCORES):
        xcore = assemble_y(res.results[i]["y"])
        out[0, i * SEQ_SLICE:(i + 1) * SEQ_SLICE] = xcore[0:SEQ_SLICE]
        out[1, i * SEQ_SLICE:(i + 1) * SEQ_SLICE] = xcore[SEQ_SLICE:TCORE]
    return out
